# version 10
# baseline (speedup 1.0000x reference)
import math
import numpy as np
import ml_dtypes

import concourse.bass as bass
import concourse.mybir as mybir
from concourse.bass_utils import run_bass_kernel_spmd

F32 = mybir.dt.float32
F32R = mybir.dt.float32r
BF16 = mybir.dt.bfloat16
AF = mybir.ActivationFunctionType
ALU = mybir.AluOpType
AXX = mybir.AxisListType.X

B, T, C, H, DH = 4, 4096, 1024, 16, 64
TC = 2048
NCC = 8
PBLK = 256
NPB = TC // PBLK
RCH = 128
NRC = TC // RCH
FBLK = 512
NFB = TC // FBLK
IF = 2816
NI = IF // 128
QMAX = 127.0
EPS = 1e-5
RMAGIC = 1.5 * 2.0 ** 23

_glo = math.log(1.0 - 0.999)
_ghi = math.log(1.0 - 0.9)
GAMMAS = [1.0 - math.exp(_glo + (_ghi - _glo) * h / (H - 1)) for h in range(H)]

DEBUG_TAPS = ()
PHASE_LIMIT = 99
SUB2 = 9
SUB3 = 3
SUB4 = 9

ENGS = ("pe", "act", "dve", "pool", "sp")


class _Op:
    __slots__ = ("eng", "fn", "sem", "amt", "needs_inc", "token", "waits", "gi",
                 "kind", "deps")

    def __init__(self, eng, fn, sem, amt, gi, kind):
        self.eng = eng
        self.fn = fn
        self.sem = sem
        self.amt = amt
        self.needs_inc = False
        self.token = None
        self.waits = ()
        self.gi = gi
        self.kind = kind
        self.deps = set()


class PB:
    NDMA = 4

    def __init__(self, nc):
        self.nc = nc
        self.ops = []
        self.kw = {}
        self.kr = {}
        self.pending = {eng: set() for eng in ENGS}
        self.sem_names = set()
        self.dma_idx = {}
        self.dma_last = {}

    def _dep_add(self, deps, op):
        if op is None:
            return
        lane = (op.eng, op.sem)
        cur = deps.get(lane)
        if cur is None or op.gi > cur.gi:
            deps[lane] = op

    def op(self, eng, fn, reads=(), writes=(), sem=None, amt=1, kind="c"):
        sem = sem or (("q" + eng) if kind == "dma" else ("e" + eng))
        o = _Op(eng, fn, sem, amt, len(self.ops), kind)
        deps = {}
        for k in reads:
            self._dep_add(deps, self.kw.get(k))
            if isinstance(k, tuple) and k and k[0] == "PS":
                for r in self.kr.get(k, {}).values():
                    if r.eng != eng:
                        self._dep_add(deps, r)
        for k in writes:
            self._dep_add(deps, self.kw.get(k))
            for r in self.kr.get(k, {}).values():
                self._dep_add(deps, r)
        for p in self.pending[eng]:
            self._dep_add(deps, p)
        self.pending[eng] = set()
        o.deps = set(deps.values())
        for k in reads:
            self.kr.setdefault(k, {})[(o.eng, o.sem)] = o
        for k in writes:
            self.kw[k] = o
            self.kr[k] = {}
        self.ops.append(o)
        return o

    def dma(self, queue, out_ap, in_ap, reads=(), writes=(), slow=False):
        def fn(e, o=out_ap, i=in_ap, s=slow):
            if s:
                return e.dma_start(out=o, in_=i, allow_slow_non_contiguous=True)
            return e.dma_start(out=o, in_=i)
        idx = self.dma_idx.get(queue, 0)
        self.dma_idx[queue] = idx + 1
        rot = idx % self.NDMA
        sem = f"q{queue}{rot}"
        o = self.op(queue, fn, reads=reads, writes=writes, sem=sem, amt=16,
                    kind="dma")
        prev = self.dma_last.get((queue, rot))
        if prev is not None:
            o.deps.add(prev)
        self.dma_last[(queue, rot)] = o
        return o

    def dma_tr(self, queue, out_ap, in_ap, reads=(), writes=()):
        def fn(e, o=out_ap, i=in_ap):
            return e.dma_start_transpose(out=o, in_=i)
        idx = self.dma_idx.get(queue, 0)
        self.dma_idx[queue] = idx + 1
        rot = idx % self.NDMA
        sem = f"q{queue}{rot}"
        o = self.op(queue, fn, reads=reads, writes=writes, sem=sem, amt=16,
                    kind="dma")
        prev = self.dma_last.get((queue, rot))
        if prev is not None:
            o.deps.add(prev)
        self.dma_last[(queue, rot)] = o
        return o

    def cc(self, ccname, kindstr, aluop, groups, in_ap, out_ap, reads, writes):
        def fn(e, ks=kindstr, op=aluop, g=groups, i=in_ap, o=out_ap):
            return e.collective_compute(ks, op, replica_groups=g, ins=[i], outs=[o])
        self.sem_names.add(ccname)
        return self.op("pool", fn, reads=reads, writes=writes, sem=ccname, amt=1,
                       kind="cc")

    def mm(self, bank, fn, reads, start, stop):
        o = _Op("pe", fn, "epe", 1, len(self.ops), "c")
        deps = {}
        for k in reads:
            self._dep_add(deps, self.kw.get(k))
        if start:
            for r in self.kr.get(bank, {}).values():
                if r.eng != "pe":
                    self._dep_add(deps, r)
            w = self.kw.get(bank)
            if w is not None and w.eng != "pe":
                self._dep_add(deps, w)
        for p in self.pending["pe"]:
            self._dep_add(deps, p)
        self.pending["pe"] = set()
        o.deps = set(deps.values())
        for k in reads:
            self.kr.setdefault(k, {})[(o.eng, o.sem)] = o
        if start:
            self.kr[bank] = {}
        if stop:
            self.kw[bank] = o
            self.kr[bank] = {}
        self.ops.append(o)
        return o

    def barrier(self):
        dmas = list(self.dma_last.values())
        bar_ops = []
        for eng in ENGS:
            o = self.op(eng, lambda e: e.drain())
            for p in dmas:
                o.deps.add(p)
            bar_ops.append(o)
        for eng in ENGS:
            self.pending[eng] = set(bar_ops)

    def finalize(self):
        for o in self.ops:
            for dd in o.deps:
                dd.needs_inc = True
        for o in self.ops:
            if o.kind in ("dma", "cc"):
                o.needs_inc = True
        counts = {}
        for o in self.ops:
            if o.needs_inc:
                counts[o.sem] = counts.get(o.sem, 0) + o.amt
                o.token = (o.sem, counts[o.sem])
        marks = {eng: {} for eng in ENGS}
        for o in self.ops:
            w = {}
            m = marks[o.eng]
            for dd in o.deps:
                s, v = dd.token
                if m.get(s, 0) < v and w.get(s, 0) < v:
                    w[s] = v
            for s, v in w.items():
                m[s] = v
            o.waits = sorted(w.items(), key=str)
        self.sem_names.update(counts.keys())
        return counts


def _consts():
    gam = np.array(GAMMAS, dtype=np.float64)
    u = np.arange(RCH)
    tt = np.arange(RCH)
    diff = tt[None, :] - u[:, None]
    dm = np.zeros((RCH, H * RCH), dtype=np.float64)
    for h in range(H):
        dm[:, h * RCH:(h + 1) * RCH] = np.where(
            diff >= 0, gam[h] ** np.maximum(diff, 0), 0.0)
    dm = np.asarray(dm.astype(ml_dtypes.bfloat16))
    gq2 = np.zeros((DH, H * RCH), dtype=np.float64)
    for h in range(H):
        gq2[:, h * RCH:(h + 1) * RCH] = gam[h] ** (tt[None, :] + 1)
    gk = np.zeros((RCH, C), dtype=np.float64)
    for h in range(H):
        gk[:, h * DH:(h + 1) * DH] = (gam[h] ** (RCH - 1 - u))[:, None]
    gk = np.asarray(gk.astype(ml_dtypes.bfloat16))
    gdec = np.zeros((DH, H * DH), dtype=np.float32)
    for h in range(H):
        gdec[:, h * DH:(h + 1) * DH] = gam[h] ** RCH
    selT = np.zeros((2, 128), dtype=np.float32)
    selT[0, :64] = 1.0
    selT[1, 64:] = 1.0
    return dict(
        dmask=dm,
        gq2=np.asarray(gq2.astype(ml_dtypes.bfloat16)),
        gk=gk, gdec=gdec,
        ident_bf=np.asarray(np.eye(128, dtype=ml_dtypes.bfloat16)),
        ident_f32=np.eye(128, dtype=np.float32),
        ones_col_bf=np.asarray(np.ones((128, 1), dtype=ml_dtypes.bfloat16)),
        one_row_bf=np.asarray(np.ones((1, 128), dtype=ml_dtypes.bfloat16)),
        one_row_f32=np.ones((1, 128), dtype=np.float32),
        one_row256=np.asarray(np.ones((1, 256), dtype=ml_dtypes.bfloat16)),
        selT=selT)


def _wl(wT):
    return np.ascontiguousarray(
        wT.reshape(8, 128, 1024).transpose(1, 0, 2).reshape(128, 8192))


def _host_prepare(inputs):
    x = np.asarray(inputs["L0_attn_x"], dtype=np.float32)
    con = _consts()
    shared = {
        "wq_l": _wl(np.asarray(inputs["L0_attn_wq"], np.float32).T),
        "wk_l": _wl(np.asarray(inputs["L0_attn_wk"], np.float32).T),
        "wv_l": _wl(np.asarray(inputs["L0_attn_wv"], np.float32).T),
        "wo_l": _wl(np.asarray(inputs["L0_attn_wo"], np.float32).T),
        "wmod": np.asarray(inputs["L0_wmod"], np.float32),
        "wg_nat": np.asarray(inputs["L0_ffn_w_gate"], np.float32),
        "wu_nat": np.asarray(inputs["L0_ffn_w_up"], np.float32),
        "wd_nat": np.asarray(inputs["L0_ffn_w_down"], np.float32),
        "gnw_v": np.ascontiguousarray(
            np.asarray(inputs["L0_attn_gn_w"], np.float32).reshape(8, 128).T),
        "gnb_v": np.ascontiguousarray(
            np.asarray(inputs["L0_attn_gn_b"], np.float32).reshape(8, 128).T),
        "clipx": np.asarray(inputs["L0_ffn_clip_x"], np.float32).reshape(1, 1),
        "clipd": np.asarray(inputs["L0_ffn_clip_down"], np.float32).reshape(1, 1),
    }
    shared.update(con)
    cmat = np.asarray(inputs["L0_c"], np.float32)
    in_maps = []
    for core in range(8):
        b, half = core // 2, core % 2
        m = dict(shared)
        m["xT"] = np.ascontiguousarray(x[b, half * TC:(half + 1) * TC, :].T)
        m["cvec"] = np.ascontiguousarray(cmat[b].reshape(8, 128).T)
        m["smask_send"] = np.full((64, 1), 1.0 if half == 0 else 0.0, np.float32)
        m["smask_recv"] = np.full((64, 1), 1.0 if half == 1 else 0.0, np.float32)
        in_maps.append(m)
    return in_maps


def build_program(debug_taps=(), phase_limit=99, sub2=9, sub3=3, sub4=9):
    nc = bass.Bass("TRN2", target_bir_lowering=False)

    d = {}
    def din(name, shape, dt=F32):
        d[name] = nc.dram_tensor(name, list(shape), dt, kind="ExternalInput")
        return d[name]

    din("xT", (C, TC))
    for n in ("wq_l", "wk_l", "wv_l", "wo_l"):
        din(n, (128, 8192))
    din("wmod", (C, 6 * C), F32R)
    din("wg_nat", (IF, C)); din("wu_nat", (IF, C)); din("wd_nat", (C, IF))
    din("gnw_v", (128, 8)); din("gnb_v", (128, 8))
    din("clipx", (1, 1)); din("clipd", (1, 1))
    din("cvec", (128, 8))
    din("smask_send", (64, 1)); din("smask_recv", (64, 1))
    din("dmask", (128, H * RCH), BF16); din("gq2", (DH, H * RCH), BF16)
    din("gk", (RCH, C), BF16); din("gdec", (DH, H * DH))
    din("ident_bf", (128, 128), BF16); din("ident_f32", (128, 128))
    din("ones_col_bf", (128, 1), BF16); din("one_row_bf", (1, 128), BF16)
    din("one_row_f32", (1, 128)); din("one_row256", (1, 256), BF16)
    din("selT", (2, 128))

    out_d = nc.dram_tensor("out", [C, TC], F32, kind="ExternalOutput")
    dbg = {}
    for t in debug_taps:
        shp, dt = {"h": ([128, NCC * TC], BF16), "x2": ([128, NCC * TC], BF16),
                   "xq": ([128, NCC * TC], BF16), "s": ([128, NI * TC], BF16),
                   "mod": ([128, 48], F32), "state": ([DH, H * DH], F32),
                   "y": ([NPB, 128, NCC * PBLK], BF16)}[t]
        dbg[t] = nc.dram_tensor("dbg_" + t, shp, dt, kind="ExternalOutput")

    hd = nc.dram_tensor("hd", [NPB, 128, NCC * PBLK], BF16)
    vtmd = nc.dram_tensor("vtmd", [NRC, 128, C], BF16)
    ktmd = nc.dram_tensor("ktmd", [NRC, 128, C], BF16)
    yd = nc.dram_tensor("yd", [NPB, 128, NCC * PBLK], BF16)
    ccs_in = nc.dram_tensor("ccs_in", [DH, H * DH], F32)
    ccs_out = nc.dram_tensor("ccs_out", [DH, H * DH], F32)
    gn_in = nc.dram_tensor("gn_in", [32, 1], F32)
    gn_out = nc.dram_tensor("gn_out", [32, 1], F32)
    mx1_in = nc.dram_tensor("mx1_in", [16, 1], F32); mx1_out = nc.dram_tensor("mx1_out", [16, 1], F32)
    mx2_in = nc.dram_tensor("mx2_in", [16, 1], F32); mx2_out = nc.dram_tensor("mx2_out", [16, 1], F32)

    SB = {}
    def sb(name, shape, dt, off):
        SB[name] = nc.alloc_sbuf_tensor_at(name, list(shape), dt, offset=off)
        return SB[name]

    SBASE = 18 * 1024
    LIM = SBASE + 190 * 1024

    o = SBASE
    x_sb = sb("x", (128, NCC * TC), BF16, o); o += 32768
    idb_sb = sb("idb", (128, 128), BF16, o); o += 256
    idf_sb = sb("idf", (128, 128), F32, o); o += 512
    onec_sb = sb("onec", (128, 1), BF16, o); o += 32
    oner_sb = sb("oner", (1, 128), BF16, o); o += 256
    onerf_sb = sb("onerf", (1, 128), F32, o); o += 512
    selT_sb = sb("selTc", (2, 128), F32, o); o += 512
    cvec_sb = sb("cvecc", (128, 8), F32, o); o += 32
    silc_sb = sb("silc", (128, 8), F32R, o); o += 32
    gnw_sb = sb("gnwc", (128, 8), F32, o); o += 32
    gnb_sb = sb("gnbc", (128, 8), F32, o); o += 32
    mod_sb = sb("modv", (128, 48), F32, o); o += 192
    modt_sb = sb("modt", (128, 48), F32, o); o += 192
    gscr_sb = sb("gscr", (128, 24), F32, o); o += 96
    clipx_sb = sb("clipxs", (1, 1), F32, o); o += 32
    clipd_sb = sb("clipds", (1, 1), F32, o); o += 32
    smask_s_sb = sb("smask_s", (64, 1), F32, o); o += 32
    smask_r_sb = sb("smask_r", (64, 1), F32, o); o += 32
    gsum_sb = sb("gsum", (8, 2), F32, o); o += 32
    gsq_sb = sb("gsq", (8, 2), F32, o); o += 32
    gsum2_sb = sb("gsum2", (2, 8), F32, o); o += 32
    gsq2_sb = sb("gsq2", (2, 8), F32, o); o += 32
    ysum_sb = sb("ysum", (128, 64), F32, o); o += 256
    ysq_sb = sb("ysq", (128, 64), F32, o); o += 256
    ysum8_sb = sb("ysum8", (128, 8), F32, o); o += 32
    ysq8_sb = sb("ysq8", (128, 8), F32, o); o += 32
    Av_sb = sb("Av", (128, 8), F32, o); o += 32
    Bv_sb = sb("Bv", (128, 8), F32, o); o += 32
    mu8_sb = sb("mu8", (128, 8), F32, o); o += 32
    sq8_sb = sb("sq8", (128, 8), F32, o); o += 32
    hmax_sb = sb("hmax64", (128, 64), F32, o); o += 256
    hmax1_sb = sb("hmax1", (128, 1), F32, o); o += 32
    smax_sb = sb("smaxT", (128, NI * NFB), F32, o); o += 512
    smax1_sb = sb("smax1", (128, 1), F32, o); o += 32
    sc16_sb = sb("sc16", (16, 1), F32, o); o += 32
    sc16b_sb = sb("sc16b", (16, 1), F32, o); o += 32
    sA_sb = sb("sA", (1, 1), F32, o); o += 32
    sB_sb = sb("sB", (1, 1), F32, o); o += 32
    rbf_sb = sb("rbf", (1, 1), BF16, o); o += 32
    rB_sb = sb("rB", (128, 1), F32, o); o += 32
    qsxB_sb = sb("qsxB", (128, 1), F32, o); o += 32
    rs2B_sb = sb("rs2B", (128, 1), F32, o); o += 32
    qsdB_sb = sb("qsdB", (128, 1), F32, o); o += 32
    wmx_sb = sb("wmx", (128, 1), F32, o); o += 32
    wri_sb = sb("wri", (128, 1), F32, o); o += 32
    wsc_sb = sb("wsc", (128, 1), F32, o); o += 32
    sgv_sb = sb("sgv", (128, NI), F32, o); o += 128
    suv_sb = sb("suv", (128, NI), F32, o); o += 128
    sqc_sb = sb("sqc", (128, 1), F32, o); o += 32
    epsv_sb = sb("epsv", (128, 1), F32, o); o += 32
    magic_sb = sb("magicc", (128, 2), F32, o); o += 32
    wmxu_sb = sb("wmxu", (128, 1), F32, o); o += 32
    wriu_sb = sb("wriu", (128, 1), F32, o); o += 32
    wscu_sb = sb("wscu", (128, 1), F32, o); o += 32
    onr2_sb = sb("onr2", (1, 256), BF16, o); o += 512
    Bvb_sb = sb("Bvb", (128, 8), BF16, o); o += 32
    R0_END = o
    assert R0_END <= SBASE + 40 * 1024, R0_END

    o = R0_END
    mrow_sb = sb("mrow", (1, TC), BF16, o); o += 4096
    rrow_sb = sb("rrow", (1, TC), BF16, o); o += 4096
    mrr_sb = sb("mrr", (1, TC), BF16, o); o += 4096
    rows_sb = sb("rowst", (1, PBLK * 5), F32, o); o += 5120
    xsq_sb = [sb(f"xsq{i}", (128, PBLK), BF16, o + i * 512) for i in range(2)]; o += 1024
    bcb_sb = sb("bcb", (128, 2 * PBLK), BF16, o); o += 1024
    hsc_sb = [sb(f"hsc{i}", (128, PBLK), BF16, o + i * 512) for i in range(2)]; o += 1024
    wfr_sb = [sb(f"wfr{i}", (128, 512), F32R, o + i * 2048) for i in range(2)]; o += 4096
    xst_sb = sb("xst", (128, TC), F32, o); o += 8192
    wf_sb = [sb(f"wf{i}", (128, 1024), F32, o + i * 4096) for i in range(2)]; o += 8192
    yl_sb = [sb(f"yl{i}", (128, NCC * PBLK), BF16, o + i * 4096) for i in range(2)]; o += 8192
    yb_sb = [sb(f"yb{i}", (128, NCC * PBLK), BF16, o + i * 4096) for i in range(2)]; o += 8192
    WA_OFF = o
    wA_sb = sb("wA", (128, 8192), BF16, o); o += 16384
    wB_sb = sb("wB", (128, 8192), BF16, o); o += 16384
    dm_sb = sb("dm", (128, H * RCH), BF16, o); o += 4096
    gq2_sb = sb("gq2c", (DH, H * RCH), BF16, o); o += 4096
    gk_sb = sb("gkc", (RCH, C), BF16, o); o += 2048
    gdec_sb = sb("gdecc", (DH, H * DH), F32, o); o += 4096
    S_sb = sb("S", (DH, H * DH), F32, o); o += 4096
    Sr_sb = sb("Srecv", (DH, H * DH), F32, o); o += 4096
    Sbf_sb = sb("Sbf", (DH, H * DH), BF16, o); o += 2048
    h_sb = [sb(f"h{i}", (128, NCC * PBLK), BF16, o + i * 4096) for i in range(2)]; o += 8192
    ktm_sb = [sb(f"ktm{i}", (128, C), BF16, o + i * 2048) for i in range(2)]; o += 4096
    vtm_sb = [sb(f"vtm{i}", (128, C), BF16, o + i * 2048) for i in range(2)]; o += 4096
    stm_sb = [sb(f"stm{i}", (128, H * RCH), BF16, o + i * 4096) for i in range(2)]; o += 8192
    modrow_sb = sb("modrow", (1, 512), F32, o); o += 2048
    qt_sb = sb("qt", (DH, H * PBLK), BF16, o); o += 8192
    kt_sb = sb("kt", (DH, H * PBLK), BF16, o); o += 8192
    assert o <= 229375, ("attn", o)

    o = R0_END
    s_sb = sb("s", (128, NI * TC), BF16, o); o += NI * TC * 2
    OUTB_OFF = o
    wig_sb = [sb(f"wig{i}", (128, 1024), BF16, o + i * 2048) for i in range(2)]; o += 4096
    wiu_sb = [sb(f"wiu{i}", (128, 1024), BF16, o + i * 2048) for i in range(2)]; o += 4096
    gsl_sb = [sb(f"gsl{i}", (128, FBLK), BF16, o + i * 1024) for i in range(2)]; o += 2048
    wdn_sb = sb("wdnat", (128, IF // 2), F32, o); o += 5632
    wdq1_sb = sb("wdq1", (128, IF // 2), F32, o); o += 5632
    wdi_sb = sb("wdint", (128, IF // 2), BF16, o); o += 2816
    wnat_sb = wdn_sb
    wq1_sb = wdq1_sb
    wint_sb = wdi_sb
    wdQ_sb = [sb(f"wdq{i}", (128, NI * 256), BF16, o + i * 11264) for i in range(2)]
    o += 22528
    x2t_sb = [sb(f"x2t{i}", (128, 256), BF16, o + i * 512) for i in range(2)]; o += 1024
    wdmx_sb = sb("wdmx", (128, 1), F32, o); o += 32
    wdri_sb = sb("wdri", (128, 1), F32, o); o += 32
    wdsc_sb = sb("wdsc", (128, 1), F32, o); o += 32
    assert o <= 196576, ("ffn", o)
    outb_sb = sb("outb", (128, 1024), F32, OUTB_OFF)
    h2_sb = sb("h2", (128, NCC * TC), BF16, 196576)
    assert 196576 >= o, ("h2", o)

    PS = [nc.alloc_psum_tensor(f"pb{i}", [128, 512], F32) for i in range(6)]
    PSB = [nc.alloc_psum_tensor(f"pbb{i}", [128, 1024], BF16) for i in range(2)]
    def bank(i):
        return ("PS", i)
    def bankB(i):
        return ("PS", "B", i)

    pb = PB(nc)

    def t_mm(bk, out_ap, lhsT, rhs, reads, start, stop, is_tr=False):
        def fn(e, o2=out_ap, l=lhsT, r=rhs, s=start, t=stop, it=is_tr):
            return e.matmul(o2, l, r, start=s, stop=t, is_transpose=it)
        return pb.mm(bk, fn, reads, start, stop)

    def a_act(out_ap, in_ap, func, reads, writes, scale=1.0, bias=0.0, accum=None):
        def fn(e, o2=out_ap, i=in_ap, f=func, s=scale, b=bias, a=accum):
            if a is not None:
                return e.activation(o2, i, f, bias=b, scale=s, accum_out=a)
            return e.activation(o2, i, f, bias=b, scale=s)
        return pb.op("act", fn, reads=reads, writes=writes)

    def v_tt(eng, out_ap, a_ap, b_ap, aluop, reads, writes):
        def fn(e, o2=out_ap, a=a_ap, b=b_ap, op2=aluop):
            return e.tensor_tensor(o2, a, b, op2)
        return pb.op(eng, fn, reads=reads, writes=writes)

    def v_ts(eng, out_ap, in_ap, s1, s2, op0, op1, reads, writes):
        def fn(e, o2=out_ap, i=in_ap, a=s1, b=s2, p0=op0, p1=op1):
            if b is None:
                return e.tensor_single_scalar(o2, i, a, p0)
            return e.tensor_scalar(o2, i, a, b, p0, p1)
        return pb.op(eng, fn, reads=reads, writes=writes)

    def v_stt(eng, out_ap, in0, scal, in1, op0, op1, reads, writes):
        def fn(e, o2=out_ap, i0=in0, s=scal, i1=in1, p0=op0, p1=op1):
            return e.scalar_tensor_tensor(o2, i0, s, i1, p0, p1)
        return pb.op(eng, fn, reads=reads, writes=writes)

    def v_red(eng, out_ap, in_ap, aluop, reads, writes, absval=False):
        def fn(e, o2=out_ap, i=in_ap, op2=aluop, av=absval):
            return e.tensor_reduce(o2, i, AXX, op2, apply_absolute_value=av)
        return pb.op(eng, fn, reads=reads, writes=writes)

    def v_copy(eng, out_ap, in_ap, reads, writes):
        def fn(e, o2=out_ap, i=in_ap):
            return e.tensor_copy(o2, i)
        return pb.op(eng, fn, reads=reads, writes=writes)

    def v_recip(eng, out_ap, in_ap, reads, writes):
        def fn(e, o2=out_ap, i=in_ap):
            return e.reciprocal(o2, i)
        return pb.op(eng, fn, reads=reads, writes=writes)

    def v_memset(eng, ap, val, writes):
        def fn(e, a=ap, v=val):
            return e.memset(a, v)
        return pb.op(eng, fn, reads=(), writes=writes)

    out_keys = []

    v_memset("dve", epsv_sb[:, :], EPS, writes=[("epsv",)])
    v_memset("dve", magic_sb[:, 0:1], RMAGIC, writes=[("magic",)])
    v_memset("dve", magic_sb[:, 1:2], -RMAGIC, writes=[("magic",)])
    nc.const_aps.register(F32, RMAGIC, magic_sb[:, 0:1])
    nc.const_aps.register(F32, -RMAGIC, magic_sb[:, 1:2])
    for nme, dst in (("dmask", dm_sb), ("gq2", gq2_sb), ("gk", gk_sb),
                     ("gdec", gdec_sb), ("ident_bf", idb_sb), ("ident_f32", idf_sb),
                     ("ones_col_bf", onec_sb), ("one_row_bf", oner_sb),
                     ("one_row_f32", onerf_sb), ("one_row256", onr2_sb),
                     ("selT", selT_sb), ("cvec", cvec_sb),
                     ("gnw_v", gnw_sb), ("gnb_v", gnb_sb),
                     ("clipx", clipx_sb), ("clipd", clipd_sb),
                     ("smask_send", smask_s_sb), ("smask_recv", smask_r_sb)):
        pb.dma("pool", dst[:, :], d[nme][:, :], reads=(), writes=[("c", nme)])

    for ccn in range(NCC):
        pb.dma("sp", xst_sb[:, :], d["xT"][ccn * 128:(ccn + 1) * 128, :],
               reads=[("xst",)], writes=[("xst",)])
        v_copy("dve", x_sb[:, ccn * TC:(ccn + 1) * TC], xst_sb[:, :],
               reads=[("xst",)], writes=[("x", ccn)])

    def load_weight(slot, src_name, slotkey, q="sp"):
        for ccn in range(NCC):
            st = wf_sb[ccn % 2]
            pb.dma(q, st[:, :], d[src_name][:, ccn * 1024:(ccn + 1) * 1024],
                   reads=[("wf", ccn % 2)], writes=[("wf", ccn % 2)])
            v_copy("dve", slot[:, ccn * 1024:(ccn + 1) * 1024], st[:, :],
                   reads=[("wf", ccn % 2)], writes=[(slotkey, ccn)])

    a_act(silc_sb[:, :], cvec_sb[:, :], AF.Silu, reads=[("c", "cvec")],
          writes=[("silc",)])
    for i in range(12):
        for j in range(NCC):
            pb.dma("act", wfr_sb[(2 * i + j) % 2][:, :],
                   d["wmod"][j * 128:(j + 1) * 128, i * 512:(i + 1) * 512],
                   reads=[("wfr", (2 * i + j) % 2)],
                   writes=[("wfr", (2 * i + j) % 2)])
            t_mm(bank(5), PS[5][:1, 0:512],
                 silc_sb[:, j:j + 1],
                 wfr_sb[(2 * i + j) % 2][:, :],
                 reads=[("wfr", (2 * i + j) % 2), ("silc",)],
                 start=(j == 0), stop=(j == 7))
        v_copy("dve", modrow_sb[:1, 0:512], PS[5][:1, 0:512],
               reads=[bank(5), ("modrow",)], writes=[("modrow",)])
        for k in range(4):
            t_mm(bank(2), PS[2][:, i * 4 + k: i * 4 + k + 1],
                 modrow_sb[:1, k * 128:(k + 1) * 128], idf_sb[0:1, 0:1],
                 reads=[("modrow",), ("c", "ident_f32")],
                 start=(i == 0 and k == 0), stop=(i == 11 and k == 3), is_tr=True)
    v_copy("dve", mod_sb[:, :], PS[2][:, 0:48], reads=[bank(2)], writes=[("modv",)])
    v_copy("dve", modt_sb[:, 0:8], mod_sb[:, 0:8], reads=[("modv",)], writes=[("mt", 0)])
    v_ts("dve", modt_sb[:, 8:16], mod_sb[:, 8:16], 1.0, None, ALU.add, None,
         reads=[("modv",)], writes=[("mt", 1)])
    v_copy("dve", modt_sb[:, 24:32], mod_sb[:, 24:32], reads=[("modv",)], writes=[("mt", 3)])
    v_ts("dve", modt_sb[:, 32:40], mod_sb[:, 32:40], 1.0, None, ALU.add, None,
         reads=[("modv",)], writes=[("mt", 4)])
    for (c0, mi) in ((16, 2), (40, 5)):
        a_act(gscr_sb[:, 0:8], mod_sb[:, c0:c0 + 8], AF.Sign,
              reads=[("modv",), ("gscr",)], writes=[("gscr",)])
        v_tt("dve", gscr_sb[:, 8:16], gscr_sb[:, 0:8], gscr_sb[:, 0:8], ALU.mult,
             reads=[("gscr",)], writes=[("gscr",)])
        v_ts("dve", gscr_sb[:, 8:16], gscr_sb[:, 8:16], -1.0, 1.0, ALU.mult, ALU.add,
             reads=[("gscr",)], writes=[("gscr",)])
        v_tt("dve", gscr_sb[:, 8:16], gscr_sb[:, 0:8], gscr_sb[:, 8:16], ALU.add,
             reads=[("gscr",)], writes=[("gscr",)])
        a_act(gscr_sb[:, 16:24], mod_sb[:, c0:c0 + 8], AF.Abs,
              reads=[("modv",), ("gscr",)], writes=[("gscr",)])
        v_ts("dve", gscr_sb[:, 16:24], gscr_sb[:, 16:24], 0.25, None, ALU.max, None,
             reads=[("gscr",)], writes=[("gscr",)])
        v_tt("dve", modt_sb[:, c0:c0 + 8], gscr_sb[:, 8:16], gscr_sb[:, 16:24],
             ALU.mult, reads=[("gscr",)], writes=[("mt", mi)])

    load_weight(wA_sb, "wk_l", "wA")
    load_weight(wB_sb, "wv_l", "wB")

    sa_c = modt_sb[:, 0:8]; a1_c = modt_sb[:, 8:16]; ga_c = modt_sb[:, 16:24]
    sb_c = modt_sb[:, 24:32]; a2_c = modt_sb[:, 32:40]; gb_c = modt_sb[:, 40:48]
    MT = [("mt", i) for i in range(6)]

    if "mod" in dbg:
        pb.dma("pool", dbg["mod"][:, :], modt_sb[:, :], reads=MT, writes=[("dbgmod",)])
        out_keys.append(("dbgmod",))

    def ln_stats(src_keyf):
        for blk in range(NPB):
            ln_stats_block(blk, src_keyf)

    def ln_stats_block(blk, src_keyf):
        if True:
            t0 = blk * PBLK
            for ccn in range(NCC):
                xs = x_sb[:, ccn * TC + t0: ccn * TC + t0 + PBLK]
                t_mm(bank(4), PS[4][:1, 0:PBLK], onec_sb[:, :], xs,
                     reads=[("c", "ones_col_bf"), src_keyf(ccn, blk)],
                     start=(ccn == 0), stop=(ccn == 7))
            for ccn in range(NCC):
                xs = x_sb[:, ccn * TC + t0: ccn * TC + t0 + PBLK]
                sqt = xsq_sb[ccn % 2]
                a_act(sqt[:, :], xs, AF.Square,
                      reads=[src_keyf(ccn, blk), ("xsq", ccn % 2)],
                      writes=[("xsq", ccn % 2)])
                t_mm(bank(5), PS[5][:1, 0:PBLK], onec_sb[:, :], sqt[:, :],
                     reads=[("c", "ones_col_bf"), ("xsq", ccn % 2)],
                     start=(ccn == 0), stop=(ccn == 7))
            rw = rows_sb
            P = PBLK
            v_ts("dve", rw[:1, 0:P], PS[4][:1, 0:P], 1.0 / C, None, ALU.mult, None,
                 reads=[bank(4), ("rows",)], writes=[("rows",)])
            v_ts("dve", rw[:1, P:2 * P], PS[5][:1, 0:P], 1.0 / C, None, ALU.mult, None,
                 reads=[bank(5), ("rows",)], writes=[("rows",)])
            v_tt("dve", rw[:1, 2 * P:3 * P], rw[:1, 0:P], rw[:1, 0:P], ALU.mult,
                 reads=[("rows",)], writes=[("rows",)])
            v_tt("dve", rw[:1, 2 * P:3 * P], rw[:1, P:2 * P], rw[:1, 2 * P:3 * P],
                 ALU.subtract, reads=[("rows",)], writes=[("rows",)])
            a_act(rw[:1, 3 * P:4 * P], rw[:1, 2 * P:3 * P], AF.Sqrt,
                  reads=[("rows",), ("epsv",)], writes=[("rows",)],
                  bias=epsv_sb[0:1, 0:1])
            v_recip("dve", rw[:1, 4 * P:5 * P], rw[:1, 3 * P:4 * P],
                    reads=[("rows",)], writes=[("rows",)])
            v_copy("dve", mrow_sb[:1, t0:t0 + P], rw[:1, 0:P],
                   reads=[("rows",)], writes=[("mrow", blk)])
            v_copy("dve", rrow_sb[:1, t0:t0 + P], rw[:1, 4 * P:5 * P],
                   reads=[("rows",)], writes=[("rrow", blk)])
            v_tt("dve", mrr_sb[:1, t0:t0 + P], mrow_sb[:1, t0:t0 + P],
                 rrow_sb[:1, t0:t0 + P], ALU.mult,
                 reads=[("mrow", blk), ("rrow", blk), ("mrr", blk)],
                 writes=[("mrr", blk)])

    def bcast_rows(blk):
        t0 = blk * PBLK
        t_mm(bank(4), PS[4][:, 0:PBLK], oner_sb[:, :], rrow_sb[:1, t0:t0 + PBLK],
             reads=[("c", "one_row_bf"), ("rrow", blk)], start=True, stop=False)
        t_mm(bank(4), PS[4][:, PBLK:2 * PBLK], oner_sb[:, :], mrr_sb[:1, t0:t0 + PBLK],
             reads=[("c", "one_row_bf"), ("mrr", blk)], start=False, stop=True)
        a_act(bcb_sb[:, :], PS[4][:, 0:2 * PBLK], AF.Copy,
              reads=[bank(4), ("bcb",)], writes=[("bcb",)])

    def h_block(blk, srckeyf, dstf, dkeyf, a_vec, s_vec, mkeys):
        t0 = blk * PBLK
        bcast_rows(blk)
        for ccn in range(NCC):
            xs = x_sb[:, ccn * TC + t0: ccn * TC + t0 + PBLK]
            h1 = hsc_sb[ccn % 2]
            v_tt("dve", h1[:, :], xs, bcb_sb[:, 0:PBLK], ALU.mult,
                 reads=[srckeyf(ccn, blk), ("bcb",), ("hsc", ccn % 2)],
                 writes=[("hsc", ccn % 2)])
            v_tt("dve", h1[:, :], h1[:, :], bcb_sb[:, PBLK:2 * PBLK], ALU.subtract,
                 reads=[("bcb",), ("hsc", ccn % 2)],
                 writes=[("hsc", ccn % 2)])
            v_ts("dve", dstf(ccn, blk), h1[:, :], a_vec[:, ccn:ccn + 1],
                 s_vec[:, ccn:ccn + 1], ALU.mult, ALU.add,
                 reads=[("hsc", ccn % 2)] + mkeys + [dkeyf(ccn, blk)],
                 writes=[dkeyf(ccn, blk)])

    NPK = 14
    PSLOT_COL = [18, 19, 20, 21, 15, 16, 17, 8, 9, 10, 11, 12, 13, 14]

    def pslot(i, m):
        base = PSLOT_COL[i] * TC + m * 1024
        return s_sb[:, base:base + 1024]

    def quant_gu(nat_name, ich, dstslot, dkey, svec, sveckey, mulq,
                 natb, natk, mx, mxk, ri, rik, sc, sck, q, rb0, qtr):
        pb.dma(q, natb[:, 0:1024], d[nat_name][ich * 128:(ich + 1) * 128, :],
               reads=[natk], writes=[natk])
        v_red("dve", mx[:, :], natb[:, 0:1024], ALU.max,
              reads=[natk, mxk], writes=[mxk], absval=True)
        v_recip("dve", ri[:, :], mx[:, :], reads=[mxk, rik],
                writes=[rik])
        v_ts("dve", ri[:, :], ri[:, :], QMAX, None, ALU.mult, None,
             reads=[rik], writes=[rik])
        if mulq:
            v_ts("dve", sc[:, :], mx[:, :], 1.0 / QMAX, None, ALU.mult, None,
                 reads=[mxk, sck], writes=[sck])
            v_tt("dve", svec[:, ich:ich + 1], sc[:, :], qsxB_sb[:, 0:1], ALU.mult,
                 reads=[sck, ("qsxB",)], writes=[(sveckey, ich)])
        else:
            v_ts("dve", svec[:, ich:ich + 1], mx[:, :], 1.0 / QMAX, None,
                 ALU.mult, None, reads=[mxk], writes=[(sveckey, ich)])
        for c2 in range(2):
            bq = rb0 + c2
            a_act(PS[bq][:, 0:512], natb[:, c2 * 512:(c2 + 1) * 512], AF.Copy,
                  reads=[natk, rik, ("magic",), bank(bq)], writes=[bank(bq)],
                  scale=ri[:, 0:1], bias=RMAGIC)
            a_act(wint_sb[:, c2 * 512:(c2 + 1) * 512], PS[bq][:, 0:512], AF.Copy,
                  reads=[bank(bq), ("magic",), ("wint",)], writes=[("wint",)],
                  bias=-RMAGIC)
        pb.dma_tr(qtr, dstslot[:, :].rearrange("p (j c) -> p j c", j=NCC),
                  wint_sb[:, 0:1024].rearrange("p (j c) -> p j c", j=NCC),
                  reads=[("wint",)], writes=[dkey + (j,) for j in range(NCC)])

    def quant_pk(i, mq=False, rb0=2):
        quant_gu("wg_nat", i, pslot(i, 0), ("pk", i, 0), sgv_sb, "sg", mq,
                 wdn_sb, ("wnat",), wmx_sb, ("wmx",), wri_sb, ("wri",),
                 wsc_sb, ("wsc",), "sp", rb0, "sp")
        quant_gu("wu_nat", i, pslot(i, 1), ("pk", i, 1), suv_sb, "su", mq,
                 wdq1_sb, ("wq1",), wmxu_sb, ("wmxu",), wriu_sb, ("wriu",),
                 wscu_sb, ("wscu",), "pool", rb0, "sp")

    def prep_wd(ccn):
        q = ccn // 2
        pb.dma("sp", wdn_sb[:, :],
               d["wd_nat"][ccn * 128:(ccn + 1) * 128, 0:IF // 2],
               reads=[("wnat",)], writes=[("wnat",)])
        pb.dma("act", wdq1_sb[:, :],
               d["wd_nat"][ccn * 128:(ccn + 1) * 128, IF // 2:IF],
               reads=[("wq1",)], writes=[("wq1",)])
        v_red("dve", wdmx_sb[:, :], wdn_sb[:, :], ALU.max,
              reads=[("wnat",), ("wdmx",)], writes=[("wdmx",)], absval=True)
        v_red("dve", wdsc_sb[:, :], wdq1_sb[:, :], ALU.max,
              reads=[("wq1",), ("wdsc",)], writes=[("wdsc",)], absval=True)
        v_tt("dve", wdmx_sb[:, :], wdmx_sb[:, :], wdsc_sb[:, :], ALU.max,
             reads=[("wdmx",), ("wdsc",)], writes=[("wdmx",)])
        v_recip("dve", wdri_sb[:, :], wdmx_sb[:, :], reads=[("wdmx",), ("wdri",)],
                writes=[("wdri",)])
        v_ts("dve", wdri_sb[:, :], wdri_sb[:, :], QMAX, None, ALU.mult, None,
             reads=[("wdri",)], writes=[("wdri",)])
        v_ts("dve", wdsc_sb[:, :], wdmx_sb[:, :], 1.0 / QMAX, None, ALU.mult,
             None, reads=[("wdmx",), ("wdsc",)], writes=[("wdsc",)])
        v_tt("dve", wdsc_sb[:, :], wdsc_sb[:, :], gb_c[:, ccn:ccn + 1], ALU.mult,
             reads=[("wdsc",), ("mt", 5)], writes=[("wdsc",)])
        for hf in range(2):
            natb, natk = ((wdn_sb, ("wnat",)) if hf == 0
                          else (wdq1_sb, ("wq1",)))
            for c2 in range(3):
                w = 512 if c2 < 2 else IF // 2 - 1024
                bq = 4 + (c2 % 2)
                a_act(PS[bq][:, 0:w], natb[:, c2 * 512:c2 * 512 + w], AF.Copy,
                      reads=[natk, ("wdri",), ("magic",), bank(bq)],
                      writes=[bank(bq)], scale=wdri_sb[:, 0:1], bias=RMAGIC)
                a_act(wdi_sb[:, c2 * 512:c2 * 512 + w], PS[bq][:, 0:w], AF.Copy,
                      reads=[bank(bq), ("magic",), ("wint",)],
                      writes=[("wint",)], bias=-RMAGIC)
            v_ts("dve", wdi_sb[:, :], wdi_sb[:, :], wdsc_sb[:, 0:1], None,
                 ALU.mult, None, reads=[("wint",), ("wdsc",)],
                 writes=[("wint",)])
            nj = NI // 2
            c0 = (ccn % 2) * 128
            pb.dma_tr("sp",
                      wdQ_sb[q % 2][:, :].rearrange(
                          "p (j k) -> p j k", k=256)[:, hf * nj:(hf + 1) * nj,
                                                     c0:c0 + 128],
                      wdi_sb[:, :].rearrange("p (j c) -> p j c", j=nj),
                      reads=[("wint",)],
                      writes=[("wdQ", q % 2, hf * nj + jj, ccn % 2)
                              for jj in range(nj)])


    def xkey(ccn, blk):
        return ("x", ccn)
    ln_stats(xkey)

    def hsl(ccn, blk):
        return h_sb[blk % 2][:, ccn * PBLK:(ccn + 1) * PBLK]

    def hkey(ccn, blk):
        return ("h", blk % 2, ccn)

    v_memset("dve", S_sb[:, :], 0.0, writes=[("S",)])

    def spill_h(b2):
        pb.dma("sp", hd[b2, :, :], h_sb[b2 % 2][:, :],
               reads=[("h", b2 % 2, ccn) for ccn in range(NCC)],
               writes=[("hd", b2)])

    h_block(0, xkey, hsl, hkey, a1_c, sa_c, [("mt", 1), ("mt", 0)])
    spill_h(0)
    for blk in range(NPB):
        for tci in range(2):
            ch = blk * 2 + tci
            kbuf = ktm_sb[ch % 2]
            vbuf = vtm_sb[ch % 2]
            for half in range(2):
                bk = bank(half)
                for ccn in range(NCC):
                    t_mm(bk, PS[half][:, 0:512],
                         hsl(ccn, blk)[:, tci * 128:(tci + 1) * 128],
                         wA_sb[:, ccn * 1024 + half * 512: ccn * 1024 + (half + 1) * 512],
                         reads=[hkey(ccn, blk), ("wA", ccn)],
                         start=(ccn == 0), stop=(ccn == 7))
                v_tt("dve", kbuf[:, half * 512:(half + 1) * 512], PS[half][:, 0:512],
                     gk_sb[:, half * 512:(half + 1) * 512], ALU.mult,
                     reads=[bk, ("c", "gk"), ("ktm", ch % 2)],
                     writes=[("ktm", ch % 2)])
                bkv = bank(2 + half)
                for ccn in range(NCC):
                    t_mm(bkv, PS[2 + half][:, 0:512],
                         hsl(ccn, blk)[:, tci * 128:(tci + 1) * 128],
                         wB_sb[:, ccn * 1024 + half * 512: ccn * 1024 + (half + 1) * 512],
                         reads=[hkey(ccn, blk), ("wB", ccn)],
                         start=(ccn == 0), stop=(ccn == 7))
                a_act(vbuf[:, half * 512:(half + 1) * 512], PS[2 + half][:, 0:512],
                      AF.Copy, reads=[bkv, ("vtm", ch % 2)], writes=[("vtm", ch % 2)])
            pb.dma("act", ktmd[ch, :, :], kbuf[:, :],
                   reads=[("ktm", ch % 2)], writes=[("ktmd", ch)])
            pb.dma("act", vtmd[ch, :, :], vbuf[:, :],
                   reads=[("vtm", ch % 2)], writes=[("vtmd", ch)])
            for h in range(H):
                hb = 4 if h < 8 else 5
                t_mm(bank(hb), PS[hb][0:DH, (h % 8) * DH:(h % 8 + 1) * DH],
                     kbuf[:, h * DH:(h + 1) * DH], vbuf[:, h * DH:(h + 1) * DH],
                     reads=[("ktm", ch % 2), ("vtm", ch % 2)],
                     start=(h % 8 == 0), stop=(h % 8 == 7))
            v_tt("dve", Sr_sb[:, :], S_sb[:, :], gdec_sb[:, :], ALU.mult,
                 reads=[("S",), ("c", "gdec"), ("Srecv",)], writes=[("Srecv",)])
            v_tt("dve", S_sb[:, 0:512], Sr_sb[:, 0:512], PS[4][0:DH, 0:512], ALU.add,
                 reads=[("Srecv",), bank(4), ("S",)], writes=[("S",)])
            v_tt("dve", S_sb[:, 512:1024], Sr_sb[:, 512:1024], PS[5][0:DH, 0:512],
                 ALU.add, reads=[("Srecv",), bank(5), ("S",)], writes=[("S",)])
            if tci == 0 and blk + 1 < NPB:
                h_block(blk + 1, xkey, hsl, hkey, a1_c, sa_c, [("mt", 1), ("mt", 0)])
                spill_h(blk + 1)

    v_ts("dve", Sr_sb[:, :], S_sb[:, :], smask_s_sb[:, 0:1], None, ALU.mult, None,
         reads=[("S",), ("c", "smask_send"), ("Srecv",)], writes=[("Srecv",)])
    pb.dma("pool", ccs_in.ap(), Sr_sb[:, :], reads=[("Srecv",)], writes=[("ccsin",)])
    pb.cc("ccstate", "AllReduce", ALU.add, [[0, 1], [2, 3], [4, 5], [6, 7]],
          ccs_in.ap().opt(), ccs_out.ap().opt(),
          reads=[("ccsin",)], writes=[("ccsout",)])
    pb.dma("pool", Sr_sb[:, :], ccs_out.ap(), reads=[("ccsout",), ("Srecv",)],
           writes=[("Srecv",)])
    v_ts("dve", S_sb[:, :], Sr_sb[:, :], smask_r_sb[:, 0:1], None, ALU.mult, None,
         reads=[("Srecv",), ("c", "smask_recv"), ("S",)], writes=[("S",)])
    v_copy("dve", Sbf_sb[:, :], S_sb[:, :], reads=[("S",), ("Sbf",)],
           writes=[("Sbf",)])

    if "state" in dbg:
        pb.dma("pool", dbg["state"][:, :], S_sb[:, :], reads=[("S",)],
               writes=[("dbgS",)])
        out_keys.append(("dbgS",))

    if phase_limit >= 2:
        def fetch_kv(ch):
            pb.dma("act", ktm_sb[ch % 2][:, :], ktmd[ch, :, :],
                   reads=[("ktmd", ch), ("ktm", ch % 2)], writes=[("ktm", ch % 2)])
            pb.dma("act", vtm_sb[ch % 2][:, :], vtmd[ch, :, :],
                   reads=[("vtmd", ch), ("vtm", ch % 2)], writes=[("vtm", ch % 2)])

        def fetch_h(b2):
            pb.dma("sp", h_sb[b2 % 2][:, :], hd[b2, :, :],
                   reads=[("hd", b2)] + [("h", b2 % 2, ccn) for ccn in range(NCC)],
                   writes=[("h", b2 % 2, ccn) for ccn in range(NCC)])

        def proj(blk):
            for (slot, skey, dst, dkey) in ((wB_sb, "wB", qt_sb, "qt"),
                                            (wA_sb, "wA", kt_sb, "kt")):
                for ecn in range(NCC):
                    bk = bank(ecn % 2)
                    for ccn in range(NCC):
                        t_mm(bk, PS[ecn % 2][:, 0:PBLK],
                             slot[:, ccn * 1024 + ecn * 128: ccn * 1024 + (ecn + 1) * 128],
                             hsl(ccn, blk),
                             reads=[(skey, ccn), hkey(ccn, blk)],
                             start=(ccn == 0), stop=(ccn == 7))
                    for sub in range(2):
                        hh = 2 * ecn + sub
                        dks = ([("qt", hh, 0), ("qt", hh, 1)] if dkey == "qt"
                               else [("kt", hh)])
                        a_act(dst[:DH, hh * PBLK:(hh + 1) * PBLK],
                              PS[ecn % 2][sub * 64:(sub + 1) * 64, 0:PBLK], AF.Copy,
                              reads=[bk] + dks, writes=dks)

        def scores(ch):
            blk, tci = ch // 2, ch % 2
            smbuf = stm_sb[ch % 2]
            for g in range(4):
                bks = bank(2 + g % 2)
                for hi in range(4):
                    hh = 4 * g + hi
                    sl = slice(hh * PBLK + tci * 128, hh * PBLK + tci * 128 + 128)
                    t_mm(bks, PS[2 + g % 2][:, hi * 128:(hi + 1) * 128],
                         kt_sb[:DH, sl], qt_sb[:DH, sl],
                         reads=[("kt", hh), ("qt", hh, tci)],
                         start=(hi == 0), stop=(hi == 3))
                v_tt("dve", smbuf[:, g * 512:(g + 1) * 512],
                     PS[2 + g % 2][:, 0:512], dm_sb[:, g * 512:(g + 1) * 512],
                     ALU.mult,
                     reads=[bks, ("c", "dmask"), ("stm", ch % 2)],
                     writes=[("stm", ch % 2)])

        def rest(ch):
            blk, tci = ch // 2, ch % 2
            kbuf = ktm_sb[ch % 2]
            vbuf = vtm_sb[ch % 2]
            smbuf = stm_sb[ch % 2]
            if sub2 >= 4:
                for h in range(H):
                    hb = 0 if h < 8 else 1
                    t_mm(bank(hb), PS[hb][0:DH, (h % 8) * DH:(h % 8 + 1) * DH],
                         kbuf[:, h * DH:(h + 1) * DH], vbuf[:, h * DH:(h + 1) * DH],
                         reads=[("ktm", ch % 2), ("vtm", ch % 2)],
                         start=(h % 8 == 0), stop=(h % 8 == 7))
            qv = qt_sb[:DH, :].rearrange("p (h t) -> p h t", h=H)[:, :, tci * 128:(tci + 1) * 128]
            gv = gq2_sb[:DH, :].rearrange("p (h t) -> p h t", h=H)
            v_tt("dve", qv, qv, gv, ALU.mult,
                 reads=[("c", "gq2")] + [("qt", hh, tci) for hh in range(H)],
                 writes=[("qt", hh, tci) for hh in range(H)])
            for g in range(4 if sub2 >= 3 else 0):
                bky = bank(4 + g % 2)
                for hi in range(4):
                    hh = 4 * g + hi
                    t_mm(bky, PS[4 + g % 2][0:DH, hi * 128:(hi + 1) * 128],
                         vbuf[:, hh * DH:(hh + 1) * DH],
                         smbuf[:, (g * 4 + hi) * 128:(g * 4 + hi + 1) * 128],
                         reads=[("vtm", ch % 2), ("stm", ch % 2)],
                         start=(hi == 0), stop=(False if sub3 >= 2 else hi == 3))
                if sub3 >= 2:
                    for hi in range(4):
                        hh = 4 * g + hi
                        sl = slice(hh * PBLK + tci * 128, hh * PBLK + tci * 128 + 128)
                        t_mm(bky, PS[4 + g % 2][0:DH, hi * 128:(hi + 1) * 128],
                             Sbf_sb[:DH, hh * DH:(hh + 1) * DH], qt_sb[:DH, sl],
                             reads=[("Sbf",), ("qt", hh, tci)],
                             start=False, stop=(hi == 3))
                for hi in range(4):
                    hh = 4 * g + hi
                    ccn = hh // 2
                    r0 = (hh % 2) * 64 if sub3 >= 3 else 0
                    a_act(yb_sb[blk % 2][r0:r0 + 64,
                                         ccn * PBLK + tci * 128: ccn * PBLK + tci * 128 + 128],
                          PS[4 + g % 2][0:DH, hi * 128:(hi + 1) * 128], AF.Copy,
                          reads=[bky, ("yb", blk % 2)], writes=[("yb", blk % 2)])
            if sub2 >= 4:
                v_tt("dve", Sr_sb[:, :], S_sb[:, :], gdec_sb[:, :], ALU.mult,
                     reads=[("S",), ("c", "gdec"), ("Srecv",)], writes=[("Srecv",)])
                v_tt("dve", S_sb[:, 0:512], Sr_sb[:, 0:512], PS[0][0:DH, 0:512],
                     ALU.add, reads=[("Srecv",), bank(0), ("S",)], writes=[("S",)])
                v_tt("dve", S_sb[:, 512:1024], Sr_sb[:, 512:1024], PS[1][0:DH, 0:512],
                     ALU.add, reads=[("Srecv",), bank(1), ("S",)], writes=[("S",)])
                v_copy("dve", Sbf_sb[:, :], S_sb[:, :], reads=[("S",), ("Sbf",)],
                       writes=[("Sbf",)])
            if ch + 2 < NRC:
                fetch_kv(ch + 2)

        def gn_stats_y(blk):
            for ccn in range(NCC if sub2 >= 5 else 0):
                ys = yb_sb[blk % 2][:, ccn * PBLK:(ccn + 1) * PBLK]
                col = blk * 8 + ccn
                a_act(hsc_sb[0][:, :], ys, AF.Copy,
                      reads=[("yb", blk % 2), ("hsc", 0)],
                      writes=[("hsc", 0), ("ysum", col)],
                      accum=ysum_sb[:, col:col + 1])
                a_act(hsc_sb[1][:, :], ys, AF.Square,
                      reads=[("yb", blk % 2), ("hsc", 1)],
                      writes=[("hsc", 1), ("ysq", col)],
                      accum=ysq_sb[:, col:col + 1])
            if sub2 >= 3:
                pb.dma("act", yd[blk, :, :], yb_sb[blk % 2][:, :],
                       reads=[("yb", blk % 2)], writes=[("yd", blk)])
                if blk < 2:
                    pb.dma("act", yl_sb[blk][:, :], yd[blk, :, :],
                           reads=[("yd", blk), ("yl", blk)], writes=[("yl", blk)])

        fetch_h(0)
        fetch_h(1)
        load_weight(wB_sb, "wq_l", "wB")
        fetch_kv(0)
        fetch_kv(1)
        for blk in range(NPB):
            proj(blk)
            if blk == NPB - 1:
                load_weight(wA_sb, "wo_l", "wA")
            if blk > 0:
                gn_stats_y(blk - 1)
            if blk + 2 < NPB:
                fetch_h(blk + 2)
            if sub2 >= 2:
                scores(2 * blk)
                scores(2 * blk + 1)
                rest(2 * blk)
                rest(2 * blk + 1)
        gn_stats_y(NPB - 1)

        if sub2 < 6:
            pb.dma("sp", out_d[0:128, 0:1024], xst_sb[:, 0:1024],
                   reads=[("xst",)], writes=[("outz",)])
            pb.op("sp", lambda e: e.nop(), reads=[("outz",)], writes=())
            pb.finalize()
            _emit(nc, pb)
            return nc
        ysr = ysum_sb[:, :].rearrange("p (a b) -> p b a", a=NPB)
        v_red("dve", ysum8_sb[:, :], ysr, ALU.add,
              reads=[("ysum", c2) for c2 in range(64)], writes=[("ysum8",)])
        ysr2 = ysq_sb[:, :].rearrange("p (a b) -> p b a", a=NPB)
        v_red("dve", ysq8_sb[:, :], ysr2, ALU.add,
              reads=[("ysq", c2) for c2 in range(64)], writes=[("ysq8",)])
        t_mm(bank(2), PS[2][0:8, 0:128], ysum8_sb[:, :], idf_sb[:, :],
             reads=[("ysum8",), ("c", "ident_f32")], start=True, stop=True, is_tr=True)
        t_mm(bank(3), PS[3][0:8, 0:128], ysq8_sb[:, :], idf_sb[:, :],
             reads=[("ysq8",), ("c", "ident_f32")], start=True, stop=True, is_tr=True)
        v_red("dve", gsum_sb[:, :],
              PS[2][0:8, 0:128].rearrange("p (h e) -> p h e", h=2), ALU.add,
              reads=[bank(2)], writes=[("gsum",)])
        v_red("dve", gsq_sb[:, :],
              PS[3][0:8, 0:128].rearrange("p (h e) -> p h e", h=2), ALU.add,
              reads=[bank(3)], writes=[("gsq",)])
        pb.dma("pool", gn_in[0:16, :], gsum_sb[:, :], reads=[("gsum",)],
               writes=[("gnin",)], slow=True)
        pb.dma("pool", gn_in[16:32, :], gsq_sb[:, :], reads=[("gsq",), ("gnin",)],
               writes=[("gnin",)], slow=True)
        pb.cc("ccgn", "AllReduce", ALU.add, [[0, 1], [2, 3], [4, 5], [6, 7]],
              gn_in.ap().opt(), gn_out.ap().opt(),
              reads=[("gnin",)], writes=[("gnout",)])
        pb.dma("pool", gsum2_sb[:, :],
               gn_out[0:16, :].rearrange("(a b) c -> b (a c)", b=2),
               reads=[("gnout",)], writes=[("gsum2",)], slow=True)
        pb.dma("pool", gsq2_sb[:, :],
               gn_out[16:32, :].rearrange("(a b) c -> b (a c)", b=2),
               reads=[("gnout",)], writes=[("gsq2",)], slow=True)
        for ccn in range(NCC):
            t_mm(bank(2), PS[2][:, ccn:ccn + 1], selT_sb[:, :], gsum2_sb[:, ccn:ccn + 1],
                 reads=[("c", "selT"), ("gsum2",)], start=(ccn == 0), stop=(ccn == 7))
            t_mm(bank(3), PS[3][:, ccn:ccn + 1], selT_sb[:, :], gsq2_sb[:, ccn:ccn + 1],
                 reads=[("c", "selT"), ("gsq2",)], start=(ccn == 0), stop=(ccn == 7))
        NTOT = float(T * DH)
        v_ts("dve", mu8_sb[:, :], PS[2][:, 0:8], 1.0 / NTOT, None, ALU.mult, None,
             reads=[bank(2)], writes=[("mu8",)])
        v_ts("dve", sq8_sb[:, :], PS[3][:, 0:8], 1.0 / NTOT, None, ALU.mult, None,
             reads=[bank(3)], writes=[("sq8",)])
        v_tt("dve", Av_sb[:, :], mu8_sb[:, :], mu8_sb[:, :], ALU.mult,
             reads=[("mu8",), ("Av",)], writes=[("Av",)])
        v_tt("dve", Av_sb[:, :], sq8_sb[:, :], Av_sb[:, :], ALU.subtract,
             reads=[("sq8",), ("Av",)], writes=[("Av",)])
        a_act(Bv_sb[:, :], Av_sb[:, :], AF.Sqrt, reads=[("Av",), ("Bv",), ("epsv",)],
              writes=[("Bv",)], bias=epsv_sb[:, 0:1])
        v_recip("dve", Av_sb[:, :], Bv_sb[:, :], reads=[("Bv",), ("Av",)],
                writes=[("Av",)])
        v_tt("dve", Av_sb[:, :], Av_sb[:, :], gnw_sb[:, :], ALU.mult,
             reads=[("Av",), ("c", "gnw_v")], writes=[("Av",)])
        v_tt("dve", Bv_sb[:, :], mu8_sb[:, :], Av_sb[:, :], ALU.mult,
             reads=[("mu8",), ("Av",), ("Bv",)], writes=[("Bv",)])
        v_tt("dve", Bv_sb[:, :], gnb_sb[:, :], Bv_sb[:, :], ALU.subtract,
             reads=[("c", "gnb_v"), ("Bv",)], writes=[("Bv",)])

        if sub2 < 7:
            pb.dma("sp", out_d[0:128, 0:1024], xst_sb[:, 0:1024],
                   reads=[("xst",)], writes=[("outz",)])
            pb.op("sp", lambda e: e.nop(), reads=[("outz",)], writes=())
            pb.finalize()
            _emit(nc, pb)
            return nc
        pb.barrier()
        v_copy("dve", Bvb_sb[:, :], Bv_sb[:, :], reads=[("Bv",)], writes=[("Bvb",)])
        for half in range(2):
            bkb = bank(2 + half)
            for ccn in range(NCC):
                t_mm(bkb, PS[2 + half][:1, 0:512], Bvb_sb[:, ccn:ccn + 1],
                     wA_sb[:, ccn * 1024 + half * 512: ccn * 1024 + (half + 1) * 512],
                     reads=[("Bvb",), ("wA", ccn)],
                     start=(ccn == 0), stop=(ccn == 7))
            a_act(dm_sb[0:1, half * 512:(half + 1) * 512], PS[2 + half][:1, 0:512],
                  AF.Copy, reads=[bkb, ("biasr",)], writes=[("biasr",)])
        for ccn in range(NCC):
            v_ts("dve", wA_sb[:, ccn * 1024:(ccn + 1) * 1024],
                 wA_sb[:, ccn * 1024:(ccn + 1) * 1024], Av_sb[:, ccn:ccn + 1],
                 None, ALU.mult, None,
                 reads=[("wA", ccn), ("Av",)], writes=[("wA", ccn)])

        def _stats_h2(blk):
            ln_stats_block(blk, lambda c2, b2: ("x2", c2, b2))
            if phase_limit >= 3:
                h_block(blk, lambda c2, b2: ("x2", c2, b2),
                        lambda c2, b2: h2_sb[:, c2 * TC + b2 * PBLK: c2 * TC + (b2 + 1) * PBLK],
                        lambda c2, b2: ("h2", c2, b2), a2_c, sb_c,
                        [("mt", 4), ("mt", 3)])
                for ccn in range(NCC):
                    col = blk * 8 + ccn
                    v_red("dve", hmax_sb[:, col:col + 1],
                          h2_sb[:, ccn * TC + blk * PBLK: ccn * TC + (blk + 1) * PBLK],
                          ALU.max, reads=[("h2", ccn, blk)],
                          writes=[("hmaxc", col)], absval=True)

        for blk in range(NPB):
            ybuf = yl_sb[blk % 2]
            for con in range(NCC):
                bk = bank(con % 2)
                for ccn in range(NCC):
                    t_mm(bk, PS[con % 2][:, 0:PBLK],
                         wA_sb[:, ccn * 1024 + con * 128: ccn * 1024 + (con + 1) * 128],
                         ybuf[:, ccn * PBLK:(ccn + 1) * PBLK],
                         reads=[("wA", ccn), ("yl", blk % 2)],
                         start=(ccn == 0), stop=False)
                t_mm(bk, PS[con % 2][:, 0:PBLK],
                     dm_sb[0:1, con * 128:(con + 1) * 128], onr2_sb[0:1, :],
                     reads=[("biasr",), ("c", "one_row256")],
                     start=False, stop=True)
                xs = x_sb[:, con * TC + blk * PBLK: con * TC + (blk + 1) * PBLK]
                v_stt("dve", xs, PS[con % 2][:, 0:PBLK], ga_c[:, con:con + 1], xs,
                      ALU.mult, ALU.add,
                      reads=[bk, ("mt", 2), ("x", con)],
                      writes=[("x", con), ("x2", con, blk)])
            if blk + 2 < NPB:
                pb.dma("act", yl_sb[blk % 2][:, :], yd[blk + 2, :, :],
                       reads=[("yd", blk + 2), ("yl", blk % 2)],
                       writes=[("yl", blk % 2)])
            if 1 <= blk <= 4:
                quant_pk(blk - 1)
            if blk > 0:
                _stats_h2(blk - 1)
        _stats_h2(NPB - 1)

        if "x2" in dbg:
            pb.dma("pool", dbg["x2"][:, :], x_sb[:, :],
                   reads=[("x2", ccn, blk) for ccn in range(NCC) for blk in range(NPB)],
                   writes=[("dbgx2",)])
            out_keys.append(("dbgx2",))
        if "y" in dbg:
            def fnd2d(e, o2=dbg["y"].ap(), i=yd.ap()):
                return e.dma_start(out=o2, in_=i)
            pb.op("pool", fnd2d, reads=[("yd", blk) for blk in range(NPB)],
                  writes=[("dbgy",)], amt=16, kind="dma")
            out_keys.append(("dbgy",))

    if phase_limit >= 3:
        pb.barrier()

        def x2key(ccn, blk):
            return ("x2", ccn, blk)

        def h2sl(ccn, blk):
            return h2_sb[:, ccn * TC + blk * PBLK: ccn * TC + (blk + 1) * PBLK]

        if sub4 < 1:
            pb.dma("sp", out_d[0:128, 0:1024], xst_sb[:, 0:1024],
                   reads=[("xst",)], writes=[("outz2",)])
            pb.op("sp", lambda e: e.nop(), reads=[("outz2",)], writes=())
            pb.finalize()
            _emit(nc, pb)
            return nc
        if sub4 < 2:
            pb.dma("sp", out_d[0:128, 0:1024], xst_sb[:, 0:1024],
                   reads=[("xst",)], writes=[("outz2",)])
            pb.op("sp", lambda e: e.nop(), reads=[("outz2",)], writes=())
            pb.finalize()
            _emit(nc, pb)
            return nc
        v_red("dve", hmax1_sb[:, :], hmax_sb[:, :], ALU.max,
              reads=[("hmaxc", c2) for c2 in range(64)], writes=[("hmax1",)])
        if sub4 < 3:
            pb.dma("sp", out_d[0:128, 0:1024], xst_sb[:, 0:1024],
                   reads=[("xst",)], writes=[("outz2",)])
            pb.op("sp", lambda e: e.nop(), reads=[("outz2",)], writes=())
            pb.finalize()
            _emit(nc, pb)
            return nc
        t_mm(bank(4), PS[4][0:1, 0:128], hmax1_sb[:, 0:1], idf_sb[:, :],
             reads=[("hmax1",), ("c", "ident_f32")], start=True, stop=True, is_tr=True)
        v_red("dve", sA_sb[:, :], PS[4][0:1, 0:128], ALU.max,
              reads=[bank(4)], writes=[("sA",)])
        if sub4 < 4:
            pb.dma("sp", out_d[0:128, 0:1024], xst_sb[:, 0:1024],
                   reads=[("xst",)], writes=[("outz2",)])
            pb.op("sp", lambda e: e.nop(), reads=[("outz2",)], writes=())
            pb.finalize()
            _emit(nc, pb)
            return nc
        v_memset("dve", sc16_sb[:, :], 0.0, writes=[("sc16",)])
        v_copy("dve", sc16_sb[0:1, :], sA_sb[:, :], reads=[("sA",), ("sc16",)],
               writes=[("sc16",)])
        pb.dma("pool", mx1_in.ap(), sc16_sb[:, :], reads=[("sc16",)],
               writes=[("mx1in",)])
        pb.cc("ccmax1", "AllReduce", ALU.max, [list(range(8))],
              mx1_in.ap().opt(), mx1_out.ap().opt(),
              reads=[("mx1in",)], writes=[("mx1out",)])
        pb.dma("pool", sc16b_sb[:, :], mx1_out.ap(), reads=[("mx1out",), ("sc16b",)],
               writes=[("sc16b",)])
        a_act(sA_sb[:, :], clipx_sb[:, :], AF.Sigmoid,
              reads=[("c", "clipx"), ("sA",)], writes=[("sA",)])
        v_tt("dve", sA_sb[:, :], sA_sb[:, :], sc16b_sb[0:1, 0:1], ALU.mult,
             reads=[("sA",), ("sc16b",)], writes=[("sA",)])
        v_ts("dve", sA_sb[:, :], sA_sb[:, :], 1.0 / QMAX, None, ALU.mult, None,
             reads=[("sA",)], writes=[("sA",)])
        v_recip("dve", sB_sb[:, :], sA_sb[:, :], reads=[("sA",), ("sB",)],
                writes=[("sB",)])
        def bcast_scalar(src_key, src_ap, dst_ap, dst_key, bcol):
            t_mm(bank(5), PS[5][:, bcol:bcol + 1], onerf_sb[:, :], src_ap,
                 reads=[("c", "one_row_f32"), src_key], start=True, stop=True)
            v_copy("dve", dst_ap, PS[5][:, bcol:bcol + 1], reads=[bank(5), dst_key],
                   writes=[dst_key])

        if sub4 < 5:
            pb.dma("sp", out_d[0:128, 0:1024], xst_sb[:, 0:1024],
                   reads=[("xst",)], writes=[("outz2",)])
            pb.op("sp", lambda e: e.nop(), reads=[("outz2",)], writes=())
            pb.finalize()
            _emit(nc, pb)
            return nc
        bcast_scalar(("sB",), sB_sb[:, :], rB_sb[:, :], ("rB",), 0)
        bcast_scalar(("sA",), sA_sb[:, :], qsxB_sb[:, :], ("qsxB",), 1)
        if sub4 < 6:
            pb.dma("sp", out_d[0:128, 0:1024], xst_sb[:, 0:1024],
                   reads=[("xst",)], writes=[("outz2",)])
            pb.op("sp", lambda e: e.nop(), reads=[("outz2",)], writes=())
            pb.finalize()
            _emit(nc, pb)
            return nc
        for blk in range(NPB):
            for ccn in range(NCC):
                hs = h2sl(ccn, blk)
                if ccn % 2 == 0:
                    bq = 4 + (blk % 2)
                    a_act(PS[bq][:, 0:PBLK], hs, AF.Copy,
                          reads=[("h2", ccn, blk), ("rB",), ("magic",), bank(bq)],
                          writes=[bank(bq)], scale=rB_sb[:, 0:1], bias=RMAGIC)
                    a_act(hs, PS[bq][:, 0:PBLK], AF.Copy,
                          reads=[bank(bq), ("magic",), ("h2", ccn, blk)],
                          writes=[("h2", ccn, blk)], bias=-RMAGIC)
                    v_ts("dve", hs, hs, QMAX, -QMAX - 1.0, ALU.min, ALU.max,
                         reads=[("h2", ccn, blk)], writes=[("h2", ccn, blk)])
                else:
                    bq = 2 + (blk % 2)
                    v_ts("dve", PS[bq][:, 0:PBLK], hs, rB_sb[:, 0:1], RMAGIC,
                         ALU.mult, ALU.add,
                         reads=[("h2", ccn, blk), ("rB",), bank(bq)],
                         writes=[bank(bq)])
                    v_ts("dve", hs, PS[bq][:, 0:PBLK], RMAGIC, QMAX,
                         ALU.subtract, ALU.min,
                         reads=[bank(bq), ("h2", ccn, blk)],
                         writes=[("h2", ccn, blk)])
                    v_ts("dve", hs, hs, -QMAX - 1.0, None, ALU.max, None,
                         reads=[("h2", ccn, blk)], writes=[("h2", ccn, blk)])

        for i in range(4, NPK):
            quant_pk(i, mq=True, rb0=4)
        for q4 in range(4):
            prep_wd(q4)

        if "xq" in dbg:
            pb.dma("pool", dbg["xq"][:, :], h2_sb[:, :],
                   reads=[("h2", ccn, blk) for ccn in range(NCC) for blk in range(NPB)],
                   writes=[("dbgxq",)])
            out_keys.append(("dbgxq",))

    if phase_limit >= 4:

        v_ts("dve", sgv_sb[:, 0:4], sgv_sb[:, 0:4], qsxB_sb[:, 0:1], None,
             ALU.mult, None, reads=[("qsxB",)] + [("sg", i) for i in range(4)],
             writes=[("sg", i) for i in range(4)])
        v_ts("dve", suv_sb[:, 0:4], suv_sb[:, 0:4], qsxB_sb[:, 0:1], None,
             ALU.mult, None, reads=[("qsxB",)] + [("su", i) for i in range(4)],
             writes=[("su", i) for i in range(4)])
        def quant_jit(ich):
            quant_gu("wg_nat", ich, wig_sb[ich % 2], ("wig", ich % 2),
                     sgv_sb, "sg", True,
                     wdn_sb, ("wnat",), wmx_sb, ("wmx",), wri_sb, ("wri",),
                     wsc_sb, ("wsc",), "sp", 4, "sp")
            quant_gu("wu_nat", ich, wiu_sb[ich % 2], ("wiu", ich % 2),
                     suv_sb, "su", True,
                     wdq1_sb, ("wq1",), wmxu_sb, ("wmxu",), wriu_sb, ("wriu",),
                     wscu_sb, ("wscu",), "act", 4, "sp")

        quant_jit(NPK)
        for ich in range(NI):
            if ich < NPK:
                wgt, gkey = pslot(ich, 0), ("pk", ich, 0)
                wut, ukey = pslot(ich, 1), ("pk", ich, 1)
            else:
                wgt, gkey = wig_sb[ich % 2], ("wig", ich % 2)
                wut, ukey = wiu_sb[ich % 2], ("wiu", ich % 2)
            if NPK <= ich + 1 < NI:
                quant_jit(ich + 1)
            for blk in range(NFB):
                bkg = bank((2 * blk) % 4)
                bku = bank((2 * blk + 1) % 4)
                for ccn in range(NCC):
                    xqs = h2_sb[:, ccn * TC + blk * FBLK: ccn * TC + (blk + 1) * FBLK]
                    t_mm(bkg, PS[(2 * blk) % 4][:, 0:FBLK],
                         wgt[:, ccn * 128:(ccn + 1) * 128], xqs,
                         reads=[gkey + (ccn,), ("h2", ccn, 2 * blk),
                                ("h2", ccn, 2 * blk + 1)],
                         start=(ccn == 0), stop=(ccn == 7))
                for ccn in range(NCC):
                    xqs = h2_sb[:, ccn * TC + blk * FBLK: ccn * TC + (blk + 1) * FBLK]
                    t_mm(bku, PS[(2 * blk + 1) % 4][:, 0:FBLK],
                         wut[:, ccn * 128:(ccn + 1) * 128], xqs,
                         reads=[ukey + (ccn,), ("h2", ccn, 2 * blk),
                                ("h2", ccn, 2 * blk + 1)],
                         start=(ccn == 0), stop=(ccn == 7))
                gsl = gsl_sb[blk % 2]
                a_act(gsl[:, :], PS[(2 * blk) % 4][:, 0:FBLK], AF.Silu,
                      reads=[bkg, ("sg", ich), ("gsl", blk % 2)],
                      writes=[("gsl", blk % 2)], scale=sgv_sb[:, ich:ich + 1])
                ss = s_sb[:, ich * TC + blk * FBLK: ich * TC + (blk + 1) * FBLK]
                v_tt("dve", ss, PS[(2 * blk + 1) % 4][:, 0:FBLK], gsl[:, :], ALU.mult,
                     reads=[bku, ("gsl", blk % 2)], writes=[("s", ich, blk)])
            v_red("dve", hmax1_sb[:, :], s_sb[:, ich * TC:(ich + 1) * TC], ALU.max,
                  reads=[("s", ich, b2) for b2 in range(NFB)] + [("hmax1",)],
                  writes=[("hmax1",)], absval=True)
            v_tt("dve", smax_sb[:, ich:ich + 1], hmax1_sb[:, :],
                 suv_sb[:, ich:ich + 1], ALU.mult,
                 reads=[("hmax1",), ("su", ich)], writes=[("smc", ich)])

        v_red("dve", smax1_sb[:, 0:1], smax_sb[:, 0:NI], ALU.max,
              reads=[("smc", c2) for c2 in range(NI)], writes=[("smax1",)])
        t_mm(bank(4), PS[4][0:1, 0:128], smax1_sb[:, 0:1], idf_sb[:, :],
             reads=[("smax1",), ("c", "ident_f32")], start=True, stop=True, is_tr=True)
        v_red("dve", sA_sb[:, :], PS[4][0:1, 0:128], ALU.max,
              reads=[bank(4), ("sA",)], writes=[("sA",)])
        v_copy("dve", sc16_sb[0:1, :], sA_sb[:, :], reads=[("sA",), ("sc16",)],
               writes=[("sc16",)])
        pb.dma("pool", mx2_in.ap(), sc16_sb[:, :], reads=[("sc16",)],
               writes=[("mx2in",)])
        pb.cc("ccmax2", "AllReduce", ALU.max, [list(range(8))],
              mx2_in.ap().opt(), mx2_out.ap().opt(),
              reads=[("mx2in",)], writes=[("mx2out",)])
        pb.dma("pool", sc16b_sb[:, :], mx2_out.ap(), reads=[("mx2out",), ("sc16b",)],
               writes=[("sc16b",)])
        a_act(sA_sb[:, :], clipd_sb[:, :], AF.Sigmoid,
              reads=[("c", "clipd"), ("sA",)], writes=[("sA",)])
        v_tt("dve", sA_sb[:, :], sA_sb[:, :], sc16b_sb[0:1, 0:1], ALU.mult,
             reads=[("sA",), ("sc16b",)], writes=[("sA",)])
        v_ts("dve", sA_sb[:, :], sA_sb[:, :], 1.0 / QMAX, None, ALU.mult, None,
             reads=[("sA",)], writes=[("sA",)])
        v_recip("dve", sB_sb[:, :], sA_sb[:, :], reads=[("sA",), ("sB",)],
                writes=[("sB",)])
        bcast_scalar(("sB",), sB_sb[:, :], rs2B_sb[:, :], ("rs2B",), 2)
        bcast_scalar(("sA",), sA_sb[:, :], qsdB_sb[:, :], ("qsdB",), 3)

        def quant_s_half(hf2, ichs=None):
            for ich in (range(NI) if ichs is None else ichs):
                v_tt("dve", sqc_sb[:, :], suv_sb[:, ich:ich + 1], rs2B_sb[:, 0:1],
                     ALU.mult, reads=[("su", ich), ("rs2B",), ("sqc",)],
                     writes=[("sqc",)])
                srd = [("s", ich, 2 * hf2), ("s", ich, 2 * hf2 + 1)]
                for c2 in range(2):
                    ss = s_sb[:, ich * TC + hf2 * 1024 + c2 * 512:
                              ich * TC + hf2 * 1024 + (c2 + 1) * 512]
                    bq = 4 + c2
                    a_act(PS[bq][:, 0:512], ss, AF.Copy,
                          reads=srd + [("sqc",), ("magic",), bank(bq)],
                          writes=[bank(bq)], scale=sqc_sb[:, 0:1], bias=RMAGIC)
                    a_act(ss, PS[bq][:, 0:512], AF.Copy,
                          reads=[bank(bq), ("magic",)] + srd,
                          writes=srd, bias=-RMAGIC)
                ssf = s_sb[:, ich * TC + hf2 * 1024: ich * TC + (hf2 + 1) * 1024]
                v_ts("dve", ssf, ssf, QMAX, -QMAX - 1.0, ALU.min, ALU.max,
                     reads=srd, writes=srd)

        quant_s_half(0)

        if "s" in dbg:
            quant_s_half(1)
            pb.dma("pool", dbg["s"][:, :], s_sb[:, :],
                   reads=[("s", ich, b2) for ich in range(NI) for b2 in range(NFB)],
                   writes=[("dbgs",)])
            out_keys.append(("dbgs",))

    if phase_limit >= 5:
        def down_pass(p):
            for tb in range(NFB):
                if p == 0 and tb < 2 and "s" not in dbg:
                    quant_s_half(1, range(11 * tb, 11 * (tb + 1)))
                if p in (1, 2) and tb == 1:
                    prep_wd(2 * (p + 1))
                if p in (1, 2) and tb == 3:
                    prep_wd(2 * (p + 1) + 1)
                for k in range(2):
                    ccn = 2 * p + k
                    g = 2 * tb + k
                    bk = bank(g % 4)
                    psd = PS[g % 4]
                    for ich in range(NI):
                        t_mm(bk, psd[:, 0:FBLK],
                             wdQ_sb[p % 2][:, ich * 256 + k * 128:
                                           ich * 256 + k * 128 + 128],
                             s_sb[:, ich * TC + tb * FBLK: ich * TC + (tb + 1) * FBLK],
                             reads=[("s", ich, tb), ("wdQ", p % 2, ich, k)],
                             start=(ich == 0), stop=(ich == NI - 1))
                    ob = outb_sb[:, (g % 2) * FBLK:(g % 2) * FBLK + FBLK]
                    v_stt("dve", ob, psd[:, 0:FBLK], qsdB_sb[:, 0:1],
                          x_sb[:, ccn * TC + tb * FBLK: ccn * TC + (tb + 1) * FBLK],
                          ALU.mult, ALU.add,
                          reads=[bk, ("qsdB",), ("x2", ccn, 2 * tb),
                                 ("x2", ccn, 2 * tb + 1), ("outb", g % 2)],
                          writes=[("outb", g % 2)])
                    pb.dma("sp",
                           out_d[ccn * 128:(ccn + 1) * 128, tb * FBLK:(tb + 1) * FBLK],
                           ob, reads=[("outb", g % 2)], writes=[("outd", p, g)])
                    out_keys.append(("outd", p, g))

        for p in range(4):
            down_pass(p)
    else:
        pb.dma("sp", out_d[0:128, 0:1024], xst_sb[:, 0:1024],
               reads=[("xst",)], writes=[("outz",)])
        out_keys.append(("outz",))

    pb.op("sp", lambda e: e.nop(), reads=list(out_keys), writes=())

    pb.finalize()
    _emit(nc, pb)
    return nc


def _emit(nc, pb):
    import contextlib
    stack = contextlib.ExitStack()
    sems = {}
    for s in sorted(pb.sem_names):
        sems[s] = stack.enter_context(nc.semaphore(s))
    by_eng = {e: [] for e in ENGS}
    for o2 in pb.ops:
        by_eng[o2.eng].append(o2)

    def emit_stream(eng_obj, ops):
        for o2 in ops:
            for s, v in o2.waits:
                eng_obj.wait_ge(sems[s], v)
            inst = o2.fn(eng_obj)
            if o2.token is not None and inst is not None:
                inst.then_inc(sems[o2.token[0]], o2.amt)

    with nc.Block() as blk:
        @blk.tensor
        def _(e):
            emit_stream(e, by_eng["pe"])

        @blk.scalar
        def _(e):
            emit_stream(e, by_eng["act"])

        @blk.vector
        def _(e):
            emit_stream(e, by_eng["dve"])

        @blk.gpsimd
        def _(e):
            emit_stream(e, by_eng["pool"])

        @blk.sync
        def _(e):
            emit_stream(e, by_eng["sp"])
    stack.close()


_CACHED = {}

def kernel(**inputs):
    in_maps = _host_prepare(inputs)
    key = (tuple(DEBUG_TAPS), PHASE_LIMIT, SUB2, SUB3, SUB4)
    if key not in _CACHED:
        _CACHED[key] = build_program(DEBUG_TAPS, PHASE_LIMIT, SUB2, SUB3, SUB4)
    nc = _CACHED[key]
    res = run_bass_kernel_spmd(nc, in_maps, core_ids=list(range(8)))
    out = np.zeros((B, T, C), dtype=np.float32)
    for core in range(8):
        b, half = core // 2, core % 2
        out[b, half * TC:(half + 1) * TC, :] = res.results[core]["out"].T
    kernel.last_results = res
    return out


# version 11
# speedup vs baseline: 1.0145x; 1.0145x over previous
import math
import numpy as np
import ml_dtypes

import concourse.bass as bass
import concourse.mybir as mybir
from concourse.bass_utils import run_bass_kernel_spmd

F32 = mybir.dt.float32
F32R = mybir.dt.float32r
BF16 = mybir.dt.bfloat16
AF = mybir.ActivationFunctionType
ALU = mybir.AluOpType
AXX = mybir.AxisListType.X

B, T, C, H, DH = 4, 4096, 1024, 16, 64
TC = 2048
NCC = 8
PBLK = 256
NPB = TC // PBLK
RCH = 128
NRC = TC // RCH
FBLK = 512
NFB = TC // FBLK
IF = 2816
NI = IF // 128
QMAX = 127.0
EPS = 1e-5
RMAGIC = 1.5 * 2.0 ** 23

_glo = math.log(1.0 - 0.999)
_ghi = math.log(1.0 - 0.9)
GAMMAS = [1.0 - math.exp(_glo + (_ghi - _glo) * h / (H - 1)) for h in range(H)]

DEBUG_TAPS = ()
PHASE_LIMIT = 99
SUB2 = 9
SUB3 = 3
SUB4 = 9

ENGS = ("pe", "act", "dve", "pool", "sp")


class _Op:
    __slots__ = ("eng", "fn", "sem", "amt", "needs_inc", "token", "waits", "gi",
                 "kind", "deps")

    def __init__(self, eng, fn, sem, amt, gi, kind):
        self.eng = eng
        self.fn = fn
        self.sem = sem
        self.amt = amt
        self.needs_inc = False
        self.token = None
        self.waits = ()
        self.gi = gi
        self.kind = kind
        self.deps = set()


class PB:
    NDMA = 4

    def __init__(self, nc):
        self.nc = nc
        self.ops = []
        self.kw = {}
        self.kr = {}
        self.pending = {eng: set() for eng in ENGS}
        self.sem_names = set()
        self.dma_idx = {}
        self.dma_last = {}

    def _dep_add(self, deps, op):
        if op is None:
            return
        lane = (op.eng, op.sem)
        cur = deps.get(lane)
        if cur is None or op.gi > cur.gi:
            deps[lane] = op

    def op(self, eng, fn, reads=(), writes=(), sem=None, amt=1, kind="c"):
        sem = sem or (("q" + eng) if kind == "dma" else ("e" + eng))
        o = _Op(eng, fn, sem, amt, len(self.ops), kind)
        deps = {}
        for k in reads:
            self._dep_add(deps, self.kw.get(k))
            if isinstance(k, tuple) and k and k[0] == "PS":
                for r in self.kr.get(k, {}).values():
                    if r.eng != eng:
                        self._dep_add(deps, r)
        for k in writes:
            self._dep_add(deps, self.kw.get(k))
            for r in self.kr.get(k, {}).values():
                self._dep_add(deps, r)
        for p in self.pending[eng]:
            self._dep_add(deps, p)
        self.pending[eng] = set()
        o.deps = set(deps.values())
        for k in reads:
            self.kr.setdefault(k, {})[(o.eng, o.sem)] = o
        for k in writes:
            self.kw[k] = o
            self.kr[k] = {}
        self.ops.append(o)
        return o

    def dma(self, queue, out_ap, in_ap, reads=(), writes=(), slow=False):
        def fn(e, o=out_ap, i=in_ap, s=slow):
            if s:
                return e.dma_start(out=o, in_=i, allow_slow_non_contiguous=True)
            return e.dma_start(out=o, in_=i)
        idx = self.dma_idx.get(queue, 0)
        self.dma_idx[queue] = idx + 1
        rot = idx % self.NDMA
        sem = f"q{queue}{rot}"
        o = self.op(queue, fn, reads=reads, writes=writes, sem=sem, amt=16,
                    kind="dma")
        prev = self.dma_last.get((queue, rot))
        if prev is not None:
            o.deps.add(prev)
        self.dma_last[(queue, rot)] = o
        return o

    def dma_tr(self, queue, out_ap, in_ap, reads=(), writes=()):
        def fn(e, o=out_ap, i=in_ap):
            return e.dma_start_transpose(out=o, in_=i)
        idx = self.dma_idx.get(queue, 0)
        self.dma_idx[queue] = idx + 1
        rot = idx % self.NDMA
        sem = f"q{queue}{rot}"
        o = self.op(queue, fn, reads=reads, writes=writes, sem=sem, amt=16,
                    kind="dma")
        prev = self.dma_last.get((queue, rot))
        if prev is not None:
            o.deps.add(prev)
        self.dma_last[(queue, rot)] = o
        return o

    def cc(self, ccname, kindstr, aluop, groups, in_ap, out_ap, reads, writes):
        def fn(e, ks=kindstr, op=aluop, g=groups, i=in_ap, o=out_ap):
            return e.collective_compute(ks, op, replica_groups=g, ins=[i], outs=[o])
        self.sem_names.add(ccname)
        return self.op("pool", fn, reads=reads, writes=writes, sem=ccname, amt=1,
                       kind="cc")

    def mm(self, bank, fn, reads, start, stop):
        o = _Op("pe", fn, "epe", 1, len(self.ops), "c")
        deps = {}
        for k in reads:
            self._dep_add(deps, self.kw.get(k))
        if start:
            for r in self.kr.get(bank, {}).values():
                if r.eng != "pe":
                    self._dep_add(deps, r)
            w = self.kw.get(bank)
            if w is not None and w.eng != "pe":
                self._dep_add(deps, w)
        for p in self.pending["pe"]:
            self._dep_add(deps, p)
        self.pending["pe"] = set()
        o.deps = set(deps.values())
        for k in reads:
            self.kr.setdefault(k, {})[(o.eng, o.sem)] = o
        if start:
            self.kr[bank] = {}
        if stop:
            self.kw[bank] = o
            self.kr[bank] = {}
        self.ops.append(o)
        return o

    def barrier(self):
        dmas = list(self.dma_last.values())
        bar_ops = []
        for eng in ENGS:
            o = self.op(eng, lambda e: e.drain())
            for p in dmas:
                o.deps.add(p)
            bar_ops.append(o)
        for eng in ENGS:
            self.pending[eng] = set(bar_ops)

    def finalize(self):
        for o in self.ops:
            for dd in o.deps:
                dd.needs_inc = True
        for o in self.ops:
            if o.kind in ("dma", "cc"):
                o.needs_inc = True
        counts = {}
        for o in self.ops:
            if o.needs_inc:
                counts[o.sem] = counts.get(o.sem, 0) + o.amt
                o.token = (o.sem, counts[o.sem])
        marks = {eng: {} for eng in ENGS}
        for o in self.ops:
            w = {}
            m = marks[o.eng]
            for dd in o.deps:
                s, v = dd.token
                if m.get(s, 0) < v and w.get(s, 0) < v:
                    w[s] = v
            for s, v in w.items():
                m[s] = v
            o.waits = sorted(w.items(), key=str)
        self.sem_names.update(counts.keys())
        return counts


def _consts():
    gam = np.array(GAMMAS, dtype=np.float64)
    u = np.arange(RCH)
    tt = np.arange(RCH)
    diff = tt[None, :] - u[:, None]
    dm = np.zeros((RCH, H * RCH), dtype=np.float64)
    for h in range(H):
        dm[:, h * RCH:(h + 1) * RCH] = np.where(
            diff >= 0, gam[h] ** np.maximum(diff, 0), 0.0)
    dm = np.asarray(dm.astype(ml_dtypes.bfloat16))
    gq2 = np.zeros((DH, H * RCH), dtype=np.float64)
    for h in range(H):
        gq2[:, h * RCH:(h + 1) * RCH] = gam[h] ** (tt[None, :] + 1)
    gk = np.zeros((RCH, C), dtype=np.float64)
    for h in range(H):
        gk[:, h * DH:(h + 1) * DH] = (gam[h] ** (RCH - 1 - u))[:, None]
    gk = np.asarray(gk.astype(ml_dtypes.bfloat16))
    gdec = np.zeros((DH, H * DH), dtype=np.float32)
    for h in range(H):
        gdec[:, h * DH:(h + 1) * DH] = gam[h] ** RCH
    selT = np.zeros((2, 128), dtype=np.float32)
    selT[0, :64] = 1.0
    selT[1, 64:] = 1.0
    return dict(
        dmask=dm,
        gq2=np.asarray(gq2.astype(ml_dtypes.bfloat16)),
        gk=gk, gdec=gdec,
        ident_bf=np.asarray(np.eye(128, dtype=ml_dtypes.bfloat16)),
        ident_f32=np.eye(128, dtype=np.float32),
        ones_col_bf=np.asarray(np.ones((128, 1), dtype=ml_dtypes.bfloat16)),
        one_row_bf=np.asarray(np.ones((1, 128), dtype=ml_dtypes.bfloat16)),
        one_row_f32=np.ones((1, 128), dtype=np.float32),
        one_row256=np.asarray(np.ones((1, 256), dtype=ml_dtypes.bfloat16)),
        selT=selT)


def _wl(wT):
    return np.ascontiguousarray(
        wT.reshape(8, 128, 1024).transpose(1, 0, 2).reshape(128, 8192))


def _host_prepare(inputs):
    x = np.asarray(inputs["L0_attn_x"], dtype=np.float32)
    con = _consts()
    shared = {
        "wq_l": _wl(np.asarray(inputs["L0_attn_wq"], np.float32).T),
        "wk_l": _wl(np.asarray(inputs["L0_attn_wk"], np.float32).T),
        "wv_l": _wl(np.asarray(inputs["L0_attn_wv"], np.float32).T),
        "wo_l": _wl(np.asarray(inputs["L0_attn_wo"], np.float32).T),
        "wmod": np.asarray(inputs["L0_wmod"], np.float32),
        "wg_nat": np.asarray(inputs["L0_ffn_w_gate"], np.float32),
        "wu_nat": np.asarray(inputs["L0_ffn_w_up"], np.float32),
        "wd_nat": np.asarray(inputs["L0_ffn_w_down"], np.float32),
        "gnw_v": np.ascontiguousarray(
            np.asarray(inputs["L0_attn_gn_w"], np.float32).reshape(8, 128).T),
        "gnb_v": np.ascontiguousarray(
            np.asarray(inputs["L0_attn_gn_b"], np.float32).reshape(8, 128).T),
        "clipx": np.asarray(inputs["L0_ffn_clip_x"], np.float32).reshape(1, 1),
        "clipd": np.asarray(inputs["L0_ffn_clip_down"], np.float32).reshape(1, 1),
    }
    shared.update(con)
    cmat = np.asarray(inputs["L0_c"], np.float32)
    in_maps = []
    for core in range(8):
        b, half = core // 2, core % 2
        m = dict(shared)
        m["xT"] = np.ascontiguousarray(x[b, half * TC:(half + 1) * TC, :].T)
        m["cvec"] = np.ascontiguousarray(cmat[b].reshape(8, 128).T)
        m["smask_send"] = np.full((64, 1), 1.0 if half == 0 else 0.0, np.float32)
        m["smask_recv"] = np.full((64, 1), 1.0 if half == 1 else 0.0, np.float32)
        in_maps.append(m)
    return in_maps


def build_program(debug_taps=(), phase_limit=99, sub2=9, sub3=3, sub4=9):
    nc = bass.Bass("TRN2", target_bir_lowering=False)

    d = {}
    def din(name, shape, dt=F32):
        d[name] = nc.dram_tensor(name, list(shape), dt, kind="ExternalInput")
        return d[name]

    din("xT", (C, TC))
    for n in ("wq_l", "wk_l", "wv_l", "wo_l"):
        din(n, (128, 8192))
    din("wmod", (C, 6 * C), F32R)
    din("wg_nat", (IF, C)); din("wu_nat", (IF, C)); din("wd_nat", (C, IF))
    din("gnw_v", (128, 8)); din("gnb_v", (128, 8))
    din("clipx", (1, 1)); din("clipd", (1, 1))
    din("cvec", (128, 8))
    din("smask_send", (64, 1)); din("smask_recv", (64, 1))
    din("dmask", (128, H * RCH), BF16); din("gq2", (DH, H * RCH), BF16)
    din("gk", (RCH, C), BF16); din("gdec", (DH, H * DH))
    din("ident_bf", (128, 128), BF16); din("ident_f32", (128, 128))
    din("ones_col_bf", (128, 1), BF16); din("one_row_bf", (1, 128), BF16)
    din("one_row_f32", (1, 128)); din("one_row256", (1, 256), BF16)
    din("selT", (2, 128))

    out_d = nc.dram_tensor("out", [C, TC], F32, kind="ExternalOutput")
    dbg = {}
    for t in debug_taps:
        shp, dt = {"h": ([128, NCC * TC], BF16), "x2": ([128, NCC * TC], BF16),
                   "xq": ([128, NCC * TC], BF16), "s": ([128, NI * TC], BF16),
                   "mod": ([128, 48], F32), "state": ([DH, H * DH], F32),
                   "y": ([NPB, 128, NCC * PBLK], BF16)}[t]
        dbg[t] = nc.dram_tensor("dbg_" + t, shp, dt, kind="ExternalOutput")

    hd = nc.dram_tensor("hd", [NPB, 128, NCC * PBLK], BF16)
    vtmd = nc.dram_tensor("vtmd", [NRC, 128, C], BF16)
    ktmd = nc.dram_tensor("ktmd", [NRC, 128, C], BF16)
    yd = nc.dram_tensor("yd", [NPB, 128, NCC * PBLK], BF16)
    ccs_in = nc.dram_tensor("ccs_in", [DH, H * DH], F32)
    ccs_out = nc.dram_tensor("ccs_out", [DH, H * DH], F32)
    gn_in = nc.dram_tensor("gn_in", [32, 1], F32)
    gn_out = nc.dram_tensor("gn_out", [32, 1], F32)
    mx1_in = nc.dram_tensor("mx1_in", [16, 1], F32); mx1_out = nc.dram_tensor("mx1_out", [16, 1], F32)
    mx2_in = nc.dram_tensor("mx2_in", [16, 1], F32); mx2_out = nc.dram_tensor("mx2_out", [16, 1], F32)

    SB = {}
    def sb(name, shape, dt, off):
        SB[name] = nc.alloc_sbuf_tensor_at(name, list(shape), dt, offset=off)
        return SB[name]

    SBASE = 18 * 1024
    LIM = SBASE + 190 * 1024

    o = SBASE
    x_sb = sb("x", (128, NCC * TC), BF16, o); o += 32768
    idb_sb = sb("idb", (128, 128), BF16, o); o += 256
    idf_sb = sb("idf", (128, 128), F32, o); o += 512
    onec_sb = sb("onec", (128, 1), BF16, o); o += 32
    oner_sb = sb("oner", (1, 128), BF16, o); o += 256
    onerf_sb = sb("onerf", (1, 128), F32, o); o += 512
    selT_sb = sb("selTc", (2, 128), F32, o); o += 512
    cvec_sb = sb("cvecc", (128, 8), F32, o); o += 32
    silc_sb = sb("silc", (128, 8), F32R, o); o += 32
    gnw_sb = sb("gnwc", (128, 8), F32, o); o += 32
    gnb_sb = sb("gnbc", (128, 8), F32, o); o += 32
    mod_sb = sb("modv", (128, 48), F32, o); o += 192
    modt_sb = sb("modt", (128, 48), F32, o); o += 192
    gscr_sb = sb("gscr", (128, 24), F32, o); o += 96
    clipx_sb = sb("clipxs", (1, 1), F32, o); o += 32
    clipd_sb = sb("clipds", (1, 1), F32, o); o += 32
    smask_s_sb = sb("smask_s", (64, 1), F32, o); o += 32
    smask_r_sb = sb("smask_r", (64, 1), F32, o); o += 32
    gsum_sb = sb("gsum", (8, 2), F32, o); o += 32
    gsq_sb = sb("gsq", (8, 2), F32, o); o += 32
    gsum2_sb = sb("gsum2", (2, 8), F32, o); o += 32
    gsq2_sb = sb("gsq2", (2, 8), F32, o); o += 32
    ysum_sb = sb("ysum", (128, 64), F32, o); o += 256
    ysq_sb = sb("ysq", (128, 64), F32, o); o += 256
    ysum8_sb = sb("ysum8", (128, 8), F32, o); o += 32
    ysq8_sb = sb("ysq8", (128, 8), F32, o); o += 32
    Av_sb = sb("Av", (128, 8), F32, o); o += 32
    Bv_sb = sb("Bv", (128, 8), F32, o); o += 32
    mu8_sb = sb("mu8", (128, 8), F32, o); o += 32
    sq8_sb = sb("sq8", (128, 8), F32, o); o += 32
    hmax_sb = sb("hmax64", (128, 64), F32, o); o += 256
    hmax1_sb = sb("hmax1", (128, 1), F32, o); o += 32
    smax_sb = sb("smaxT", (128, NI * NFB), F32, o); o += 512
    smax1_sb = sb("smax1", (128, 1), F32, o); o += 32
    sc16_sb = sb("sc16", (16, 1), F32, o); o += 32
    sc16b_sb = sb("sc16b", (16, 1), F32, o); o += 32
    sA_sb = sb("sA", (1, 1), F32, o); o += 32
    sB_sb = sb("sB", (1, 1), F32, o); o += 32
    rbf_sb = sb("rbf", (1, 1), BF16, o); o += 32
    rB_sb = sb("rB", (128, 1), F32, o); o += 32
    qsxB_sb = sb("qsxB", (128, 1), F32, o); o += 32
    rs2B_sb = sb("rs2B", (128, 1), F32, o); o += 32
    qsdB_sb = sb("qsdB", (128, 1), F32, o); o += 32
    wmx_sb = sb("wmx", (128, 1), F32, o); o += 32
    wri_sb = sb("wri", (128, 1), F32, o); o += 32
    wsc_sb = sb("wsc", (128, 1), F32, o); o += 32
    sgv_sb = sb("sgv", (128, NI), F32, o); o += 128
    suv_sb = sb("suv", (128, NI), F32, o); o += 128
    sqc_sb = sb("sqc", (128, 1), F32, o); o += 32
    epsv_sb = sb("epsv", (128, 1), F32, o); o += 32
    magic_sb = sb("magicc", (128, 2), F32, o); o += 32
    wmxu_sb = sb("wmxu", (128, 1), F32, o); o += 32
    wriu_sb = sb("wriu", (128, 1), F32, o); o += 32
    wscu_sb = sb("wscu", (128, 1), F32, o); o += 32
    onr2_sb = sb("onr2", (1, 256), BF16, o); o += 512
    Bvb_sb = sb("Bvb", (128, 8), BF16, o); o += 32
    R0_END = o
    assert R0_END <= SBASE + 40 * 1024, R0_END

    o = R0_END
    mrow_sb = sb("mrow", (1, TC), BF16, o); o += 4096
    rrow_sb = sb("rrow", (1, TC), BF16, o); o += 4096
    mrr_sb = sb("mrr", (1, TC), BF16, o); o += 4096
    rows_sb = sb("rowst", (1, PBLK * 5), F32, o); o += 5120
    xsq_sb = [sb(f"xsq{i}", (128, PBLK), BF16, o + i * 512) for i in range(2)]; o += 1024
    bcb_sb = sb("bcb", (128, 2 * PBLK), BF16, o); o += 1024
    hsc_sb = [sb(f"hsc{i}", (128, PBLK), BF16, o + i * 512) for i in range(2)]; o += 1024
    wfr_sb = [sb(f"wfr{i}", (128, 512), F32R, o + i * 2048) for i in range(2)]; o += 4096
    xst_sb = sb("xst", (128, TC), F32, o); o += 8192
    wf_sb = [sb(f"wf{i}", (128, 1024), F32, o + i * 4096) for i in range(2)]; o += 8192
    yl_sb = [sb(f"yl{i}", (128, NCC * PBLK), BF16, o + i * 4096) for i in range(2)]; o += 8192
    yb_sb = [sb(f"yb{i}", (128, NCC * PBLK), BF16, o + i * 4096) for i in range(2)]; o += 8192
    WA_OFF = o
    wA_sb = sb("wA", (128, 8192), BF16, o); o += 16384
    wB_sb = sb("wB", (128, 8192), BF16, o); o += 16384
    dm_sb = sb("dm", (128, H * RCH), BF16, o); o += 4096
    gq2_sb = sb("gq2c", (DH, H * RCH), BF16, o); o += 4096
    gk_sb = sb("gkc", (RCH, C), BF16, o); o += 2048
    gdec_sb = sb("gdecc", (DH, H * DH), F32, o); o += 4096
    S_sb = sb("S", (DH, H * DH), F32, o); o += 4096
    Sr_sb = sb("Srecv", (DH, H * DH), F32, o); o += 4096
    Sbf_sb = sb("Sbf", (DH, H * DH), BF16, o); o += 2048
    h_sb = [sb(f"h{i}", (128, NCC * PBLK), BF16, o + i * 4096) for i in range(2)]; o += 8192
    ktm_sb = [sb(f"ktm{i}", (128, C), BF16, o + i * 2048) for i in range(2)]; o += 4096
    vtm_sb = [sb(f"vtm{i}", (128, C), BF16, o + i * 2048) for i in range(2)]; o += 4096
    stm_sb = [sb(f"stm{i}", (128, H * RCH), BF16, o + i * 4096) for i in range(2)]; o += 8192
    modrow_sb = sb("modrow", (1, 512), F32, o); o += 2048
    qt_sb = sb("qt", (DH, H * PBLK), BF16, o); o += 8192
    kt_sb = sb("kt", (DH, H * PBLK), BF16, o); o += 8192
    assert o <= 229375, ("attn", o)

    o = R0_END
    s_sb = sb("s", (128, NI * TC), BF16, o); o += NI * TC * 2
    OUTB_OFF = o
    wig_sb = [sb(f"wig{i}", (128, 1024), BF16, o + i * 2048) for i in range(2)]; o += 4096
    wiu_sb = [sb(f"wiu{i}", (128, 1024), BF16, o + i * 2048) for i in range(2)]; o += 4096
    gsl_sb = [sb(f"gsl{i}", (128, FBLK), BF16, o + i * 1024) for i in range(2)]; o += 2048
    wdn_sb = sb("wdnat", (128, IF // 2), F32, o); o += 5632
    wdq1_sb = sb("wdq1", (128, IF // 2), F32, o); o += 5632
    wdi_sb = sb("wdint", (128, IF // 2), BF16, o); o += 2816
    wnat_sb = wdn_sb
    wq1_sb = wdq1_sb
    wint_sb = wdi_sb
    wdQ_sb = [sb(f"wdq{i}", (128, NI * 256), BF16, o + i * 11264) for i in range(2)]
    o += 22528
    x2t_sb = [sb(f"x2t{i}", (128, 256), BF16, o + i * 512) for i in range(2)]; o += 1024
    nb1_sb = sb("nb1", (128, 1024), F32, R0_END + 16384)
    nb2_sb = sb("nb2", (128, 1024), F32, R0_END + 20480)
    wdmx_sb = sb("wdmx", (128, 1), F32, o); o += 32
    wdri_sb = sb("wdri", (128, 1), F32, o); o += 32
    wdsc_sb = sb("wdsc", (128, 1), F32, o); o += 32
    assert o <= 196576, ("ffn", o)
    outb_sb = sb("outb", (128, 1024), F32, OUTB_OFF)
    h2_sb = sb("h2", (128, NCC * TC), BF16, 196576)
    assert 196576 >= o, ("h2", o)

    PS = [nc.alloc_psum_tensor(f"pb{i}", [128, 512], F32) for i in range(6)]
    PSB = [nc.alloc_psum_tensor(f"pbb{i}", [128, 1024], BF16) for i in range(2)]
    def bank(i):
        return ("PS", i)
    def bankB(i):
        return ("PS", "B", i)

    pb = PB(nc)

    def t_mm(bk, out_ap, lhsT, rhs, reads, start, stop, is_tr=False):
        def fn(e, o2=out_ap, l=lhsT, r=rhs, s=start, t=stop, it=is_tr):
            return e.matmul(o2, l, r, start=s, stop=t, is_transpose=it)
        return pb.mm(bk, fn, reads, start, stop)

    def a_act(out_ap, in_ap, func, reads, writes, scale=1.0, bias=0.0, accum=None):
        def fn(e, o2=out_ap, i=in_ap, f=func, s=scale, b=bias, a=accum):
            if a is not None:
                return e.activation(o2, i, f, bias=b, scale=s, accum_out=a)
            return e.activation(o2, i, f, bias=b, scale=s)
        return pb.op("act", fn, reads=reads, writes=writes)

    def v_tt(eng, out_ap, a_ap, b_ap, aluop, reads, writes):
        def fn(e, o2=out_ap, a=a_ap, b=b_ap, op2=aluop):
            return e.tensor_tensor(o2, a, b, op2)
        return pb.op(eng, fn, reads=reads, writes=writes)

    def v_ts(eng, out_ap, in_ap, s1, s2, op0, op1, reads, writes):
        def fn(e, o2=out_ap, i=in_ap, a=s1, b=s2, p0=op0, p1=op1):
            if b is None:
                return e.tensor_single_scalar(o2, i, a, p0)
            return e.tensor_scalar(o2, i, a, b, p0, p1)
        return pb.op(eng, fn, reads=reads, writes=writes)

    def v_stt(eng, out_ap, in0, scal, in1, op0, op1, reads, writes):
        def fn(e, o2=out_ap, i0=in0, s=scal, i1=in1, p0=op0, p1=op1):
            return e.scalar_tensor_tensor(o2, i0, s, i1, p0, p1)
        return pb.op(eng, fn, reads=reads, writes=writes)

    def v_red(eng, out_ap, in_ap, aluop, reads, writes, absval=False):
        def fn(e, o2=out_ap, i=in_ap, op2=aluop, av=absval):
            return e.tensor_reduce(o2, i, AXX, op2, apply_absolute_value=av)
        return pb.op(eng, fn, reads=reads, writes=writes)

    def v_copy(eng, out_ap, in_ap, reads, writes):
        def fn(e, o2=out_ap, i=in_ap):
            return e.tensor_copy(o2, i)
        return pb.op(eng, fn, reads=reads, writes=writes)

    def v_recip(eng, out_ap, in_ap, reads, writes):
        def fn(e, o2=out_ap, i=in_ap):
            return e.reciprocal(o2, i)
        return pb.op(eng, fn, reads=reads, writes=writes)

    def v_memset(eng, ap, val, writes):
        def fn(e, a=ap, v=val):
            return e.memset(a, v)
        return pb.op(eng, fn, reads=(), writes=writes)

    out_keys = []

    v_memset("dve", epsv_sb[:, :], EPS, writes=[("epsv",)])
    v_memset("dve", magic_sb[:, 0:1], RMAGIC, writes=[("magic",)])
    v_memset("dve", magic_sb[:, 1:2], -RMAGIC, writes=[("magic",)])
    nc.const_aps.register(F32, RMAGIC, magic_sb[:, 0:1])
    nc.const_aps.register(F32, -RMAGIC, magic_sb[:, 1:2])
    for nme, dst in (("dmask", dm_sb), ("gq2", gq2_sb), ("gk", gk_sb),
                     ("gdec", gdec_sb), ("ident_bf", idb_sb), ("ident_f32", idf_sb),
                     ("ones_col_bf", onec_sb), ("one_row_bf", oner_sb),
                     ("one_row_f32", onerf_sb), ("one_row256", onr2_sb),
                     ("selT", selT_sb), ("cvec", cvec_sb),
                     ("gnw_v", gnw_sb), ("gnb_v", gnb_sb),
                     ("clipx", clipx_sb), ("clipd", clipd_sb),
                     ("smask_send", smask_s_sb), ("smask_recv", smask_r_sb)):
        pb.dma("pool", dst[:, :], d[nme][:, :], reads=(), writes=[("c", nme)])

    for ccn in range(NCC):
        pb.dma("sp", xst_sb[:, :], d["xT"][ccn * 128:(ccn + 1) * 128, :],
               reads=[("xst",)], writes=[("xst",)])
        v_copy("dve", x_sb[:, ccn * TC:(ccn + 1) * TC], xst_sb[:, :],
               reads=[("xst",)], writes=[("x", ccn)])

    def load_weight(slot, src_name, slotkey, q="sp"):
        for ccn in range(NCC):
            st = wf_sb[ccn % 2]
            pb.dma(q, st[:, :], d[src_name][:, ccn * 1024:(ccn + 1) * 1024],
                   reads=[("wf", ccn % 2)], writes=[("wf", ccn % 2)])
            v_copy("dve", slot[:, ccn * 1024:(ccn + 1) * 1024], st[:, :],
                   reads=[("wf", ccn % 2)], writes=[(slotkey, ccn)])

    a_act(silc_sb[:, :], cvec_sb[:, :], AF.Silu, reads=[("c", "cvec")],
          writes=[("silc",)])
    for i in range(12):
        for j in range(NCC):
            pb.dma("act", wfr_sb[(2 * i + j) % 2][:, :],
                   d["wmod"][j * 128:(j + 1) * 128, i * 512:(i + 1) * 512],
                   reads=[("wfr", (2 * i + j) % 2)],
                   writes=[("wfr", (2 * i + j) % 2)])
            t_mm(bank(5), PS[5][:1, 0:512],
                 silc_sb[:, j:j + 1],
                 wfr_sb[(2 * i + j) % 2][:, :],
                 reads=[("wfr", (2 * i + j) % 2), ("silc",)],
                 start=(j == 0), stop=(j == 7))
        v_copy("dve", modrow_sb[:1, 0:512], PS[5][:1, 0:512],
               reads=[bank(5), ("modrow",)], writes=[("modrow",)])
        for k in range(4):
            t_mm(bank(2), PS[2][:, i * 4 + k: i * 4 + k + 1],
                 modrow_sb[:1, k * 128:(k + 1) * 128], idf_sb[0:1, 0:1],
                 reads=[("modrow",), ("c", "ident_f32")],
                 start=(i == 0 and k == 0), stop=(i == 11 and k == 3), is_tr=True)
    v_copy("dve", mod_sb[:, :], PS[2][:, 0:48], reads=[bank(2)], writes=[("modv",)])
    v_copy("dve", modt_sb[:, 0:8], mod_sb[:, 0:8], reads=[("modv",)], writes=[("mt", 0)])
    v_ts("dve", modt_sb[:, 8:16], mod_sb[:, 8:16], 1.0, None, ALU.add, None,
         reads=[("modv",)], writes=[("mt", 1)])
    v_copy("dve", modt_sb[:, 24:32], mod_sb[:, 24:32], reads=[("modv",)], writes=[("mt", 3)])
    v_ts("dve", modt_sb[:, 32:40], mod_sb[:, 32:40], 1.0, None, ALU.add, None,
         reads=[("modv",)], writes=[("mt", 4)])
    for (c0, mi) in ((16, 2), (40, 5)):
        a_act(gscr_sb[:, 0:8], mod_sb[:, c0:c0 + 8], AF.Sign,
              reads=[("modv",), ("gscr",)], writes=[("gscr",)])
        v_tt("dve", gscr_sb[:, 8:16], gscr_sb[:, 0:8], gscr_sb[:, 0:8], ALU.mult,
             reads=[("gscr",)], writes=[("gscr",)])
        v_ts("dve", gscr_sb[:, 8:16], gscr_sb[:, 8:16], -1.0, 1.0, ALU.mult, ALU.add,
             reads=[("gscr",)], writes=[("gscr",)])
        v_tt("dve", gscr_sb[:, 8:16], gscr_sb[:, 0:8], gscr_sb[:, 8:16], ALU.add,
             reads=[("gscr",)], writes=[("gscr",)])
        a_act(gscr_sb[:, 16:24], mod_sb[:, c0:c0 + 8], AF.Abs,
              reads=[("modv",), ("gscr",)], writes=[("gscr",)])
        v_ts("dve", gscr_sb[:, 16:24], gscr_sb[:, 16:24], 0.25, None, ALU.max, None,
             reads=[("gscr",)], writes=[("gscr",)])
        v_tt("dve", modt_sb[:, c0:c0 + 8], gscr_sb[:, 8:16], gscr_sb[:, 16:24],
             ALU.mult, reads=[("gscr",)], writes=[("mt", mi)])

    load_weight(wA_sb, "wk_l", "wA")
    load_weight(wB_sb, "wv_l", "wB")

    sa_c = modt_sb[:, 0:8]; a1_c = modt_sb[:, 8:16]; ga_c = modt_sb[:, 16:24]
    sb_c = modt_sb[:, 24:32]; a2_c = modt_sb[:, 32:40]; gb_c = modt_sb[:, 40:48]
    MT = [("mt", i) for i in range(6)]

    if "mod" in dbg:
        pb.dma("pool", dbg["mod"][:, :], modt_sb[:, :], reads=MT, writes=[("dbgmod",)])
        out_keys.append(("dbgmod",))

    def ln_stats(src_keyf):
        for blk in range(NPB):
            ln_stats_block(blk, src_keyf)

    def ln_stats_block(blk, src_keyf):
        if True:
            t0 = blk * PBLK
            for ccn in range(NCC):
                xs = x_sb[:, ccn * TC + t0: ccn * TC + t0 + PBLK]
                t_mm(bank(4), PS[4][:1, 0:PBLK], onec_sb[:, :], xs,
                     reads=[("c", "ones_col_bf"), src_keyf(ccn, blk)],
                     start=(ccn == 0), stop=(ccn == 7))
            for ccn in range(NCC):
                xs = x_sb[:, ccn * TC + t0: ccn * TC + t0 + PBLK]
                sqt = xsq_sb[ccn % 2]
                a_act(sqt[:, :], xs, AF.Square,
                      reads=[src_keyf(ccn, blk), ("xsq", ccn % 2)],
                      writes=[("xsq", ccn % 2)])
                t_mm(bank(5), PS[5][:1, 0:PBLK], onec_sb[:, :], sqt[:, :],
                     reads=[("c", "ones_col_bf"), ("xsq", ccn % 2)],
                     start=(ccn == 0), stop=(ccn == 7))
            rw = rows_sb
            P = PBLK
            v_ts("dve", rw[:1, 0:P], PS[4][:1, 0:P], 1.0 / C, None, ALU.mult, None,
                 reads=[bank(4), ("rows",)], writes=[("rows",)])
            v_ts("dve", rw[:1, P:2 * P], PS[5][:1, 0:P], 1.0 / C, None, ALU.mult, None,
                 reads=[bank(5), ("rows",)], writes=[("rows",)])
            v_tt("dve", rw[:1, 2 * P:3 * P], rw[:1, 0:P], rw[:1, 0:P], ALU.mult,
                 reads=[("rows",)], writes=[("rows",)])
            v_tt("dve", rw[:1, 2 * P:3 * P], rw[:1, P:2 * P], rw[:1, 2 * P:3 * P],
                 ALU.subtract, reads=[("rows",)], writes=[("rows",)])
            a_act(rw[:1, 3 * P:4 * P], rw[:1, 2 * P:3 * P], AF.Sqrt,
                  reads=[("rows",), ("epsv",)], writes=[("rows",)],
                  bias=epsv_sb[0:1, 0:1])
            v_recip("dve", rw[:1, 4 * P:5 * P], rw[:1, 3 * P:4 * P],
                    reads=[("rows",)], writes=[("rows",)])
            v_copy("dve", mrow_sb[:1, t0:t0 + P], rw[:1, 0:P],
                   reads=[("rows",)], writes=[("mrow", blk)])
            v_copy("dve", rrow_sb[:1, t0:t0 + P], rw[:1, 4 * P:5 * P],
                   reads=[("rows",)], writes=[("rrow", blk)])
            v_tt("dve", mrr_sb[:1, t0:t0 + P], mrow_sb[:1, t0:t0 + P],
                 rrow_sb[:1, t0:t0 + P], ALU.mult,
                 reads=[("mrow", blk), ("rrow", blk), ("mrr", blk)],
                 writes=[("mrr", blk)])

    def bcast_rows(blk):
        t0 = blk * PBLK
        t_mm(bank(4), PS[4][:, 0:PBLK], oner_sb[:, :], rrow_sb[:1, t0:t0 + PBLK],
             reads=[("c", "one_row_bf"), ("rrow", blk)], start=True, stop=False)
        t_mm(bank(4), PS[4][:, PBLK:2 * PBLK], oner_sb[:, :], mrr_sb[:1, t0:t0 + PBLK],
             reads=[("c", "one_row_bf"), ("mrr", blk)], start=False, stop=True)
        a_act(bcb_sb[:, :], PS[4][:, 0:2 * PBLK], AF.Copy,
              reads=[bank(4), ("bcb",)], writes=[("bcb",)])

    def h_block(blk, srckeyf, dstf, dkeyf, a_vec, s_vec, mkeys):
        t0 = blk * PBLK
        bcast_rows(blk)
        for ccn in range(NCC):
            xs = x_sb[:, ccn * TC + t0: ccn * TC + t0 + PBLK]
            h1 = hsc_sb[ccn % 2]
            v_tt("dve", h1[:, :], xs, bcb_sb[:, 0:PBLK], ALU.mult,
                 reads=[srckeyf(ccn, blk), ("bcb",), ("hsc", ccn % 2)],
                 writes=[("hsc", ccn % 2)])
            v_tt("dve", h1[:, :], h1[:, :], bcb_sb[:, PBLK:2 * PBLK], ALU.subtract,
                 reads=[("bcb",), ("hsc", ccn % 2)],
                 writes=[("hsc", ccn % 2)])
            v_ts("dve", dstf(ccn, blk), h1[:, :], a_vec[:, ccn:ccn + 1],
                 s_vec[:, ccn:ccn + 1], ALU.mult, ALU.add,
                 reads=[("hsc", ccn % 2)] + mkeys + [dkeyf(ccn, blk)],
                 writes=[dkeyf(ccn, blk)])

    NPK = 14
    PSLOT_COL = [18, 19, 20, 21, 15, 16, 17, 8, 9, 10, 11, 12, 13, 14]

    def pslot(i, m):
        base = PSLOT_COL[i] * TC + m * 1024
        return s_sb[:, base:base + 1024]

    def quant_gu(nat_name, ich, dstslot, dkey, svec, sveckey, mulq,
                 natb, natk, mx, mxk, ri, rik, sc, sck, q, rb0, qtr):
        pb.dma(q, natb[:, 0:1024], d[nat_name][ich * 128:(ich + 1) * 128, :],
               reads=[natk], writes=[natk])
        v_red("dve", mx[:, :], natb[:, 0:1024], ALU.max,
              reads=[natk, mxk], writes=[mxk], absval=True)
        v_recip("dve", ri[:, :], mx[:, :], reads=[mxk, rik],
                writes=[rik])
        v_ts("dve", ri[:, :], ri[:, :], QMAX, None, ALU.mult, None,
             reads=[rik], writes=[rik])
        if mulq:
            v_ts("dve", sc[:, :], mx[:, :], 1.0 / QMAX, None, ALU.mult, None,
                 reads=[mxk, sck], writes=[sck])
            v_tt("dve", svec[:, ich:ich + 1], sc[:, :], qsxB_sb[:, 0:1], ALU.mult,
                 reads=[sck, ("qsxB",)], writes=[(sveckey, ich)])
        else:
            v_ts("dve", svec[:, ich:ich + 1], mx[:, :], 1.0 / QMAX, None,
                 ALU.mult, None, reads=[mxk], writes=[(sveckey, ich)])
        for c2 in range(2):
            bq = rb0 + c2
            a_act(PS[bq][:, 0:512], natb[:, c2 * 512:(c2 + 1) * 512], AF.Copy,
                  reads=[natk, rik, ("magic",), bank(bq)], writes=[bank(bq)],
                  scale=ri[:, 0:1], bias=RMAGIC)
            a_act(wint_sb[:, c2 * 512:(c2 + 1) * 512], PS[bq][:, 0:512], AF.Copy,
                  reads=[bank(bq), ("magic",), ("wint",)], writes=[("wint",)],
                  bias=-RMAGIC)
        pb.dma_tr(qtr, dstslot[:, :].rearrange("p (j c) -> p j c", j=NCC),
                  wint_sb[:, 0:1024].rearrange("p (j c) -> p j c", j=NCC),
                  reads=[("wint",)], writes=[dkey + (j,) for j in range(NCC)])

    def quant_pk(i, mq=False, rb0=2, dbuf=False):
        ng, nk = ((nb1_sb, ("nb1",)) if dbuf and i % 2 else (wdn_sb, ("wnat",)))
        nu, uk = ((nb2_sb, ("nb2",)) if dbuf and i % 2 else (wdq1_sb, ("wq1",)))
        quant_gu("wg_nat", i, pslot(i, 0), ("pk", i, 0), sgv_sb, "sg", mq,
                 ng, nk, wmx_sb, ("wmx",), wri_sb, ("wri",),
                 wsc_sb, ("wsc",), "sp", rb0, "sp")
        quant_gu("wu_nat", i, pslot(i, 1), ("pk", i, 1), suv_sb, "su", mq,
                 nu, uk, wmxu_sb, ("wmxu",), wriu_sb, ("wriu",),
                 wscu_sb, ("wscu",), "pool", rb0, "sp")

    def prep_wd(ccn):
        q = ccn // 2
        pb.dma("sp", wdn_sb[:, :],
               d["wd_nat"][ccn * 128:(ccn + 1) * 128, 0:IF // 2],
               reads=[("wnat",)], writes=[("wnat",)])
        pb.dma("act", wdq1_sb[:, :],
               d["wd_nat"][ccn * 128:(ccn + 1) * 128, IF // 2:IF],
               reads=[("wq1",)], writes=[("wq1",)])
        v_red("dve", wdmx_sb[:, :], wdn_sb[:, :], ALU.max,
              reads=[("wnat",), ("wdmx",)], writes=[("wdmx",)], absval=True)
        v_red("dve", wdsc_sb[:, :], wdq1_sb[:, :], ALU.max,
              reads=[("wq1",), ("wdsc",)], writes=[("wdsc",)], absval=True)
        v_tt("dve", wdmx_sb[:, :], wdmx_sb[:, :], wdsc_sb[:, :], ALU.max,
             reads=[("wdmx",), ("wdsc",)], writes=[("wdmx",)])
        v_recip("dve", wdri_sb[:, :], wdmx_sb[:, :], reads=[("wdmx",), ("wdri",)],
                writes=[("wdri",)])
        v_ts("dve", wdri_sb[:, :], wdri_sb[:, :], QMAX, None, ALU.mult, None,
             reads=[("wdri",)], writes=[("wdri",)])
        v_ts("dve", wdsc_sb[:, :], wdmx_sb[:, :], 1.0 / QMAX, None, ALU.mult,
             None, reads=[("wdmx",), ("wdsc",)], writes=[("wdsc",)])
        v_tt("dve", wdsc_sb[:, :], wdsc_sb[:, :], gb_c[:, ccn:ccn + 1], ALU.mult,
             reads=[("wdsc",), ("mt", 5)], writes=[("wdsc",)])
        for hf in range(2):
            natb, natk = ((wdn_sb, ("wnat",)) if hf == 0
                          else (wdq1_sb, ("wq1",)))
            for c2 in range(3):
                w = 512 if c2 < 2 else IF // 2 - 1024
                bq = 4 + (c2 % 2)
                a_act(PS[bq][:, 0:w], natb[:, c2 * 512:c2 * 512 + w], AF.Copy,
                      reads=[natk, ("wdri",), ("magic",), bank(bq)],
                      writes=[bank(bq)], scale=wdri_sb[:, 0:1], bias=RMAGIC)
                a_act(wdi_sb[:, c2 * 512:c2 * 512 + w], PS[bq][:, 0:w], AF.Copy,
                      reads=[bank(bq), ("magic",), ("wint",)],
                      writes=[("wint",)], bias=-RMAGIC)
            v_ts("dve", wdi_sb[:, :], wdi_sb[:, :], wdsc_sb[:, 0:1], None,
                 ALU.mult, None, reads=[("wint",), ("wdsc",)],
                 writes=[("wint",)])
            nj = NI // 2
            c0 = (ccn % 2) * 128
            pb.dma_tr("sp",
                      wdQ_sb[q % 2][:, :].rearrange(
                          "p (j k) -> p j k", k=256)[:, hf * nj:(hf + 1) * nj,
                                                     c0:c0 + 128],
                      wdi_sb[:, :].rearrange("p (j c) -> p j c", j=nj),
                      reads=[("wint",)],
                      writes=[("wdQ", q % 2, hf * nj + jj, ccn % 2)
                              for jj in range(nj)])


    def xkey(ccn, blk):
        return ("x", ccn)
    ln_stats(xkey)

    def hsl(ccn, blk):
        return h_sb[blk % 2][:, ccn * PBLK:(ccn + 1) * PBLK]

    def hkey(ccn, blk):
        return ("h", blk % 2, ccn)

    v_memset("dve", S_sb[:, :], 0.0, writes=[("S",)])

    def spill_h(b2):
        pb.dma("sp", hd[b2, :, :], h_sb[b2 % 2][:, :],
               reads=[("h", b2 % 2, ccn) for ccn in range(NCC)],
               writes=[("hd", b2)])

    h_block(0, xkey, hsl, hkey, a1_c, sa_c, [("mt", 1), ("mt", 0)])
    spill_h(0)
    for blk in range(NPB):
        for tci in range(2):
            ch = blk * 2 + tci
            kbuf = ktm_sb[ch % 2]
            vbuf = vtm_sb[ch % 2]
            for half in range(2):
                bk = bank(half)
                for ccn in range(NCC):
                    t_mm(bk, PS[half][:, 0:512],
                         hsl(ccn, blk)[:, tci * 128:(tci + 1) * 128],
                         wA_sb[:, ccn * 1024 + half * 512: ccn * 1024 + (half + 1) * 512],
                         reads=[hkey(ccn, blk), ("wA", ccn)],
                         start=(ccn == 0), stop=(ccn == 7))
                v_tt("dve", kbuf[:, half * 512:(half + 1) * 512], PS[half][:, 0:512],
                     gk_sb[:, half * 512:(half + 1) * 512], ALU.mult,
                     reads=[bk, ("c", "gk"), ("ktm", ch % 2)],
                     writes=[("ktm", ch % 2)])
                bkv = bank(2 + half)
                for ccn in range(NCC):
                    t_mm(bkv, PS[2 + half][:, 0:512],
                         hsl(ccn, blk)[:, tci * 128:(tci + 1) * 128],
                         wB_sb[:, ccn * 1024 + half * 512: ccn * 1024 + (half + 1) * 512],
                         reads=[hkey(ccn, blk), ("wB", ccn)],
                         start=(ccn == 0), stop=(ccn == 7))
                a_act(vbuf[:, half * 512:(half + 1) * 512], PS[2 + half][:, 0:512],
                      AF.Copy, reads=[bkv, ("vtm", ch % 2)], writes=[("vtm", ch % 2)])
            pb.dma("act", ktmd[ch, :, :], kbuf[:, :],
                   reads=[("ktm", ch % 2)], writes=[("ktmd", ch)])
            pb.dma("act", vtmd[ch, :, :], vbuf[:, :],
                   reads=[("vtm", ch % 2)], writes=[("vtmd", ch)])
            for h in range(H):
                hb = 4 if h < 8 else 5
                t_mm(bank(hb), PS[hb][0:DH, (h % 8) * DH:(h % 8 + 1) * DH],
                     kbuf[:, h * DH:(h + 1) * DH], vbuf[:, h * DH:(h + 1) * DH],
                     reads=[("ktm", ch % 2), ("vtm", ch % 2)],
                     start=(h % 8 == 0), stop=(h % 8 == 7))
            v_tt("dve", Sr_sb[:, :], S_sb[:, :], gdec_sb[:, :], ALU.mult,
                 reads=[("S",), ("c", "gdec"), ("Srecv",)], writes=[("Srecv",)])
            v_tt("dve", S_sb[:, 0:512], Sr_sb[:, 0:512], PS[4][0:DH, 0:512], ALU.add,
                 reads=[("Srecv",), bank(4), ("S",)], writes=[("S",)])
            v_tt("dve", S_sb[:, 512:1024], Sr_sb[:, 512:1024], PS[5][0:DH, 0:512],
                 ALU.add, reads=[("Srecv",), bank(5), ("S",)], writes=[("S",)])
            if tci == 0 and blk + 1 < NPB:
                h_block(blk + 1, xkey, hsl, hkey, a1_c, sa_c, [("mt", 1), ("mt", 0)])
                spill_h(blk + 1)

    v_ts("dve", Sr_sb[:, :], S_sb[:, :], smask_s_sb[:, 0:1], None, ALU.mult, None,
         reads=[("S",), ("c", "smask_send"), ("Srecv",)], writes=[("Srecv",)])
    pb.dma("pool", ccs_in.ap(), Sr_sb[:, :], reads=[("Srecv",)], writes=[("ccsin",)])
    pb.cc("ccstate", "AllReduce", ALU.add, [[0, 1], [2, 3], [4, 5], [6, 7]],
          ccs_in.ap().opt(), ccs_out.ap().opt(),
          reads=[("ccsin",)], writes=[("ccsout",)])
    pb.dma("pool", Sr_sb[:, :], ccs_out.ap(), reads=[("ccsout",), ("Srecv",)],
           writes=[("Srecv",)])
    v_ts("dve", S_sb[:, :], Sr_sb[:, :], smask_r_sb[:, 0:1], None, ALU.mult, None,
         reads=[("Srecv",), ("c", "smask_recv"), ("S",)], writes=[("S",)])
    v_copy("dve", Sbf_sb[:, :], S_sb[:, :], reads=[("S",), ("Sbf",)],
           writes=[("Sbf",)])

    if "state" in dbg:
        pb.dma("pool", dbg["state"][:, :], S_sb[:, :], reads=[("S",)],
               writes=[("dbgS",)])
        out_keys.append(("dbgS",))

    if phase_limit >= 2:
        def fetch_kv(ch):
            pb.dma("act", ktm_sb[ch % 2][:, :], ktmd[ch, :, :],
                   reads=[("ktmd", ch), ("ktm", ch % 2)], writes=[("ktm", ch % 2)])
            pb.dma("act", vtm_sb[ch % 2][:, :], vtmd[ch, :, :],
                   reads=[("vtmd", ch), ("vtm", ch % 2)], writes=[("vtm", ch % 2)])

        def fetch_h(b2):
            pb.dma("sp", h_sb[b2 % 2][:, :], hd[b2, :, :],
                   reads=[("hd", b2)] + [("h", b2 % 2, ccn) for ccn in range(NCC)],
                   writes=[("h", b2 % 2, ccn) for ccn in range(NCC)])

        def proj(blk):
            for (slot, skey, dst, dkey) in ((wB_sb, "wB", qt_sb, "qt"),
                                            (wA_sb, "wA", kt_sb, "kt")):
                for ecn in range(NCC):
                    bk = bank(ecn % 2)
                    for ccn in range(NCC):
                        t_mm(bk, PS[ecn % 2][:, 0:PBLK],
                             slot[:, ccn * 1024 + ecn * 128: ccn * 1024 + (ecn + 1) * 128],
                             hsl(ccn, blk),
                             reads=[(skey, ccn), hkey(ccn, blk)],
                             start=(ccn == 0), stop=(ccn == 7))
                    for sub in range(2):
                        hh = 2 * ecn + sub
                        dks = ([("qt", hh, 0), ("qt", hh, 1)] if dkey == "qt"
                               else [("kt", hh)])
                        a_act(dst[:DH, hh * PBLK:(hh + 1) * PBLK],
                              PS[ecn % 2][sub * 64:(sub + 1) * 64, 0:PBLK], AF.Copy,
                              reads=[bk] + dks, writes=dks)

        def scores(ch):
            blk, tci = ch // 2, ch % 2
            smbuf = stm_sb[ch % 2]
            for g in range(4):
                bks = bank(2 + g % 2)
                for hi in range(4):
                    hh = 4 * g + hi
                    sl = slice(hh * PBLK + tci * 128, hh * PBLK + tci * 128 + 128)
                    t_mm(bks, PS[2 + g % 2][:, hi * 128:(hi + 1) * 128],
                         kt_sb[:DH, sl], qt_sb[:DH, sl],
                         reads=[("kt", hh), ("qt", hh, tci)],
                         start=(hi == 0), stop=(hi == 3))
                v_tt("dve", smbuf[:, g * 512:(g + 1) * 512],
                     PS[2 + g % 2][:, 0:512], dm_sb[:, g * 512:(g + 1) * 512],
                     ALU.mult,
                     reads=[bks, ("c", "dmask"), ("stm", ch % 2)],
                     writes=[("stm", ch % 2)])

        def rest(ch):
            blk, tci = ch // 2, ch % 2
            kbuf = ktm_sb[ch % 2]
            vbuf = vtm_sb[ch % 2]
            smbuf = stm_sb[ch % 2]
            if sub2 >= 4:
                for h in range(H):
                    hb = 0 if h < 8 else 1
                    t_mm(bank(hb), PS[hb][0:DH, (h % 8) * DH:(h % 8 + 1) * DH],
                         kbuf[:, h * DH:(h + 1) * DH], vbuf[:, h * DH:(h + 1) * DH],
                         reads=[("ktm", ch % 2), ("vtm", ch % 2)],
                         start=(h % 8 == 0), stop=(h % 8 == 7))
            qv = qt_sb[:DH, :].rearrange("p (h t) -> p h t", h=H)[:, :, tci * 128:(tci + 1) * 128]
            gv = gq2_sb[:DH, :].rearrange("p (h t) -> p h t", h=H)
            v_tt("dve", qv, qv, gv, ALU.mult,
                 reads=[("c", "gq2")] + [("qt", hh, tci) for hh in range(H)],
                 writes=[("qt", hh, tci) for hh in range(H)])
            for g in range(4 if sub2 >= 3 else 0):
                bky = bank(4 + g % 2)
                for hi in range(4):
                    hh = 4 * g + hi
                    t_mm(bky, PS[4 + g % 2][0:DH, hi * 128:(hi + 1) * 128],
                         vbuf[:, hh * DH:(hh + 1) * DH],
                         smbuf[:, (g * 4 + hi) * 128:(g * 4 + hi + 1) * 128],
                         reads=[("vtm", ch % 2), ("stm", ch % 2)],
                         start=(hi == 0), stop=(False if sub3 >= 2 else hi == 3))
                if sub3 >= 2:
                    for hi in range(4):
                        hh = 4 * g + hi
                        sl = slice(hh * PBLK + tci * 128, hh * PBLK + tci * 128 + 128)
                        t_mm(bky, PS[4 + g % 2][0:DH, hi * 128:(hi + 1) * 128],
                             Sbf_sb[:DH, hh * DH:(hh + 1) * DH], qt_sb[:DH, sl],
                             reads=[("Sbf",), ("qt", hh, tci)],
                             start=False, stop=(hi == 3))
                for hi in range(4):
                    hh = 4 * g + hi
                    ccn = hh // 2
                    r0 = (hh % 2) * 64 if sub3 >= 3 else 0
                    a_act(yb_sb[blk % 2][r0:r0 + 64,
                                         ccn * PBLK + tci * 128: ccn * PBLK + tci * 128 + 128],
                          PS[4 + g % 2][0:DH, hi * 128:(hi + 1) * 128], AF.Copy,
                          reads=[bky, ("yb", blk % 2)], writes=[("yb", blk % 2)])
            if sub2 >= 4:
                v_tt("dve", Sr_sb[:, :], S_sb[:, :], gdec_sb[:, :], ALU.mult,
                     reads=[("S",), ("c", "gdec"), ("Srecv",)], writes=[("Srecv",)])
                v_tt("dve", S_sb[:, 0:512], Sr_sb[:, 0:512], PS[0][0:DH, 0:512],
                     ALU.add, reads=[("Srecv",), bank(0), ("S",)], writes=[("S",)])
                v_tt("dve", S_sb[:, 512:1024], Sr_sb[:, 512:1024], PS[1][0:DH, 0:512],
                     ALU.add, reads=[("Srecv",), bank(1), ("S",)], writes=[("S",)])
                v_copy("dve", Sbf_sb[:, :], S_sb[:, :], reads=[("S",), ("Sbf",)],
                       writes=[("Sbf",)])
            if ch + 2 < NRC:
                fetch_kv(ch + 2)

        def gn_stats_y(blk):
            for ccn in range(NCC if sub2 >= 5 else 0):
                ys = yb_sb[blk % 2][:, ccn * PBLK:(ccn + 1) * PBLK]
                col = blk * 8 + ccn
                a_act(hsc_sb[0][:, :], ys, AF.Copy,
                      reads=[("yb", blk % 2), ("hsc", 0)],
                      writes=[("hsc", 0), ("ysum", col)],
                      accum=ysum_sb[:, col:col + 1])
                a_act(hsc_sb[1][:, :], ys, AF.Square,
                      reads=[("yb", blk % 2), ("hsc", 1)],
                      writes=[("hsc", 1), ("ysq", col)],
                      accum=ysq_sb[:, col:col + 1])
            if sub2 >= 3:
                pb.dma("act", yd[blk, :, :], yb_sb[blk % 2][:, :],
                       reads=[("yb", blk % 2)], writes=[("yd", blk)])
                if blk < 2:
                    pb.dma("act", yl_sb[blk][:, :], yd[blk, :, :],
                           reads=[("yd", blk), ("yl", blk)], writes=[("yl", blk)])

        fetch_h(0)
        fetch_h(1)
        load_weight(wB_sb, "wq_l", "wB")
        fetch_kv(0)
        fetch_kv(1)
        for blk in range(NPB):
            proj(blk)
            if blk == NPB - 1:
                load_weight(wA_sb, "wo_l", "wA")
            if blk > 0:
                gn_stats_y(blk - 1)
            if blk + 2 < NPB:
                fetch_h(blk + 2)
            if sub2 >= 2:
                scores(2 * blk)
                scores(2 * blk + 1)
                rest(2 * blk)
                rest(2 * blk + 1)
        gn_stats_y(NPB - 1)

        if sub2 < 6:
            pb.dma("sp", out_d[0:128, 0:1024], xst_sb[:, 0:1024],
                   reads=[("xst",)], writes=[("outz",)])
            pb.op("sp", lambda e: e.nop(), reads=[("outz",)], writes=())
            pb.finalize()
            _emit(nc, pb)
            return nc
        ysr = ysum_sb[:, :].rearrange("p (a b) -> p b a", a=NPB)
        v_red("dve", ysum8_sb[:, :], ysr, ALU.add,
              reads=[("ysum", c2) for c2 in range(64)], writes=[("ysum8",)])
        ysr2 = ysq_sb[:, :].rearrange("p (a b) -> p b a", a=NPB)
        v_red("dve", ysq8_sb[:, :], ysr2, ALU.add,
              reads=[("ysq", c2) for c2 in range(64)], writes=[("ysq8",)])
        t_mm(bank(2), PS[2][0:8, 0:128], ysum8_sb[:, :], idf_sb[:, :],
             reads=[("ysum8",), ("c", "ident_f32")], start=True, stop=True, is_tr=True)
        t_mm(bank(3), PS[3][0:8, 0:128], ysq8_sb[:, :], idf_sb[:, :],
             reads=[("ysq8",), ("c", "ident_f32")], start=True, stop=True, is_tr=True)
        v_red("dve", gsum_sb[:, :],
              PS[2][0:8, 0:128].rearrange("p (h e) -> p h e", h=2), ALU.add,
              reads=[bank(2)], writes=[("gsum",)])
        v_red("dve", gsq_sb[:, :],
              PS[3][0:8, 0:128].rearrange("p (h e) -> p h e", h=2), ALU.add,
              reads=[bank(3)], writes=[("gsq",)])
        pb.dma("pool", gn_in[0:16, :], gsum_sb[:, :], reads=[("gsum",)],
               writes=[("gnin",)], slow=True)
        pb.dma("pool", gn_in[16:32, :], gsq_sb[:, :], reads=[("gsq",), ("gnin",)],
               writes=[("gnin",)], slow=True)
        pb.cc("ccgn", "AllReduce", ALU.add, [[0, 1], [2, 3], [4, 5], [6, 7]],
              gn_in.ap().opt(), gn_out.ap().opt(),
              reads=[("gnin",)], writes=[("gnout",)])
        pb.dma("pool", gsum2_sb[:, :],
               gn_out[0:16, :].rearrange("(a b) c -> b (a c)", b=2),
               reads=[("gnout",)], writes=[("gsum2",)], slow=True)
        pb.dma("pool", gsq2_sb[:, :],
               gn_out[16:32, :].rearrange("(a b) c -> b (a c)", b=2),
               reads=[("gnout",)], writes=[("gsq2",)], slow=True)
        for ccn in range(NCC):
            t_mm(bank(2), PS[2][:, ccn:ccn + 1], selT_sb[:, :], gsum2_sb[:, ccn:ccn + 1],
                 reads=[("c", "selT"), ("gsum2",)], start=(ccn == 0), stop=(ccn == 7))
            t_mm(bank(3), PS[3][:, ccn:ccn + 1], selT_sb[:, :], gsq2_sb[:, ccn:ccn + 1],
                 reads=[("c", "selT"), ("gsq2",)], start=(ccn == 0), stop=(ccn == 7))
        NTOT = float(T * DH)
        v_ts("dve", mu8_sb[:, :], PS[2][:, 0:8], 1.0 / NTOT, None, ALU.mult, None,
             reads=[bank(2)], writes=[("mu8",)])
        v_ts("dve", sq8_sb[:, :], PS[3][:, 0:8], 1.0 / NTOT, None, ALU.mult, None,
             reads=[bank(3)], writes=[("sq8",)])
        v_tt("dve", Av_sb[:, :], mu8_sb[:, :], mu8_sb[:, :], ALU.mult,
             reads=[("mu8",), ("Av",)], writes=[("Av",)])
        v_tt("dve", Av_sb[:, :], sq8_sb[:, :], Av_sb[:, :], ALU.subtract,
             reads=[("sq8",), ("Av",)], writes=[("Av",)])
        a_act(Bv_sb[:, :], Av_sb[:, :], AF.Sqrt, reads=[("Av",), ("Bv",), ("epsv",)],
              writes=[("Bv",)], bias=epsv_sb[:, 0:1])
        v_recip("dve", Av_sb[:, :], Bv_sb[:, :], reads=[("Bv",), ("Av",)],
                writes=[("Av",)])
        v_tt("dve", Av_sb[:, :], Av_sb[:, :], gnw_sb[:, :], ALU.mult,
             reads=[("Av",), ("c", "gnw_v")], writes=[("Av",)])
        v_tt("dve", Bv_sb[:, :], mu8_sb[:, :], Av_sb[:, :], ALU.mult,
             reads=[("mu8",), ("Av",), ("Bv",)], writes=[("Bv",)])
        v_tt("dve", Bv_sb[:, :], gnb_sb[:, :], Bv_sb[:, :], ALU.subtract,
             reads=[("c", "gnb_v"), ("Bv",)], writes=[("Bv",)])

        if sub2 < 7:
            pb.dma("sp", out_d[0:128, 0:1024], xst_sb[:, 0:1024],
                   reads=[("xst",)], writes=[("outz",)])
            pb.op("sp", lambda e: e.nop(), reads=[("outz",)], writes=())
            pb.finalize()
            _emit(nc, pb)
            return nc
        pb.barrier()
        v_copy("dve", Bvb_sb[:, :], Bv_sb[:, :], reads=[("Bv",)], writes=[("Bvb",)])
        for half in range(2):
            bkb = bank(2 + half)
            for ccn in range(NCC):
                t_mm(bkb, PS[2 + half][:1, 0:512], Bvb_sb[:, ccn:ccn + 1],
                     wA_sb[:, ccn * 1024 + half * 512: ccn * 1024 + (half + 1) * 512],
                     reads=[("Bvb",), ("wA", ccn)],
                     start=(ccn == 0), stop=(ccn == 7))
            a_act(dm_sb[0:1, half * 512:(half + 1) * 512], PS[2 + half][:1, 0:512],
                  AF.Copy, reads=[bkb, ("biasr",)], writes=[("biasr",)])
        for ccn in range(NCC):
            v_ts("dve", wA_sb[:, ccn * 1024:(ccn + 1) * 1024],
                 wA_sb[:, ccn * 1024:(ccn + 1) * 1024], Av_sb[:, ccn:ccn + 1],
                 None, ALU.mult, None,
                 reads=[("wA", ccn), ("Av",)], writes=[("wA", ccn)])

        def _stats_h2(blk):
            ln_stats_block(blk, lambda c2, b2: ("x2", c2, b2))
            if phase_limit >= 3:
                h_block(blk, lambda c2, b2: ("x2", c2, b2),
                        lambda c2, b2: h2_sb[:, c2 * TC + b2 * PBLK: c2 * TC + (b2 + 1) * PBLK],
                        lambda c2, b2: ("h2", c2, b2), a2_c, sb_c,
                        [("mt", 4), ("mt", 3)])
                for ccn in range(NCC):
                    col = blk * 8 + ccn
                    v_red("dve", hmax_sb[:, col:col + 1],
                          h2_sb[:, ccn * TC + blk * PBLK: ccn * TC + (blk + 1) * PBLK],
                          ALU.max, reads=[("h2", ccn, blk)],
                          writes=[("hmaxc", col)], absval=True)

        for blk in range(NPB):
            ybuf = yl_sb[blk % 2]
            for con in range(NCC):
                bk = bank(con % 2)
                for ccn in range(NCC):
                    t_mm(bk, PS[con % 2][:, 0:PBLK],
                         wA_sb[:, ccn * 1024 + con * 128: ccn * 1024 + (con + 1) * 128],
                         ybuf[:, ccn * PBLK:(ccn + 1) * PBLK],
                         reads=[("wA", ccn), ("yl", blk % 2)],
                         start=(ccn == 0), stop=False)
                t_mm(bk, PS[con % 2][:, 0:PBLK],
                     dm_sb[0:1, con * 128:(con + 1) * 128], onr2_sb[0:1, :],
                     reads=[("biasr",), ("c", "one_row256")],
                     start=False, stop=True)
                xs = x_sb[:, con * TC + blk * PBLK: con * TC + (blk + 1) * PBLK]
                v_stt("dve", xs, PS[con % 2][:, 0:PBLK], ga_c[:, con:con + 1], xs,
                      ALU.mult, ALU.add,
                      reads=[bk, ("mt", 2), ("x", con)],
                      writes=[("x", con), ("x2", con, blk)])
            if blk + 2 < NPB:
                pb.dma("act", yl_sb[blk % 2][:, :], yd[blk + 2, :, :],
                       reads=[("yd", blk + 2), ("yl", blk % 2)],
                       writes=[("yl", blk % 2)])
            if 1 <= blk <= 4:
                quant_pk(blk - 1)
            if blk > 0:
                _stats_h2(blk - 1)
        _stats_h2(NPB - 1)

        if "x2" in dbg:
            pb.dma("pool", dbg["x2"][:, :], x_sb[:, :],
                   reads=[("x2", ccn, blk) for ccn in range(NCC) for blk in range(NPB)],
                   writes=[("dbgx2",)])
            out_keys.append(("dbgx2",))
        if "y" in dbg:
            def fnd2d(e, o2=dbg["y"].ap(), i=yd.ap()):
                return e.dma_start(out=o2, in_=i)
            pb.op("pool", fnd2d, reads=[("yd", blk) for blk in range(NPB)],
                  writes=[("dbgy",)], amt=16, kind="dma")
            out_keys.append(("dbgy",))

    if phase_limit >= 3:
        pb.barrier()

        def x2key(ccn, blk):
            return ("x2", ccn, blk)

        def h2sl(ccn, blk):
            return h2_sb[:, ccn * TC + blk * PBLK: ccn * TC + (blk + 1) * PBLK]

        if sub4 < 1:
            pb.dma("sp", out_d[0:128, 0:1024], xst_sb[:, 0:1024],
                   reads=[("xst",)], writes=[("outz2",)])
            pb.op("sp", lambda e: e.nop(), reads=[("outz2",)], writes=())
            pb.finalize()
            _emit(nc, pb)
            return nc
        if sub4 < 2:
            pb.dma("sp", out_d[0:128, 0:1024], xst_sb[:, 0:1024],
                   reads=[("xst",)], writes=[("outz2",)])
            pb.op("sp", lambda e: e.nop(), reads=[("outz2",)], writes=())
            pb.finalize()
            _emit(nc, pb)
            return nc
        v_red("dve", hmax1_sb[:, :], hmax_sb[:, :], ALU.max,
              reads=[("hmaxc", c2) for c2 in range(64)], writes=[("hmax1",)])
        if sub4 < 3:
            pb.dma("sp", out_d[0:128, 0:1024], xst_sb[:, 0:1024],
                   reads=[("xst",)], writes=[("outz2",)])
            pb.op("sp", lambda e: e.nop(), reads=[("outz2",)], writes=())
            pb.finalize()
            _emit(nc, pb)
            return nc
        t_mm(bank(4), PS[4][0:1, 0:128], hmax1_sb[:, 0:1], idf_sb[:, :],
             reads=[("hmax1",), ("c", "ident_f32")], start=True, stop=True, is_tr=True)
        v_red("dve", sA_sb[:, :], PS[4][0:1, 0:128], ALU.max,
              reads=[bank(4)], writes=[("sA",)])
        if sub4 < 4:
            pb.dma("sp", out_d[0:128, 0:1024], xst_sb[:, 0:1024],
                   reads=[("xst",)], writes=[("outz2",)])
            pb.op("sp", lambda e: e.nop(), reads=[("outz2",)], writes=())
            pb.finalize()
            _emit(nc, pb)
            return nc
        v_memset("dve", sc16_sb[:, :], 0.0, writes=[("sc16",)])
        v_copy("dve", sc16_sb[0:1, :], sA_sb[:, :], reads=[("sA",), ("sc16",)],
               writes=[("sc16",)])
        pb.dma("pool", mx1_in.ap(), sc16_sb[:, :], reads=[("sc16",)],
               writes=[("mx1in",)])
        pb.cc("ccmax1", "AllReduce", ALU.max, [list(range(8))],
              mx1_in.ap().opt(), mx1_out.ap().opt(),
              reads=[("mx1in",)], writes=[("mx1out",)])
        pb.dma("pool", sc16b_sb[:, :], mx1_out.ap(), reads=[("mx1out",), ("sc16b",)],
               writes=[("sc16b",)])
        a_act(sA_sb[:, :], clipx_sb[:, :], AF.Sigmoid,
              reads=[("c", "clipx"), ("sA",)], writes=[("sA",)])
        v_tt("dve", sA_sb[:, :], sA_sb[:, :], sc16b_sb[0:1, 0:1], ALU.mult,
             reads=[("sA",), ("sc16b",)], writes=[("sA",)])
        v_ts("dve", sA_sb[:, :], sA_sb[:, :], 1.0 / QMAX, None, ALU.mult, None,
             reads=[("sA",)], writes=[("sA",)])
        v_recip("dve", sB_sb[:, :], sA_sb[:, :], reads=[("sA",), ("sB",)],
                writes=[("sB",)])
        def bcast_scalar(src_key, src_ap, dst_ap, dst_key, bcol):
            t_mm(bank(5), PS[5][:, bcol:bcol + 1], onerf_sb[:, :], src_ap,
                 reads=[("c", "one_row_f32"), src_key], start=True, stop=True)
            v_copy("dve", dst_ap, PS[5][:, bcol:bcol + 1], reads=[bank(5), dst_key],
                   writes=[dst_key])

        if sub4 < 5:
            pb.dma("sp", out_d[0:128, 0:1024], xst_sb[:, 0:1024],
                   reads=[("xst",)], writes=[("outz2",)])
            pb.op("sp", lambda e: e.nop(), reads=[("outz2",)], writes=())
            pb.finalize()
            _emit(nc, pb)
            return nc
        bcast_scalar(("sB",), sB_sb[:, :], rB_sb[:, :], ("rB",), 0)
        bcast_scalar(("sA",), sA_sb[:, :], qsxB_sb[:, :], ("qsxB",), 1)
        if sub4 < 6:
            pb.dma("sp", out_d[0:128, 0:1024], xst_sb[:, 0:1024],
                   reads=[("xst",)], writes=[("outz2",)])
            pb.op("sp", lambda e: e.nop(), reads=[("outz2",)], writes=())
            pb.finalize()
            _emit(nc, pb)
            return nc
        for blk in range(NPB):
            for ccn in range(NCC):
                hs = h2sl(ccn, blk)
                if ccn % 2 == 0:
                    bq = 4 + (blk % 2)
                    a_act(PS[bq][:, 0:PBLK], hs, AF.Copy,
                          reads=[("h2", ccn, blk), ("rB",), ("magic",), bank(bq)],
                          writes=[bank(bq)], scale=rB_sb[:, 0:1], bias=RMAGIC)
                    a_act(hs, PS[bq][:, 0:PBLK], AF.Copy,
                          reads=[bank(bq), ("magic",), ("h2", ccn, blk)],
                          writes=[("h2", ccn, blk)], bias=-RMAGIC)
                    v_ts("dve", hs, hs, QMAX, -QMAX - 1.0, ALU.min, ALU.max,
                         reads=[("h2", ccn, blk)], writes=[("h2", ccn, blk)])
                else:
                    bq = 2 + (blk % 2)
                    v_ts("dve", PS[bq][:, 0:PBLK], hs, rB_sb[:, 0:1], RMAGIC,
                         ALU.mult, ALU.add,
                         reads=[("h2", ccn, blk), ("rB",), bank(bq)],
                         writes=[bank(bq)])
                    v_ts("dve", hs, PS[bq][:, 0:PBLK], RMAGIC, QMAX,
                         ALU.subtract, ALU.min,
                         reads=[bank(bq), ("h2", ccn, blk)],
                         writes=[("h2", ccn, blk)])
                    v_ts("dve", hs, hs, -QMAX - 1.0, None, ALU.max, None,
                         reads=[("h2", ccn, blk)], writes=[("h2", ccn, blk)])

        for i in range(4, NPK):
            quant_pk(i, mq=True, rb0=4, dbuf=True)

        if "xq" in dbg:
            pb.dma("pool", dbg["xq"][:, :], h2_sb[:, :],
                   reads=[("h2", ccn, blk) for ccn in range(NCC) for blk in range(NPB)],
                   writes=[("dbgxq",)])
            out_keys.append(("dbgxq",))

    if phase_limit >= 4:

        v_ts("dve", sgv_sb[:, 0:4], sgv_sb[:, 0:4], qsxB_sb[:, 0:1], None,
             ALU.mult, None, reads=[("qsxB",)] + [("sg", i) for i in range(4)],
             writes=[("sg", i) for i in range(4)])
        v_ts("dve", suv_sb[:, 0:4], suv_sb[:, 0:4], qsxB_sb[:, 0:1], None,
             ALU.mult, None, reads=[("qsxB",)] + [("su", i) for i in range(4)],
             writes=[("su", i) for i in range(4)])
        def quant_jit(ich):
            quant_gu("wg_nat", ich, wig_sb[ich % 2], ("wig", ich % 2),
                     sgv_sb, "sg", True,
                     wdn_sb, ("wnat",), wmx_sb, ("wmx",), wri_sb, ("wri",),
                     wsc_sb, ("wsc",), "sp", 4, "sp")
            quant_gu("wu_nat", ich, wiu_sb[ich % 2], ("wiu", ich % 2),
                     suv_sb, "su", True,
                     wdq1_sb, ("wq1",), wmxu_sb, ("wmxu",), wriu_sb, ("wriu",),
                     wscu_sb, ("wscu",), "act", 4, "sp")

        quant_jit(NPK)
        for ich in range(NI):
            if ich < NPK:
                wgt, gkey = pslot(ich, 0), ("pk", ich, 0)
                wut, ukey = pslot(ich, 1), ("pk", ich, 1)
            else:
                wgt, gkey = wig_sb[ich % 2], ("wig", ich % 2)
                wut, ukey = wiu_sb[ich % 2], ("wiu", ich % 2)
            if NPK <= ich + 1 < NI:
                quant_jit(ich + 1)
            if ich in (1, 3, 5, 7):
                prep_wd((ich - 1) // 2)
            for blk in range(NFB):
                bkg = bank((2 * blk) % 4)
                bku = bank((2 * blk + 1) % 4)
                for ccn in range(NCC):
                    xqs = h2_sb[:, ccn * TC + blk * FBLK: ccn * TC + (blk + 1) * FBLK]
                    t_mm(bkg, PS[(2 * blk) % 4][:, 0:FBLK],
                         wgt[:, ccn * 128:(ccn + 1) * 128], xqs,
                         reads=[gkey + (ccn,), ("h2", ccn, 2 * blk),
                                ("h2", ccn, 2 * blk + 1)],
                         start=(ccn == 0), stop=(ccn == 7))
                for ccn in range(NCC):
                    xqs = h2_sb[:, ccn * TC + blk * FBLK: ccn * TC + (blk + 1) * FBLK]
                    t_mm(bku, PS[(2 * blk + 1) % 4][:, 0:FBLK],
                         wut[:, ccn * 128:(ccn + 1) * 128], xqs,
                         reads=[ukey + (ccn,), ("h2", ccn, 2 * blk),
                                ("h2", ccn, 2 * blk + 1)],
                         start=(ccn == 0), stop=(ccn == 7))
                gsl = gsl_sb[blk % 2]
                a_act(gsl[:, :], PS[(2 * blk) % 4][:, 0:FBLK], AF.Silu,
                      reads=[bkg, ("sg", ich), ("gsl", blk % 2)],
                      writes=[("gsl", blk % 2)], scale=sgv_sb[:, ich:ich + 1])
                ss = s_sb[:, ich * TC + blk * FBLK: ich * TC + (blk + 1) * FBLK]
                v_tt("dve", ss, PS[(2 * blk + 1) % 4][:, 0:FBLK], gsl[:, :], ALU.mult,
                     reads=[bku, ("gsl", blk % 2)], writes=[("s", ich, blk)])
            v_red("dve", hmax1_sb[:, :], s_sb[:, ich * TC:(ich + 1) * TC], ALU.max,
                  reads=[("s", ich, b2) for b2 in range(NFB)] + [("hmax1",)],
                  writes=[("hmax1",)], absval=True)
            v_tt("dve", smax_sb[:, ich:ich + 1], hmax1_sb[:, :],
                 suv_sb[:, ich:ich + 1], ALU.mult,
                 reads=[("hmax1",), ("su", ich)], writes=[("smc", ich)])

        v_red("dve", smax1_sb[:, 0:1], smax_sb[:, 0:NI], ALU.max,
              reads=[("smc", c2) for c2 in range(NI)], writes=[("smax1",)])
        t_mm(bank(4), PS[4][0:1, 0:128], smax1_sb[:, 0:1], idf_sb[:, :],
             reads=[("smax1",), ("c", "ident_f32")], start=True, stop=True, is_tr=True)
        v_red("dve", sA_sb[:, :], PS[4][0:1, 0:128], ALU.max,
              reads=[bank(4), ("sA",)], writes=[("sA",)])
        v_copy("dve", sc16_sb[0:1, :], sA_sb[:, :], reads=[("sA",), ("sc16",)],
               writes=[("sc16",)])
        pb.dma("pool", mx2_in.ap(), sc16_sb[:, :], reads=[("sc16",)],
               writes=[("mx2in",)])
        pb.cc("ccmax2", "AllReduce", ALU.max, [list(range(8))],
              mx2_in.ap().opt(), mx2_out.ap().opt(),
              reads=[("mx2in",)], writes=[("mx2out",)])
        pb.dma("pool", sc16b_sb[:, :], mx2_out.ap(), reads=[("mx2out",), ("sc16b",)],
               writes=[("sc16b",)])
        a_act(sA_sb[:, :], clipd_sb[:, :], AF.Sigmoid,
              reads=[("c", "clipd"), ("sA",)], writes=[("sA",)])
        v_tt("dve", sA_sb[:, :], sA_sb[:, :], sc16b_sb[0:1, 0:1], ALU.mult,
             reads=[("sA",), ("sc16b",)], writes=[("sA",)])
        v_ts("dve", sA_sb[:, :], sA_sb[:, :], 1.0 / QMAX, None, ALU.mult, None,
             reads=[("sA",)], writes=[("sA",)])
        v_recip("dve", sB_sb[:, :], sA_sb[:, :], reads=[("sA",), ("sB",)],
                writes=[("sB",)])
        bcast_scalar(("sB",), sB_sb[:, :], rs2B_sb[:, :], ("rs2B",), 2)
        bcast_scalar(("sA",), sA_sb[:, :], qsdB_sb[:, :], ("qsdB",), 3)

        def quant_s_half(hf2, ichs=None):
            for ich in (range(NI) if ichs is None else ichs):
                v_tt("dve", sqc_sb[:, :], suv_sb[:, ich:ich + 1], rs2B_sb[:, 0:1],
                     ALU.mult, reads=[("su", ich), ("rs2B",), ("sqc",)],
                     writes=[("sqc",)])
                srd = [("s", ich, 2 * hf2), ("s", ich, 2 * hf2 + 1)]
                for c2 in range(2):
                    ss = s_sb[:, ich * TC + hf2 * 1024 + c2 * 512:
                              ich * TC + hf2 * 1024 + (c2 + 1) * 512]
                    bq = 4 + c2
                    a_act(PS[bq][:, 0:512], ss, AF.Copy,
                          reads=srd + [("sqc",), ("magic",), bank(bq)],
                          writes=[bank(bq)], scale=sqc_sb[:, 0:1], bias=RMAGIC)
                    a_act(ss, PS[bq][:, 0:512], AF.Copy,
                          reads=[bank(bq), ("magic",)] + srd,
                          writes=srd, bias=-RMAGIC)
                ssf = s_sb[:, ich * TC + hf2 * 1024: ich * TC + (hf2 + 1) * 1024]
                v_ts("dve", ssf, ssf, QMAX, -QMAX - 1.0, ALU.min, ALU.max,
                     reads=srd, writes=srd)

        quant_s_half(0)

        if "s" in dbg:
            quant_s_half(1)
            pb.dma("pool", dbg["s"][:, :], s_sb[:, :],
                   reads=[("s", ich, b2) for ich in range(NI) for b2 in range(NFB)],
                   writes=[("dbgs",)])
            out_keys.append(("dbgs",))

    if phase_limit >= 5:
        def down_pass(p):
            for tb in range(NFB):
                if p == 0 and tb < 2 and "s" not in dbg:
                    quant_s_half(1, range(11 * tb, 11 * (tb + 1)))
                if p in (1, 2) and tb == 1:
                    prep_wd(2 * (p + 1))
                if p in (1, 2) and tb == 3:
                    prep_wd(2 * (p + 1) + 1)
                for k in range(2):
                    ccn = 2 * p + k
                    g = 2 * tb + k
                    bk = bank(g % 4)
                    psd = PS[g % 4]
                    for ich in range(NI):
                        t_mm(bk, psd[:, 0:FBLK],
                             wdQ_sb[p % 2][:, ich * 256 + k * 128:
                                           ich * 256 + k * 128 + 128],
                             s_sb[:, ich * TC + tb * FBLK: ich * TC + (tb + 1) * FBLK],
                             reads=[("s", ich, tb), ("wdQ", p % 2, ich, k)],
                             start=(ich == 0), stop=(ich == NI - 1))
                    ob = outb_sb[:, (g % 2) * FBLK:(g % 2) * FBLK + FBLK]
                    v_stt("dve", ob, psd[:, 0:FBLK], qsdB_sb[:, 0:1],
                          x_sb[:, ccn * TC + tb * FBLK: ccn * TC + (tb + 1) * FBLK],
                          ALU.mult, ALU.add,
                          reads=[bk, ("qsdB",), ("x2", ccn, 2 * tb),
                                 ("x2", ccn, 2 * tb + 1), ("outb", g % 2)],
                          writes=[("outb", g % 2)])
                    pb.dma("sp",
                           out_d[ccn * 128:(ccn + 1) * 128, tb * FBLK:(tb + 1) * FBLK],
                           ob, reads=[("outb", g % 2)], writes=[("outd", p, g)])
                    out_keys.append(("outd", p, g))

        for p in range(4):
            down_pass(p)
    else:
        pb.dma("sp", out_d[0:128, 0:1024], xst_sb[:, 0:1024],
               reads=[("xst",)], writes=[("outz",)])
        out_keys.append(("outz",))

    pb.op("sp", lambda e: e.nop(), reads=list(out_keys), writes=())

    pb.finalize()
    _emit(nc, pb)
    return nc


def _emit(nc, pb):
    import contextlib
    stack = contextlib.ExitStack()
    sems = {}
    for s in sorted(pb.sem_names):
        sems[s] = stack.enter_context(nc.semaphore(s))
    by_eng = {e: [] for e in ENGS}
    for o2 in pb.ops:
        by_eng[o2.eng].append(o2)

    def emit_stream(eng_obj, ops):
        for o2 in ops:
            for s, v in o2.waits:
                eng_obj.wait_ge(sems[s], v)
            inst = o2.fn(eng_obj)
            if o2.token is not None and inst is not None:
                inst.then_inc(sems[o2.token[0]], o2.amt)

    with nc.Block() as blk:
        @blk.tensor
        def _(e):
            emit_stream(e, by_eng["pe"])

        @blk.scalar
        def _(e):
            emit_stream(e, by_eng["act"])

        @blk.vector
        def _(e):
            emit_stream(e, by_eng["dve"])

        @blk.gpsimd
        def _(e):
            emit_stream(e, by_eng["pool"])

        @blk.sync
        def _(e):
            emit_stream(e, by_eng["sp"])
    stack.close()


_CACHED = {}

def kernel(**inputs):
    in_maps = _host_prepare(inputs)
    key = (tuple(DEBUG_TAPS), PHASE_LIMIT, SUB2, SUB3, SUB4)
    if key not in _CACHED:
        _CACHED[key] = build_program(DEBUG_TAPS, PHASE_LIMIT, SUB2, SUB3, SUB4)
    nc = _CACHED[key]
    res = run_bass_kernel_spmd(nc, in_maps, core_ids=list(range(8)))
    out = np.zeros((B, T, C), dtype=np.float32)
    for core in range(8):
        b, half = core // 2, core % 2
        out[b, half * TC:(half + 1) * TC, :] = res.results[core]["out"].T
    kernel.last_results = res
    return out


# version 13
# speedup vs baseline: 1.0285x; 1.0138x over previous
import math
import numpy as np
import ml_dtypes

import concourse.bass as bass
import concourse.mybir as mybir
from concourse.bass_utils import run_bass_kernel_spmd

F32 = mybir.dt.float32
F32R = mybir.dt.float32r
BF16 = mybir.dt.bfloat16
AF = mybir.ActivationFunctionType
ALU = mybir.AluOpType
AXX = mybir.AxisListType.X

B, T, C, H, DH = 4, 4096, 1024, 16, 64
TC = 2048
NCC = 8
PBLK = 256
NPB = TC // PBLK
RCH = 128
NRC = TC // RCH
FBLK = 512
NFB = TC // FBLK
IF = 2816
NI = IF // 128
QMAX = 127.0
EPS = 1e-5
RMAGIC = 1.5 * 2.0 ** 23

_glo = math.log(1.0 - 0.999)
_ghi = math.log(1.0 - 0.9)
GAMMAS = [1.0 - math.exp(_glo + (_ghi - _glo) * h / (H - 1)) for h in range(H)]

DEBUG_TAPS = ()
PHASE_LIMIT = 99
SUB2 = 9
SUB3 = 3
SUB4 = 9

ENGS = ("pe", "act", "dve", "pool", "sp")


class _Op:
    __slots__ = ("eng", "fn", "sem", "amt", "needs_inc", "token", "waits", "gi",
                 "kind", "deps")

    def __init__(self, eng, fn, sem, amt, gi, kind):
        self.eng = eng
        self.fn = fn
        self.sem = sem
        self.amt = amt
        self.needs_inc = False
        self.token = None
        self.waits = ()
        self.gi = gi
        self.kind = kind
        self.deps = set()


class PB:
    NDMA = 4

    def __init__(self, nc):
        self.nc = nc
        self.ops = []
        self.kw = {}
        self.kr = {}
        self.pending = {eng: set() for eng in ENGS}
        self.sem_names = set()
        self.dma_idx = {}
        self.dma_last = {}

    def _dep_add(self, deps, op):
        if op is None:
            return
        lane = (op.eng, op.sem)
        cur = deps.get(lane)
        if cur is None or op.gi > cur.gi:
            deps[lane] = op

    def op(self, eng, fn, reads=(), writes=(), sem=None, amt=1, kind="c"):
        sem = sem or (("q" + eng) if kind == "dma" else ("e" + eng))
        o = _Op(eng, fn, sem, amt, len(self.ops), kind)
        deps = {}
        for k in reads:
            self._dep_add(deps, self.kw.get(k))
            if isinstance(k, tuple) and k and k[0] == "PS":
                for r in self.kr.get(k, {}).values():
                    if r.eng != eng:
                        self._dep_add(deps, r)
        for k in writes:
            self._dep_add(deps, self.kw.get(k))
            for r in self.kr.get(k, {}).values():
                self._dep_add(deps, r)
        for p in self.pending[eng]:
            self._dep_add(deps, p)
        self.pending[eng] = set()
        o.deps = set(deps.values())
        for k in reads:
            self.kr.setdefault(k, {})[(o.eng, o.sem)] = o
        for k in writes:
            self.kw[k] = o
            self.kr[k] = {}
        self.ops.append(o)
        return o

    def dma(self, queue, out_ap, in_ap, reads=(), writes=(), slow=False):
        def fn(e, o=out_ap, i=in_ap, s=slow):
            if s:
                return e.dma_start(out=o, in_=i, allow_slow_non_contiguous=True)
            return e.dma_start(out=o, in_=i)
        idx = self.dma_idx.get(queue, 0)
        self.dma_idx[queue] = idx + 1
        rot = idx % self.NDMA
        sem = f"q{queue}{rot}"
        o = self.op(queue, fn, reads=reads, writes=writes, sem=sem, amt=16,
                    kind="dma")
        prev = self.dma_last.get((queue, rot))
        if prev is not None:
            o.deps.add(prev)
        self.dma_last[(queue, rot)] = o
        return o

    def dma_tr(self, queue, out_ap, in_ap, reads=(), writes=()):
        def fn(e, o=out_ap, i=in_ap):
            return e.dma_start_transpose(out=o, in_=i)
        idx = self.dma_idx.get(queue, 0)
        self.dma_idx[queue] = idx + 1
        rot = idx % self.NDMA
        sem = f"q{queue}{rot}"
        o = self.op(queue, fn, reads=reads, writes=writes, sem=sem, amt=16,
                    kind="dma")
        prev = self.dma_last.get((queue, rot))
        if prev is not None:
            o.deps.add(prev)
        self.dma_last[(queue, rot)] = o
        return o

    def cc(self, ccname, kindstr, aluop, groups, in_ap, out_ap, reads, writes):
        def fn(e, ks=kindstr, op=aluop, g=groups, i=in_ap, o=out_ap):
            return e.collective_compute(ks, op, replica_groups=g, ins=[i], outs=[o])
        self.sem_names.add(ccname)
        return self.op("pool", fn, reads=reads, writes=writes, sem=ccname, amt=1,
                       kind="cc")

    def mm(self, bank, fn, reads, start, stop):
        o = _Op("pe", fn, "epe", 1, len(self.ops), "c")
        deps = {}
        for k in reads:
            self._dep_add(deps, self.kw.get(k))
        if start:
            for r in self.kr.get(bank, {}).values():
                if r.eng != "pe":
                    self._dep_add(deps, r)
            w = self.kw.get(bank)
            if w is not None and w.eng != "pe":
                self._dep_add(deps, w)
        for p in self.pending["pe"]:
            self._dep_add(deps, p)
        self.pending["pe"] = set()
        o.deps = set(deps.values())
        for k in reads:
            self.kr.setdefault(k, {})[(o.eng, o.sem)] = o
        if start:
            self.kr[bank] = {}
        if stop:
            self.kw[bank] = o
            self.kr[bank] = {}
        self.ops.append(o)
        return o

    def barrier(self):
        dmas = list(self.dma_last.values())
        bar_ops = []
        for eng in ENGS:
            o = self.op(eng, lambda e: e.drain())
            for p in dmas:
                o.deps.add(p)
            bar_ops.append(o)
        for eng in ENGS:
            self.pending[eng] = set(bar_ops)

    def finalize(self):
        for o in self.ops:
            for dd in o.deps:
                dd.needs_inc = True
        for o in self.ops:
            if o.kind in ("dma", "cc"):
                o.needs_inc = True
        counts = {}
        for o in self.ops:
            if o.needs_inc:
                counts[o.sem] = counts.get(o.sem, 0) + o.amt
                o.token = (o.sem, counts[o.sem])
        marks = {eng: {} for eng in ENGS}
        for o in self.ops:
            w = {}
            m = marks[o.eng]
            for dd in o.deps:
                s, v = dd.token
                if m.get(s, 0) < v and w.get(s, 0) < v:
                    w[s] = v
            for s, v in w.items():
                m[s] = v
            o.waits = sorted(w.items(), key=str)
        self.sem_names.update(counts.keys())
        return counts


def _consts():
    gam = np.array(GAMMAS, dtype=np.float64)
    u = np.arange(RCH)
    tt = np.arange(RCH)
    diff = tt[None, :] - u[:, None]
    dm = np.zeros((RCH, H * RCH), dtype=np.float64)
    for h in range(H):
        dm[:, h * RCH:(h + 1) * RCH] = np.where(
            diff >= 0, gam[h] ** np.maximum(diff, 0), 0.0)
    dm = np.asarray(dm.astype(ml_dtypes.bfloat16))
    gq2 = np.zeros((DH, H * RCH), dtype=np.float64)
    for h in range(H):
        gq2[:, h * RCH:(h + 1) * RCH] = gam[h] ** (tt[None, :] + 1)
    gk = np.zeros((RCH, C), dtype=np.float64)
    for h in range(H):
        gk[:, h * DH:(h + 1) * DH] = (gam[h] ** (RCH - 1 - u))[:, None]
    gk = np.asarray(gk.astype(ml_dtypes.bfloat16))
    gdec = np.zeros((DH, H * DH), dtype=np.float32)
    for h in range(H):
        gdec[:, h * DH:(h + 1) * DH] = gam[h] ** RCH
    selT = np.zeros((2, 128), dtype=np.float32)
    selT[0, :64] = 1.0
    selT[1, 64:] = 1.0
    return dict(
        dmask=dm,
        gq2=np.asarray(gq2.astype(ml_dtypes.bfloat16)),
        gk=gk, gdec=gdec,
        ident_bf=np.asarray(np.eye(128, dtype=ml_dtypes.bfloat16)),
        ident_f32=np.eye(128, dtype=np.float32),
        ones_col_bf=np.asarray(np.ones((128, 1), dtype=ml_dtypes.bfloat16)),
        one_row_bf=np.asarray(np.ones((1, 128), dtype=ml_dtypes.bfloat16)),
        one_row_f32=np.ones((1, 128), dtype=np.float32),
        one_row256=np.asarray(np.ones((1, 256), dtype=ml_dtypes.bfloat16)),
        selT=selT)


def _wl(wT):
    return np.ascontiguousarray(
        wT.reshape(8, 128, 1024).transpose(1, 0, 2).reshape(128, 8192))


def _host_prepare(inputs):
    x = np.asarray(inputs["L0_attn_x"], dtype=np.float32)
    con = _consts()
    shared = {
        "wq_l": _wl(np.asarray(inputs["L0_attn_wq"], np.float32).T),
        "wk_l": _wl(np.asarray(inputs["L0_attn_wk"], np.float32).T),
        "wv_l": _wl(np.asarray(inputs["L0_attn_wv"], np.float32).T),
        "wo_l": _wl(np.asarray(inputs["L0_attn_wo"], np.float32).T),
        "wmod": np.asarray(inputs["L0_wmod"], np.float32),
        "wg_nat": np.asarray(inputs["L0_ffn_w_gate"], np.float32),
        "wu_nat": np.asarray(inputs["L0_ffn_w_up"], np.float32),
        "wd_nat": np.asarray(inputs["L0_ffn_w_down"], np.float32),
        "gnw_v": np.ascontiguousarray(
            np.asarray(inputs["L0_attn_gn_w"], np.float32).reshape(8, 128).T),
        "gnb_v": np.ascontiguousarray(
            np.asarray(inputs["L0_attn_gn_b"], np.float32).reshape(8, 128).T),
        "clipx": np.asarray(inputs["L0_ffn_clip_x"], np.float32).reshape(1, 1),
        "clipd": np.asarray(inputs["L0_ffn_clip_down"], np.float32).reshape(1, 1),
    }
    shared.update(con)
    cmat = np.asarray(inputs["L0_c"], np.float32)
    in_maps = []
    for core in range(8):
        b, half = core // 2, core % 2
        m = dict(shared)
        m["xT"] = np.ascontiguousarray(x[b, half * TC:(half + 1) * TC, :].T)
        m["cvec"] = np.ascontiguousarray(cmat[b].reshape(8, 128).T)
        m["smask_send"] = np.full((64, 1), 1.0 if half == 0 else 0.0, np.float32)
        m["smask_recv"] = np.full((64, 1), 1.0 if half == 1 else 0.0, np.float32)
        in_maps.append(m)
    return in_maps


def build_program(debug_taps=(), phase_limit=99, sub2=9, sub3=3, sub4=9):
    nc = bass.Bass("TRN2", target_bir_lowering=False)

    d = {}
    def din(name, shape, dt=F32):
        d[name] = nc.dram_tensor(name, list(shape), dt, kind="ExternalInput")
        return d[name]

    din("xT", (C, TC))
    for n in ("wq_l", "wk_l", "wv_l", "wo_l"):
        din(n, (128, 8192))
    din("wmod", (C, 6 * C), F32R)
    din("wg_nat", (IF, C)); din("wu_nat", (IF, C)); din("wd_nat", (C, IF))
    din("gnw_v", (128, 8)); din("gnb_v", (128, 8))
    din("clipx", (1, 1)); din("clipd", (1, 1))
    din("cvec", (128, 8))
    din("smask_send", (64, 1)); din("smask_recv", (64, 1))
    din("dmask", (128, H * RCH), BF16); din("gq2", (DH, H * RCH), BF16)
    din("gk", (RCH, C), BF16); din("gdec", (DH, H * DH))
    din("ident_bf", (128, 128), BF16); din("ident_f32", (128, 128))
    din("ones_col_bf", (128, 1), BF16); din("one_row_bf", (1, 128), BF16)
    din("one_row_f32", (1, 128)); din("one_row256", (1, 256), BF16)
    din("selT", (2, 128))

    out_d = nc.dram_tensor("out", [C, TC], F32, kind="ExternalOutput")
    dbg = {}
    for t in debug_taps:
        shp, dt = {"h": ([128, NCC * TC], BF16), "x2": ([128, NCC * TC], BF16),
                   "xq": ([128, NCC * TC], BF16), "s": ([128, NI * TC], BF16),
                   "mod": ([128, 48], F32), "state": ([DH, H * DH], F32),
                   "y": ([NPB, 128, NCC * PBLK], BF16)}[t]
        dbg[t] = nc.dram_tensor("dbg_" + t, shp, dt, kind="ExternalOutput")

    hd = nc.dram_tensor("hd", [NPB, 128, NCC * PBLK], BF16)
    vtmd = nc.dram_tensor("vtmd", [NRC, 128, C], BF16)
    ktmd = nc.dram_tensor("ktmd", [NRC, 128, C], BF16)
    yd = nc.dram_tensor("yd", [NPB, 128, NCC * PBLK], BF16)
    ccs_in = nc.dram_tensor("ccs_in", [DH, H * DH], F32)
    ccs_out = nc.dram_tensor("ccs_out", [DH, H * DH], F32)
    gn_in = nc.dram_tensor("gn_in", [32, 1], F32)
    gn_out = nc.dram_tensor("gn_out", [32, 1], F32)
    mx1_in = nc.dram_tensor("mx1_in", [16, 1], F32); mx1_out = nc.dram_tensor("mx1_out", [16, 1], F32)
    mx2_in = nc.dram_tensor("mx2_in", [16, 1], F32); mx2_out = nc.dram_tensor("mx2_out", [16, 1], F32)

    SB = {}
    def sb(name, shape, dt, off):
        SB[name] = nc.alloc_sbuf_tensor_at(name, list(shape), dt, offset=off)
        return SB[name]

    SBASE = 18 * 1024
    LIM = SBASE + 190 * 1024

    o = SBASE
    x_sb = sb("x", (128, NCC * TC), BF16, o); o += 32768
    idb_sb = sb("idb", (128, 128), BF16, o); o += 256
    idf_sb = sb("idf", (128, 128), F32, o); o += 512
    onec_sb = sb("onec", (128, 1), BF16, o); o += 32
    oner_sb = sb("oner", (1, 128), BF16, o); o += 256
    onerf_sb = sb("onerf", (1, 128), F32, o); o += 512
    selT_sb = sb("selTc", (2, 128), F32, o); o += 512
    cvec_sb = sb("cvecc", (128, 8), F32, o); o += 32
    silc_sb = sb("silc", (128, 8), F32R, o); o += 32
    gnw_sb = sb("gnwc", (128, 8), F32, o); o += 32
    gnb_sb = sb("gnbc", (128, 8), F32, o); o += 32
    mod_sb = sb("modv", (128, 48), F32, o); o += 192
    modt_sb = sb("modt", (128, 48), F32, o); o += 192
    gscr_sb = sb("gscr", (128, 24), F32, o); o += 96
    clipx_sb = sb("clipxs", (1, 1), F32, o); o += 32
    clipd_sb = sb("clipds", (1, 1), F32, o); o += 32
    smask_s_sb = sb("smask_s", (64, 1), F32, o); o += 32
    smask_r_sb = sb("smask_r", (64, 1), F32, o); o += 32
    gsum_sb = sb("gsum", (8, 2), F32, o); o += 32
    gsq_sb = sb("gsq", (8, 2), F32, o); o += 32
    gsum2_sb = sb("gsum2", (2, 8), F32, o); o += 32
    gsq2_sb = sb("gsq2", (2, 8), F32, o); o += 32
    ysum_sb = sb("ysum", (128, 64), F32, o); o += 256
    ysq_sb = sb("ysq", (128, 64), F32, o); o += 256
    ysum8_sb = sb("ysum8", (128, 8), F32, o); o += 32
    ysq8_sb = sb("ysq8", (128, 8), F32, o); o += 32
    Av_sb = sb("Av", (128, 8), F32, o); o += 32
    Bv_sb = sb("Bv", (128, 8), F32, o); o += 32
    mu8_sb = sb("mu8", (128, 8), F32, o); o += 32
    sq8_sb = sb("sq8", (128, 8), F32, o); o += 32
    hmax_sb = sb("hmax64", (128, 64), F32, o); o += 256
    hmax1_sb = sb("hmax1", (128, 1), F32, o); o += 32
    smax_sb = sb("smaxT", (128, NI * NFB), F32, o); o += 512
    smax1_sb = sb("smax1", (128, 1), F32, o); o += 32
    sc16_sb = sb("sc16", (16, 1), F32, o); o += 32
    sc16b_sb = sb("sc16b", (16, 1), F32, o); o += 32
    sA_sb = sb("sA", (1, 1), F32, o); o += 32
    sB_sb = sb("sB", (1, 1), F32, o); o += 32
    rbf_sb = sb("rbf", (1, 1), BF16, o); o += 32
    rB_sb = sb("rB", (128, 1), F32, o); o += 32
    qsxB_sb = sb("qsxB", (128, 1), F32, o); o += 32
    rs2B_sb = sb("rs2B", (128, 1), F32, o); o += 32
    qsdB_sb = sb("qsdB", (128, 1), F32, o); o += 32
    wmx_sb = sb("wmx", (128, 1), F32, o); o += 32
    wri_sb = sb("wri", (128, 1), F32, o); o += 32
    wsc_sb = sb("wsc", (128, 1), F32, o); o += 32
    sgv_sb = sb("sgv", (128, NI), F32, o); o += 128
    suv_sb = sb("suv", (128, NI), F32, o); o += 128
    sqc_sb = sb("sqc", (128, 1), F32, o); o += 32
    epsv_sb = sb("epsv", (128, 1), F32, o); o += 32
    magic_sb = sb("magicc", (128, 2), F32, o); o += 32
    wmxu_sb = sb("wmxu", (128, 1), F32, o); o += 32
    wriu_sb = sb("wriu", (128, 1), F32, o); o += 32
    wscu_sb = sb("wscu", (128, 1), F32, o); o += 32
    onr2_sb = sb("onr2", (1, 256), BF16, o); o += 512
    Bvb_sb = sb("Bvb", (128, 8), BF16, o); o += 32
    R0_END = o
    assert R0_END <= SBASE + 40 * 1024, R0_END

    o = R0_END
    mrow_sb = sb("mrow", (1, TC), BF16, o); o += 4096
    rrow_sb = sb("rrow", (1, TC), BF16, o); o += 4096
    mrr_sb = sb("mrr", (1, TC), BF16, o); o += 4096
    rows_sb = sb("rowst", (1, PBLK * 5), F32, o); o += 5120
    xsq_sb = [sb(f"xsq{i}", (128, PBLK), BF16, o + i * 512) for i in range(2)]; o += 1024
    bcb_sb = sb("bcb", (128, 2 * PBLK), BF16, o); o += 1024
    hsc_sb = [sb(f"hsc{i}", (128, PBLK), BF16, o + i * 512) for i in range(2)]; o += 1024
    wfr_sb = [sb(f"wfr{i}", (128, 512), F32R, o + i * 2048) for i in range(2)]; o += 4096
    xst_sb = sb("xst", (128, TC), F32, o); o += 8192
    wf_sb = [sb(f"wf{i}", (128, 1024), F32, o + i * 4096) for i in range(2)]; o += 8192
    yl_sb = [sb(f"yl{i}", (128, NCC * PBLK), BF16, o + i * 4096) for i in range(2)]; o += 8192
    yb_sb = [sb(f"yb{i}", (128, NCC * PBLK), BF16, o + i * 4096) for i in range(2)]; o += 8192
    WA_OFF = o
    wA_sb = sb("wA", (128, 8192), BF16, o); o += 16384
    wB_sb = sb("wB", (128, 8192), BF16, o); o += 16384
    dm_sb = sb("dm", (128, H * RCH), BF16, o); o += 4096
    gq2_sb = sb("gq2c", (DH, H * RCH), BF16, o); o += 4096
    gk_sb = sb("gkc", (RCH, C), BF16, o); o += 2048
    gdec_sb = sb("gdecc", (DH, H * DH), F32, o); o += 4096
    S_sb = sb("S", (DH, H * DH), F32, o); o += 4096
    Sr_sb = sb("Srecv", (DH, H * DH), F32, o); o += 4096
    Sbf_sb = sb("Sbf", (DH, H * DH), BF16, o); o += 2048
    h_sb = [sb(f"h{i}", (128, NCC * PBLK), BF16, o + i * 4096) for i in range(2)]; o += 8192
    ktm_sb = [sb(f"ktm{i}", (128, C), BF16, o + i * 2048) for i in range(2)]; o += 4096
    vtm_sb = [sb(f"vtm{i}", (128, C), BF16, o + i * 2048) for i in range(2)]; o += 4096
    stm_sb = [sb(f"stm{i}", (128, H * RCH), BF16, o + i * 4096) for i in range(2)]; o += 8192
    modrow_sb = sb("modrow", (1, 512), F32, o); o += 2048
    qt_sb = sb("qt", (DH, H * PBLK), BF16, o); o += 8192
    kt_sb = sb("kt", (DH, H * PBLK), BF16, o); o += 8192
    assert o <= 229375, ("attn", o)

    o = R0_END
    s_sb = sb("s", (128, NI * TC), BF16, o); o += NI * TC * 2
    OUTB_OFF = o
    wig_sb = [sb(f"wig{i}", (128, 1024), BF16, o + i * 2048) for i in range(2)]; o += 4096
    wiu_sb = [sb(f"wiu{i}", (128, 1024), BF16, o + i * 2048) for i in range(2)]; o += 4096
    gsl_sb = [sb(f"gsl{i}", (128, FBLK), BF16, o + i * 1024) for i in range(2)]; o += 2048
    wdn_sb = sb("wdnat", (128, IF // 2), F32, o); o += 5632
    wdq1_sb = sb("wdq1", (128, IF // 2), F32, o); o += 5632
    wdi_sb = sb("wdint", (128, IF // 2), BF16, o); o += 2816
    wnat_sb = wdn_sb
    wq1_sb = wdq1_sb
    wint_sb = wdi_sb
    wdQ_sb = [sb(f"wdq{i}", (128, NI * 256), BF16, o + i * 11264) for i in range(2)]
    o += 22528
    x2t_sb = [sb(f"x2t{i}", (128, 256), BF16, o + i * 512) for i in range(2)]; o += 1024
    nb1_sb = sb("nb1", (128, 1024), F32, R0_END + 16384)
    nb2_sb = sb("nb2", (128, 1024), F32, R0_END + 20480)
    wdmx_sb = sb("wdmx", (128, 1), F32, o); o += 32
    wdri_sb = sb("wdri", (128, 1), F32, o); o += 32
    wdsc_sb = sb("wdsc", (128, 1), F32, o); o += 32
    assert o <= 196576, ("ffn", o)
    outb_sb = sb("outb", (128, 1024), F32, OUTB_OFF)
    h2_sb = sb("h2", (128, NCC * TC), BF16, 196576)
    assert 196576 >= o, ("h2", o)

    PS = [nc.alloc_psum_tensor(f"pb{i}", [128, 512], F32) for i in range(6)]
    PSB = [nc.alloc_psum_tensor(f"pbb{i}", [128, 1024], BF16) for i in range(2)]
    def bank(i):
        return ("PS", i)
    def bankB(i):
        return ("PS", "B", i)

    pb = PB(nc)

    def t_mm(bk, out_ap, lhsT, rhs, reads, start, stop, is_tr=False):
        def fn(e, o2=out_ap, l=lhsT, r=rhs, s=start, t=stop, it=is_tr):
            return e.matmul(o2, l, r, start=s, stop=t, is_transpose=it)
        return pb.mm(bk, fn, reads, start, stop)

    def a_act(out_ap, in_ap, func, reads, writes, scale=1.0, bias=0.0, accum=None):
        def fn(e, o2=out_ap, i=in_ap, f=func, s=scale, b=bias, a=accum):
            if a is not None:
                return e.activation(o2, i, f, bias=b, scale=s, accum_out=a)
            return e.activation(o2, i, f, bias=b, scale=s)
        return pb.op("act", fn, reads=reads, writes=writes)

    def v_tt(eng, out_ap, a_ap, b_ap, aluop, reads, writes):
        def fn(e, o2=out_ap, a=a_ap, b=b_ap, op2=aluop):
            return e.tensor_tensor(o2, a, b, op2)
        return pb.op(eng, fn, reads=reads, writes=writes)

    def v_ts(eng, out_ap, in_ap, s1, s2, op0, op1, reads, writes):
        def fn(e, o2=out_ap, i=in_ap, a=s1, b=s2, p0=op0, p1=op1):
            if b is None:
                return e.tensor_single_scalar(o2, i, a, p0)
            return e.tensor_scalar(o2, i, a, b, p0, p1)
        return pb.op(eng, fn, reads=reads, writes=writes)

    def v_stt(eng, out_ap, in0, scal, in1, op0, op1, reads, writes):
        def fn(e, o2=out_ap, i0=in0, s=scal, i1=in1, p0=op0, p1=op1):
            return e.scalar_tensor_tensor(o2, i0, s, i1, p0, p1)
        return pb.op(eng, fn, reads=reads, writes=writes)

    def v_red(eng, out_ap, in_ap, aluop, reads, writes, absval=False):
        def fn(e, o2=out_ap, i=in_ap, op2=aluop, av=absval):
            return e.tensor_reduce(o2, i, AXX, op2, apply_absolute_value=av)
        return pb.op(eng, fn, reads=reads, writes=writes)

    def v_copy(eng, out_ap, in_ap, reads, writes):
        def fn(e, o2=out_ap, i=in_ap):
            return e.tensor_copy(o2, i)
        return pb.op(eng, fn, reads=reads, writes=writes)

    def v_recip(eng, out_ap, in_ap, reads, writes):
        def fn(e, o2=out_ap, i=in_ap):
            return e.reciprocal(o2, i)
        return pb.op(eng, fn, reads=reads, writes=writes)

    def v_memset(eng, ap, val, writes):
        def fn(e, a=ap, v=val):
            return e.memset(a, v)
        return pb.op(eng, fn, reads=(), writes=writes)

    out_keys = []

    v_memset("dve", epsv_sb[:, :], EPS, writes=[("epsv",)])
    v_memset("dve", magic_sb[:, 0:1], RMAGIC, writes=[("magic",)])
    v_memset("dve", magic_sb[:, 1:2], -RMAGIC, writes=[("magic",)])
    nc.const_aps.register(F32, RMAGIC, magic_sb[:, 0:1])
    nc.const_aps.register(F32, -RMAGIC, magic_sb[:, 1:2])
    for nme, dst in (("dmask", dm_sb), ("gq2", gq2_sb), ("gk", gk_sb),
                     ("gdec", gdec_sb), ("ident_bf", idb_sb), ("ident_f32", idf_sb),
                     ("ones_col_bf", onec_sb), ("one_row_bf", oner_sb),
                     ("one_row_f32", onerf_sb), ("one_row256", onr2_sb),
                     ("selT", selT_sb), ("cvec", cvec_sb),
                     ("gnw_v", gnw_sb), ("gnb_v", gnb_sb),
                     ("clipx", clipx_sb), ("clipd", clipd_sb),
                     ("smask_send", smask_s_sb), ("smask_recv", smask_r_sb)):
        pb.dma("pool", dst[:, :], d[nme][:, :], reads=(), writes=[("c", nme)])

    for ccn in range(NCC):
        pb.dma("sp", xst_sb[:, :], d["xT"][ccn * 128:(ccn + 1) * 128, :],
               reads=[("xst",)], writes=[("xst",)])
        v_copy("dve", x_sb[:, ccn * TC:(ccn + 1) * TC], xst_sb[:, :],
               reads=[("xst",)], writes=[("x", ccn)])

    def load_weight(slot, src_name, slotkey, q="sp"):
        for ccn in range(NCC):
            st = wf_sb[ccn % 2]
            pb.dma(q, st[:, :], d[src_name][:, ccn * 1024:(ccn + 1) * 1024],
                   reads=[("wf", ccn % 2)], writes=[("wf", ccn % 2)])
            v_copy("dve", slot[:, ccn * 1024:(ccn + 1) * 1024], st[:, :],
                   reads=[("wf", ccn % 2)], writes=[(slotkey, ccn)])

    a_act(silc_sb[:, :], cvec_sb[:, :], AF.Silu, reads=[("c", "cvec")],
          writes=[("silc",)])
    for i in range(12):
        for j in range(NCC):
            pb.dma("act", wfr_sb[(2 * i + j) % 2][:, :],
                   d["wmod"][j * 128:(j + 1) * 128, i * 512:(i + 1) * 512],
                   reads=[("wfr", (2 * i + j) % 2)],
                   writes=[("wfr", (2 * i + j) % 2)])
            t_mm(bank(5), PS[5][:1, 0:512],
                 silc_sb[:, j:j + 1],
                 wfr_sb[(2 * i + j) % 2][:, :],
                 reads=[("wfr", (2 * i + j) % 2), ("silc",)],
                 start=(j == 0), stop=(j == 7))
        v_copy("dve", modrow_sb[:1, 0:512], PS[5][:1, 0:512],
               reads=[bank(5), ("modrow",)], writes=[("modrow",)])
        for k in range(4):
            t_mm(bank(2), PS[2][:, i * 4 + k: i * 4 + k + 1],
                 modrow_sb[:1, k * 128:(k + 1) * 128], idf_sb[0:1, 0:1],
                 reads=[("modrow",), ("c", "ident_f32")],
                 start=(i == 0 and k == 0), stop=(i == 11 and k == 3), is_tr=True)
    v_copy("dve", mod_sb[:, :], PS[2][:, 0:48], reads=[bank(2)], writes=[("modv",)])
    v_copy("dve", modt_sb[:, 0:8], mod_sb[:, 0:8], reads=[("modv",)], writes=[("mt", 0)])
    v_ts("dve", modt_sb[:, 8:16], mod_sb[:, 8:16], 1.0, None, ALU.add, None,
         reads=[("modv",)], writes=[("mt", 1)])
    v_copy("dve", modt_sb[:, 24:32], mod_sb[:, 24:32], reads=[("modv",)], writes=[("mt", 3)])
    v_ts("dve", modt_sb[:, 32:40], mod_sb[:, 32:40], 1.0, None, ALU.add, None,
         reads=[("modv",)], writes=[("mt", 4)])
    for (c0, mi) in ((16, 2), (40, 5)):
        a_act(gscr_sb[:, 0:8], mod_sb[:, c0:c0 + 8], AF.Sign,
              reads=[("modv",), ("gscr",)], writes=[("gscr",)])
        v_tt("dve", gscr_sb[:, 8:16], gscr_sb[:, 0:8], gscr_sb[:, 0:8], ALU.mult,
             reads=[("gscr",)], writes=[("gscr",)])
        v_ts("dve", gscr_sb[:, 8:16], gscr_sb[:, 8:16], -1.0, 1.0, ALU.mult, ALU.add,
             reads=[("gscr",)], writes=[("gscr",)])
        v_tt("dve", gscr_sb[:, 8:16], gscr_sb[:, 0:8], gscr_sb[:, 8:16], ALU.add,
             reads=[("gscr",)], writes=[("gscr",)])
        a_act(gscr_sb[:, 16:24], mod_sb[:, c0:c0 + 8], AF.Abs,
              reads=[("modv",), ("gscr",)], writes=[("gscr",)])
        v_ts("dve", gscr_sb[:, 16:24], gscr_sb[:, 16:24], 0.25, None, ALU.max, None,
             reads=[("gscr",)], writes=[("gscr",)])
        v_tt("dve", modt_sb[:, c0:c0 + 8], gscr_sb[:, 8:16], gscr_sb[:, 16:24],
             ALU.mult, reads=[("gscr",)], writes=[("mt", mi)])

    load_weight(wA_sb, "wk_l", "wA")
    load_weight(wB_sb, "wv_l", "wB")

    sa_c = modt_sb[:, 0:8]; a1_c = modt_sb[:, 8:16]; ga_c = modt_sb[:, 16:24]
    sb_c = modt_sb[:, 24:32]; a2_c = modt_sb[:, 32:40]; gb_c = modt_sb[:, 40:48]
    MT = [("mt", i) for i in range(6)]

    if "mod" in dbg:
        pb.dma("pool", dbg["mod"][:, :], modt_sb[:, :], reads=MT, writes=[("dbgmod",)])
        out_keys.append(("dbgmod",))

    def ln_stats(src_keyf):
        for blk in range(NPB):
            ln_stats_block(blk, src_keyf)

    def ln_stats_block(blk, src_keyf):
        if True:
            t0 = blk * PBLK
            for ccn in range(NCC):
                xs = x_sb[:, ccn * TC + t0: ccn * TC + t0 + PBLK]
                t_mm(bank(4), PS[4][:1, 0:PBLK], onec_sb[:, :], xs,
                     reads=[("c", "ones_col_bf"), src_keyf(ccn, blk)],
                     start=(ccn == 0), stop=(ccn == 7))
            for ccn in range(NCC):
                xs = x_sb[:, ccn * TC + t0: ccn * TC + t0 + PBLK]
                sqt = xsq_sb[ccn % 2]
                a_act(sqt[:, :], xs, AF.Square,
                      reads=[src_keyf(ccn, blk), ("xsq", ccn % 2)],
                      writes=[("xsq", ccn % 2)])
                t_mm(bank(5), PS[5][:1, 0:PBLK], onec_sb[:, :], sqt[:, :],
                     reads=[("c", "ones_col_bf"), ("xsq", ccn % 2)],
                     start=(ccn == 0), stop=(ccn == 7))
            rw = rows_sb
            P = PBLK
            v_ts("dve", rw[:1, 0:P], PS[4][:1, 0:P], 1.0 / C, None, ALU.mult, None,
                 reads=[bank(4), ("rows",)], writes=[("rows",)])
            v_ts("dve", rw[:1, P:2 * P], PS[5][:1, 0:P], 1.0 / C, None, ALU.mult, None,
                 reads=[bank(5), ("rows",)], writes=[("rows",)])
            v_tt("dve", rw[:1, 2 * P:3 * P], rw[:1, 0:P], rw[:1, 0:P], ALU.mult,
                 reads=[("rows",)], writes=[("rows",)])
            v_tt("dve", rw[:1, 2 * P:3 * P], rw[:1, P:2 * P], rw[:1, 2 * P:3 * P],
                 ALU.subtract, reads=[("rows",)], writes=[("rows",)])
            a_act(rw[:1, 3 * P:4 * P], rw[:1, 2 * P:3 * P], AF.Sqrt,
                  reads=[("rows",), ("epsv",)], writes=[("rows",)],
                  bias=epsv_sb[0:1, 0:1])
            v_recip("dve", rw[:1, 4 * P:5 * P], rw[:1, 3 * P:4 * P],
                    reads=[("rows",)], writes=[("rows",)])
            v_copy("dve", mrow_sb[:1, t0:t0 + P], rw[:1, 0:P],
                   reads=[("rows",)], writes=[("mrow", blk)])
            v_copy("dve", rrow_sb[:1, t0:t0 + P], rw[:1, 4 * P:5 * P],
                   reads=[("rows",)], writes=[("rrow", blk)])
            v_tt("dve", mrr_sb[:1, t0:t0 + P], mrow_sb[:1, t0:t0 + P],
                 rrow_sb[:1, t0:t0 + P], ALU.mult,
                 reads=[("mrow", blk), ("rrow", blk), ("mrr", blk)],
                 writes=[("mrr", blk)])

    def bcast_rows(blk):
        t0 = blk * PBLK
        t_mm(bank(4), PS[4][:, 0:PBLK], oner_sb[:, :], rrow_sb[:1, t0:t0 + PBLK],
             reads=[("c", "one_row_bf"), ("rrow", blk)], start=True, stop=False)
        t_mm(bank(4), PS[4][:, PBLK:2 * PBLK], oner_sb[:, :], mrr_sb[:1, t0:t0 + PBLK],
             reads=[("c", "one_row_bf"), ("mrr", blk)], start=False, stop=True)
        a_act(bcb_sb[:, :], PS[4][:, 0:2 * PBLK], AF.Copy,
              reads=[bank(4), ("bcb",)], writes=[("bcb",)])

    def h_block(blk, srckeyf, dstf, dkeyf, a_vec, s_vec, mkeys):
        t0 = blk * PBLK
        bcast_rows(blk)
        for ccn in range(NCC):
            xs = x_sb[:, ccn * TC + t0: ccn * TC + t0 + PBLK]
            h1 = hsc_sb[ccn % 2]
            v_tt("dve", h1[:, :], xs, bcb_sb[:, 0:PBLK], ALU.mult,
                 reads=[srckeyf(ccn, blk), ("bcb",), ("hsc", ccn % 2)],
                 writes=[("hsc", ccn % 2)])
            v_tt("dve", h1[:, :], h1[:, :], bcb_sb[:, PBLK:2 * PBLK], ALU.subtract,
                 reads=[("bcb",), ("hsc", ccn % 2)],
                 writes=[("hsc", ccn % 2)])
            v_ts("dve", dstf(ccn, blk), h1[:, :], a_vec[:, ccn:ccn + 1],
                 s_vec[:, ccn:ccn + 1], ALU.mult, ALU.add,
                 reads=[("hsc", ccn % 2)] + mkeys + [dkeyf(ccn, blk)],
                 writes=[dkeyf(ccn, blk)])

    NPK = 14
    PSLOT_COL = [18, 19, 20, 21, 15, 16, 17, 8, 9, 10, 11, 12, 13, 14]

    def pslot(i, m):
        base = PSLOT_COL[i] * TC + m * 1024
        return s_sb[:, base:base + 1024]

    def quant_load(nat_name, ich, natb, natk, q):
        pb.dma(q, natb[:, 0:1024], d[nat_name][ich * 128:(ich + 1) * 128, :],
               reads=[natk], writes=[natk])

    def quant_gu(nat_name, ich, dstslot, dkey, svec, sveckey, mulq,
                 natb, natk, mx, mxk, ri, rik, sc, sck, q, rb0, qtr,
                 preloaded=False):
        if not preloaded:
            quant_load(nat_name, ich, natb, natk, q)
        v_red("dve", mx[:, :], natb[:, 0:1024], ALU.max,
              reads=[natk, mxk], writes=[mxk], absval=True)
        v_recip("dve", ri[:, :], mx[:, :], reads=[mxk, rik],
                writes=[rik])
        v_ts("dve", ri[:, :], ri[:, :], QMAX, None, ALU.mult, None,
             reads=[rik], writes=[rik])
        if mulq:
            v_ts("dve", sc[:, :], mx[:, :], 1.0 / QMAX, None, ALU.mult, None,
                 reads=[mxk, sck], writes=[sck])
            v_tt("dve", svec[:, ich:ich + 1], sc[:, :], qsxB_sb[:, 0:1], ALU.mult,
                 reads=[sck, ("qsxB",)], writes=[(sveckey, ich)])
        else:
            v_ts("dve", svec[:, ich:ich + 1], mx[:, :], 1.0 / QMAX, None,
                 ALU.mult, None, reads=[mxk], writes=[(sveckey, ich)])
        for c2 in range(2):
            bq = rb0 + c2
            a_act(PS[bq][:, 0:512], natb[:, c2 * 512:(c2 + 1) * 512], AF.Copy,
                  reads=[natk, rik, ("magic",), bank(bq)], writes=[bank(bq)],
                  scale=ri[:, 0:1], bias=RMAGIC)
            a_act(wint_sb[:, c2 * 512:(c2 + 1) * 512], PS[bq][:, 0:512], AF.Copy,
                  reads=[bank(bq), ("magic",), ("wint",)], writes=[("wint",)],
                  bias=-RMAGIC)
        pb.dma_tr(qtr, dstslot[:, :].rearrange("p (j c) -> p j c", j=NCC),
                  wint_sb[:, 0:1024].rearrange("p (j c) -> p j c", j=NCC),
                  reads=[("wint",)], writes=[dkey + (j,) for j in range(NCC)])

    def quant_pk(i, mq=False, rb0=2, dbuf=False):
        ng, nk = ((nb1_sb, ("nb1",)) if dbuf and i % 2 else (wdn_sb, ("wnat",)))
        nu, uk = ((nb2_sb, ("nb2",)) if dbuf and i % 2 else (wdq1_sb, ("wq1",)))
        quant_gu("wg_nat", i, pslot(i, 0), ("pk", i, 0), sgv_sb, "sg", mq,
                 ng, nk, wmx_sb, ("wmx",), wri_sb, ("wri",),
                 wsc_sb, ("wsc",), "sp", rb0, "sp")
        quant_gu("wu_nat", i, pslot(i, 1), ("pk", i, 1), suv_sb, "su", mq,
                 nu, uk, wmxu_sb, ("wmxu",), wriu_sb, ("wriu",),
                 wscu_sb, ("wscu",), "pool", rb0, "sp")

    def prep_wd(ccn):
        q = ccn // 2
        pb.dma("sp", wdn_sb[:, :],
               d["wd_nat"][ccn * 128:(ccn + 1) * 128, 0:IF // 2],
               reads=[("wnat",)], writes=[("wnat",)])
        pb.dma("act", wdq1_sb[:, :],
               d["wd_nat"][ccn * 128:(ccn + 1) * 128, IF // 2:IF],
               reads=[("wq1",)], writes=[("wq1",)])
        v_red("dve", wdmx_sb[:, :], wdn_sb[:, :], ALU.max,
              reads=[("wnat",), ("wdmx",)], writes=[("wdmx",)], absval=True)
        v_red("dve", wdsc_sb[:, :], wdq1_sb[:, :], ALU.max,
              reads=[("wq1",), ("wdsc",)], writes=[("wdsc",)], absval=True)
        v_tt("dve", wdmx_sb[:, :], wdmx_sb[:, :], wdsc_sb[:, :], ALU.max,
             reads=[("wdmx",), ("wdsc",)], writes=[("wdmx",)])
        v_recip("dve", wdri_sb[:, :], wdmx_sb[:, :], reads=[("wdmx",), ("wdri",)],
                writes=[("wdri",)])
        v_ts("dve", wdri_sb[:, :], wdri_sb[:, :], QMAX, None, ALU.mult, None,
             reads=[("wdri",)], writes=[("wdri",)])
        v_ts("dve", wdsc_sb[:, :], wdmx_sb[:, :], 1.0 / QMAX, None, ALU.mult,
             None, reads=[("wdmx",), ("wdsc",)], writes=[("wdsc",)])
        v_tt("dve", wdsc_sb[:, :], wdsc_sb[:, :], gb_c[:, ccn:ccn + 1], ALU.mult,
             reads=[("wdsc",), ("mt", 5)], writes=[("wdsc",)])
        for hf in range(2):
            natb, natk = ((wdn_sb, ("wnat",)) if hf == 0
                          else (wdq1_sb, ("wq1",)))
            for c2 in range(3):
                w = 512 if c2 < 2 else IF // 2 - 1024
                bq = 4 + (c2 % 2)
                a_act(PS[bq][:, 0:w], natb[:, c2 * 512:c2 * 512 + w], AF.Copy,
                      reads=[natk, ("wdri",), ("magic",), bank(bq)],
                      writes=[bank(bq)], scale=wdri_sb[:, 0:1], bias=RMAGIC)
                a_act(wdi_sb[:, c2 * 512:c2 * 512 + w], PS[bq][:, 0:w], AF.Copy,
                      reads=[bank(bq), ("magic",), ("wint",)],
                      writes=[("wint",)], bias=-RMAGIC)
            v_ts("dve", wdi_sb[:, :], wdi_sb[:, :], wdsc_sb[:, 0:1], None,
                 ALU.mult, None, reads=[("wint",), ("wdsc",)],
                 writes=[("wint",)])
            nj = NI // 2
            c0 = (ccn % 2) * 128
            pb.dma_tr("sp",
                      wdQ_sb[q % 2][:, :].rearrange(
                          "p (j k) -> p j k", k=256)[:, hf * nj:(hf + 1) * nj,
                                                     c0:c0 + 128],
                      wdi_sb[:, :].rearrange("p (j c) -> p j c", j=nj),
                      reads=[("wint",)],
                      writes=[("wdQ", q % 2, hf * nj + jj, ccn % 2)
                              for jj in range(nj)])


    def xkey(ccn, blk):
        return ("x", ccn)
    ln_stats(xkey)

    def hsl(ccn, blk):
        return h_sb[blk % 2][:, ccn * PBLK:(ccn + 1) * PBLK]

    def hkey(ccn, blk):
        return ("h", blk % 2, ccn)

    v_memset("dve", S_sb[:, :], 0.0, writes=[("S",)])

    def spill_h(b2):
        pb.dma("sp", hd[b2, :, :], h_sb[b2 % 2][:, :],
               reads=[("h", b2 % 2, ccn) for ccn in range(NCC)],
               writes=[("hd", b2)])

    h_block(0, xkey, hsl, hkey, a1_c, sa_c, [("mt", 1), ("mt", 0)])
    spill_h(0)
    for blk in range(NPB):
        for tci in range(2):
            ch = blk * 2 + tci
            kbuf = ktm_sb[ch % 2]
            vbuf = vtm_sb[ch % 2]
            for half in range(2):
                bk = bank(half)
                for ccn in range(NCC):
                    t_mm(bk, PS[half][:, 0:512],
                         hsl(ccn, blk)[:, tci * 128:(tci + 1) * 128],
                         wA_sb[:, ccn * 1024 + half * 512: ccn * 1024 + (half + 1) * 512],
                         reads=[hkey(ccn, blk), ("wA", ccn)],
                         start=(ccn == 0), stop=(ccn == 7))
                v_tt("dve", kbuf[:, half * 512:(half + 1) * 512], PS[half][:, 0:512],
                     gk_sb[:, half * 512:(half + 1) * 512], ALU.mult,
                     reads=[bk, ("c", "gk"), ("ktm", ch % 2)],
                     writes=[("ktm", ch % 2)])
                bkv = bank(2 + half)
                for ccn in range(NCC):
                    t_mm(bkv, PS[2 + half][:, 0:512],
                         hsl(ccn, blk)[:, tci * 128:(tci + 1) * 128],
                         wB_sb[:, ccn * 1024 + half * 512: ccn * 1024 + (half + 1) * 512],
                         reads=[hkey(ccn, blk), ("wB", ccn)],
                         start=(ccn == 0), stop=(ccn == 7))
                a_act(vbuf[:, half * 512:(half + 1) * 512], PS[2 + half][:, 0:512],
                      AF.Copy, reads=[bkv, ("vtm", ch % 2)], writes=[("vtm", ch % 2)])
            pb.dma("act", ktmd[ch, :, :], kbuf[:, :],
                   reads=[("ktm", ch % 2)], writes=[("ktmd", ch)])
            pb.dma("act", vtmd[ch, :, :], vbuf[:, :],
                   reads=[("vtm", ch % 2)], writes=[("vtmd", ch)])
            for h in range(H):
                hb = 4 if h < 8 else 5
                t_mm(bank(hb), PS[hb][0:DH, (h % 8) * DH:(h % 8 + 1) * DH],
                     kbuf[:, h * DH:(h + 1) * DH], vbuf[:, h * DH:(h + 1) * DH],
                     reads=[("ktm", ch % 2), ("vtm", ch % 2)],
                     start=(h % 8 == 0), stop=(h % 8 == 7))
            v_tt("dve", Sr_sb[:, :], S_sb[:, :], gdec_sb[:, :], ALU.mult,
                 reads=[("S",), ("c", "gdec"), ("Srecv",)], writes=[("Srecv",)])
            v_tt("dve", S_sb[:, 0:512], Sr_sb[:, 0:512], PS[4][0:DH, 0:512], ALU.add,
                 reads=[("Srecv",), bank(4), ("S",)], writes=[("S",)])
            v_tt("dve", S_sb[:, 512:1024], Sr_sb[:, 512:1024], PS[5][0:DH, 0:512],
                 ALU.add, reads=[("Srecv",), bank(5), ("S",)], writes=[("S",)])
            if tci == 0 and blk + 1 < NPB:
                h_block(blk + 1, xkey, hsl, hkey, a1_c, sa_c, [("mt", 1), ("mt", 0)])
                spill_h(blk + 1)

    v_ts("dve", Sr_sb[:, :], S_sb[:, :], smask_s_sb[:, 0:1], None, ALU.mult, None,
         reads=[("S",), ("c", "smask_send"), ("Srecv",)], writes=[("Srecv",)])
    pb.dma("pool", ccs_in.ap(), Sr_sb[:, :], reads=[("Srecv",)], writes=[("ccsin",)])
    pb.cc("ccstate", "AllReduce", ALU.add, [[0, 1], [2, 3], [4, 5], [6, 7]],
          ccs_in.ap().opt(), ccs_out.ap().opt(),
          reads=[("ccsin",)], writes=[("ccsout",)])
    pb.dma("pool", Sr_sb[:, :], ccs_out.ap(), reads=[("ccsout",), ("Srecv",)],
           writes=[("Srecv",)])
    v_ts("dve", S_sb[:, :], Sr_sb[:, :], smask_r_sb[:, 0:1], None, ALU.mult, None,
         reads=[("Srecv",), ("c", "smask_recv"), ("S",)], writes=[("S",)])
    v_copy("dve", Sbf_sb[:, :], S_sb[:, :], reads=[("S",), ("Sbf",)],
           writes=[("Sbf",)])

    if "state" in dbg:
        pb.dma("pool", dbg["state"][:, :], S_sb[:, :], reads=[("S",)],
               writes=[("dbgS",)])
        out_keys.append(("dbgS",))

    if phase_limit >= 2:
        def fetch_kv(ch):
            pb.dma("act", ktm_sb[ch % 2][:, :], ktmd[ch, :, :],
                   reads=[("ktmd", ch), ("ktm", ch % 2)], writes=[("ktm", ch % 2)])
            pb.dma("act", vtm_sb[ch % 2][:, :], vtmd[ch, :, :],
                   reads=[("vtmd", ch), ("vtm", ch % 2)], writes=[("vtm", ch % 2)])

        def fetch_h(b2):
            pb.dma("sp", h_sb[b2 % 2][:, :], hd[b2, :, :],
                   reads=[("hd", b2)] + [("h", b2 % 2, ccn) for ccn in range(NCC)],
                   writes=[("h", b2 % 2, ccn) for ccn in range(NCC)])

        def proj(blk):
            for (slot, skey, dst, dkey) in ((wB_sb, "wB", qt_sb, "qt"),
                                            (wA_sb, "wA", kt_sb, "kt")):
                for ecn in range(NCC):
                    bk = bank(ecn % 2)
                    for ccn in range(NCC):
                        t_mm(bk, PS[ecn % 2][:, 0:PBLK],
                             slot[:, ccn * 1024 + ecn * 128: ccn * 1024 + (ecn + 1) * 128],
                             hsl(ccn, blk),
                             reads=[(skey, ccn), hkey(ccn, blk)],
                             start=(ccn == 0), stop=(ccn == 7))
                    for sub in range(2):
                        hh = 2 * ecn + sub
                        dks = ([("qt", hh, 0), ("qt", hh, 1)] if dkey == "qt"
                               else [("kt", hh)])
                        a_act(dst[:DH, hh * PBLK:(hh + 1) * PBLK],
                              PS[ecn % 2][sub * 64:(sub + 1) * 64, 0:PBLK], AF.Copy,
                              reads=[bk] + dks, writes=dks)

        def scores(ch):
            blk, tci = ch // 2, ch % 2
            smbuf = stm_sb[ch % 2]
            for g in range(4):
                bks = bank(2 + g % 2)
                for hi in range(4):
                    hh = 4 * g + hi
                    sl = slice(hh * PBLK + tci * 128, hh * PBLK + tci * 128 + 128)
                    t_mm(bks, PS[2 + g % 2][:, hi * 128:(hi + 1) * 128],
                         kt_sb[:DH, sl], qt_sb[:DH, sl],
                         reads=[("kt", hh), ("qt", hh, tci)],
                         start=(hi == 0), stop=(hi == 3))
                v_tt("dve", smbuf[:, g * 512:(g + 1) * 512],
                     PS[2 + g % 2][:, 0:512], dm_sb[:, g * 512:(g + 1) * 512],
                     ALU.mult,
                     reads=[bks, ("c", "dmask"), ("stm", ch % 2)],
                     writes=[("stm", ch % 2)])

        def rest(ch):
            blk, tci = ch // 2, ch % 2
            kbuf = ktm_sb[ch % 2]
            vbuf = vtm_sb[ch % 2]
            smbuf = stm_sb[ch % 2]
            if sub2 >= 4:
                for h in range(H):
                    hb = 0 if h < 8 else 1
                    t_mm(bank(hb), PS[hb][0:DH, (h % 8) * DH:(h % 8 + 1) * DH],
                         kbuf[:, h * DH:(h + 1) * DH], vbuf[:, h * DH:(h + 1) * DH],
                         reads=[("ktm", ch % 2), ("vtm", ch % 2)],
                         start=(h % 8 == 0), stop=(h % 8 == 7))
            qv = qt_sb[:DH, :].rearrange("p (h t) -> p h t", h=H)[:, :, tci * 128:(tci + 1) * 128]
            gv = gq2_sb[:DH, :].rearrange("p (h t) -> p h t", h=H)
            v_tt("dve", qv, qv, gv, ALU.mult,
                 reads=[("c", "gq2")] + [("qt", hh, tci) for hh in range(H)],
                 writes=[("qt", hh, tci) for hh in range(H)])
            for g in range(4 if sub2 >= 3 else 0):
                bky = bank(4 + g % 2)
                for hi in range(4):
                    hh = 4 * g + hi
                    t_mm(bky, PS[4 + g % 2][0:DH, hi * 128:(hi + 1) * 128],
                         vbuf[:, hh * DH:(hh + 1) * DH],
                         smbuf[:, (g * 4 + hi) * 128:(g * 4 + hi + 1) * 128],
                         reads=[("vtm", ch % 2), ("stm", ch % 2)],
                         start=(hi == 0), stop=(False if sub3 >= 2 else hi == 3))
                if sub3 >= 2:
                    for hi in range(4):
                        hh = 4 * g + hi
                        sl = slice(hh * PBLK + tci * 128, hh * PBLK + tci * 128 + 128)
                        t_mm(bky, PS[4 + g % 2][0:DH, hi * 128:(hi + 1) * 128],
                             Sbf_sb[:DH, hh * DH:(hh + 1) * DH], qt_sb[:DH, sl],
                             reads=[("Sbf",), ("qt", hh, tci)],
                             start=False, stop=(hi == 3))
                for hi in range(4):
                    hh = 4 * g + hi
                    ccn = hh // 2
                    r0 = (hh % 2) * 64 if sub3 >= 3 else 0
                    a_act(yb_sb[blk % 2][r0:r0 + 64,
                                         ccn * PBLK + tci * 128: ccn * PBLK + tci * 128 + 128],
                          PS[4 + g % 2][0:DH, hi * 128:(hi + 1) * 128], AF.Copy,
                          reads=[bky, ("yb", blk % 2)], writes=[("yb", blk % 2)])
            if sub2 >= 4:
                v_tt("dve", Sr_sb[:, :], S_sb[:, :], gdec_sb[:, :], ALU.mult,
                     reads=[("S",), ("c", "gdec"), ("Srecv",)], writes=[("Srecv",)])
                v_tt("dve", S_sb[:, 0:512], Sr_sb[:, 0:512], PS[0][0:DH, 0:512],
                     ALU.add, reads=[("Srecv",), bank(0), ("S",)], writes=[("S",)])
                v_tt("dve", S_sb[:, 512:1024], Sr_sb[:, 512:1024], PS[1][0:DH, 0:512],
                     ALU.add, reads=[("Srecv",), bank(1), ("S",)], writes=[("S",)])
                v_copy("dve", Sbf_sb[:, :], S_sb[:, :], reads=[("S",), ("Sbf",)],
                       writes=[("Sbf",)])
            if ch + 2 < NRC:
                fetch_kv(ch + 2)

        def gn_stats_y(blk):
            for ccn in range(NCC if sub2 >= 5 else 0):
                ys = yb_sb[blk % 2][:, ccn * PBLK:(ccn + 1) * PBLK]
                col = blk * 8 + ccn
                a_act(hsc_sb[0][:, :], ys, AF.Copy,
                      reads=[("yb", blk % 2), ("hsc", 0)],
                      writes=[("hsc", 0), ("ysum", col)],
                      accum=ysum_sb[:, col:col + 1])
                a_act(hsc_sb[1][:, :], ys, AF.Square,
                      reads=[("yb", blk % 2), ("hsc", 1)],
                      writes=[("hsc", 1), ("ysq", col)],
                      accum=ysq_sb[:, col:col + 1])
            if sub2 >= 3:
                pb.dma("act", yd[blk, :, :], yb_sb[blk % 2][:, :],
                       reads=[("yb", blk % 2)], writes=[("yd", blk)])
                if blk < 2:
                    pb.dma("act", yl_sb[blk][:, :], yd[blk, :, :],
                           reads=[("yd", blk), ("yl", blk)], writes=[("yl", blk)])

        fetch_h(0)
        fetch_h(1)
        load_weight(wB_sb, "wq_l", "wB")
        fetch_kv(0)
        fetch_kv(1)
        for blk in range(NPB):
            proj(blk)
            if blk == NPB - 1:
                load_weight(wA_sb, "wo_l", "wA")
            if blk > 0:
                gn_stats_y(blk - 1)
            if blk + 2 < NPB:
                fetch_h(blk + 2)
            if sub2 >= 2:
                scores(2 * blk)
                scores(2 * blk + 1)
                rest(2 * blk)
                rest(2 * blk + 1)
        gn_stats_y(NPB - 1)

        if sub2 < 6:
            pb.dma("sp", out_d[0:128, 0:1024], xst_sb[:, 0:1024],
                   reads=[("xst",)], writes=[("outz",)])
            pb.op("sp", lambda e: e.nop(), reads=[("outz",)], writes=())
            pb.finalize()
            _emit(nc, pb)
            return nc
        ysr = ysum_sb[:, :].rearrange("p (a b) -> p b a", a=NPB)
        v_red("dve", ysum8_sb[:, :], ysr, ALU.add,
              reads=[("ysum", c2) for c2 in range(64)], writes=[("ysum8",)])
        ysr2 = ysq_sb[:, :].rearrange("p (a b) -> p b a", a=NPB)
        v_red("dve", ysq8_sb[:, :], ysr2, ALU.add,
              reads=[("ysq", c2) for c2 in range(64)], writes=[("ysq8",)])
        t_mm(bank(2), PS[2][0:8, 0:128], ysum8_sb[:, :], idf_sb[:, :],
             reads=[("ysum8",), ("c", "ident_f32")], start=True, stop=True, is_tr=True)
        t_mm(bank(3), PS[3][0:8, 0:128], ysq8_sb[:, :], idf_sb[:, :],
             reads=[("ysq8",), ("c", "ident_f32")], start=True, stop=True, is_tr=True)
        v_red("dve", gsum_sb[:, :],
              PS[2][0:8, 0:128].rearrange("p (h e) -> p h e", h=2), ALU.add,
              reads=[bank(2)], writes=[("gsum",)])
        v_red("dve", gsq_sb[:, :],
              PS[3][0:8, 0:128].rearrange("p (h e) -> p h e", h=2), ALU.add,
              reads=[bank(3)], writes=[("gsq",)])
        pb.dma("pool", gn_in[0:16, :], gsum_sb[:, :], reads=[("gsum",)],
               writes=[("gnin",)], slow=True)
        pb.dma("pool", gn_in[16:32, :], gsq_sb[:, :], reads=[("gsq",), ("gnin",)],
               writes=[("gnin",)], slow=True)
        pb.cc("ccgn", "AllReduce", ALU.add, [[0, 1], [2, 3], [4, 5], [6, 7]],
              gn_in.ap().opt(), gn_out.ap().opt(),
              reads=[("gnin",)], writes=[("gnout",)])
        pb.dma("pool", gsum2_sb[:, :],
               gn_out[0:16, :].rearrange("(a b) c -> b (a c)", b=2),
               reads=[("gnout",)], writes=[("gsum2",)], slow=True)
        pb.dma("pool", gsq2_sb[:, :],
               gn_out[16:32, :].rearrange("(a b) c -> b (a c)", b=2),
               reads=[("gnout",)], writes=[("gsq2",)], slow=True)
        for ccn in range(NCC):
            t_mm(bank(2), PS[2][:, ccn:ccn + 1], selT_sb[:, :], gsum2_sb[:, ccn:ccn + 1],
                 reads=[("c", "selT"), ("gsum2",)], start=(ccn == 0), stop=(ccn == 7))
            t_mm(bank(3), PS[3][:, ccn:ccn + 1], selT_sb[:, :], gsq2_sb[:, ccn:ccn + 1],
                 reads=[("c", "selT"), ("gsq2",)], start=(ccn == 0), stop=(ccn == 7))
        NTOT = float(T * DH)
        v_ts("dve", mu8_sb[:, :], PS[2][:, 0:8], 1.0 / NTOT, None, ALU.mult, None,
             reads=[bank(2)], writes=[("mu8",)])
        v_ts("dve", sq8_sb[:, :], PS[3][:, 0:8], 1.0 / NTOT, None, ALU.mult, None,
             reads=[bank(3)], writes=[("sq8",)])
        v_tt("dve", Av_sb[:, :], mu8_sb[:, :], mu8_sb[:, :], ALU.mult,
             reads=[("mu8",), ("Av",)], writes=[("Av",)])
        v_tt("dve", Av_sb[:, :], sq8_sb[:, :], Av_sb[:, :], ALU.subtract,
             reads=[("sq8",), ("Av",)], writes=[("Av",)])
        a_act(Bv_sb[:, :], Av_sb[:, :], AF.Sqrt, reads=[("Av",), ("Bv",), ("epsv",)],
              writes=[("Bv",)], bias=epsv_sb[:, 0:1])
        v_recip("dve", Av_sb[:, :], Bv_sb[:, :], reads=[("Bv",), ("Av",)],
                writes=[("Av",)])
        v_tt("dve", Av_sb[:, :], Av_sb[:, :], gnw_sb[:, :], ALU.mult,
             reads=[("Av",), ("c", "gnw_v")], writes=[("Av",)])
        v_tt("dve", Bv_sb[:, :], mu8_sb[:, :], Av_sb[:, :], ALU.mult,
             reads=[("mu8",), ("Av",), ("Bv",)], writes=[("Bv",)])
        v_tt("dve", Bv_sb[:, :], gnb_sb[:, :], Bv_sb[:, :], ALU.subtract,
             reads=[("c", "gnb_v"), ("Bv",)], writes=[("Bv",)])

        if sub2 < 7:
            pb.dma("sp", out_d[0:128, 0:1024], xst_sb[:, 0:1024],
                   reads=[("xst",)], writes=[("outz",)])
            pb.op("sp", lambda e: e.nop(), reads=[("outz",)], writes=())
            pb.finalize()
            _emit(nc, pb)
            return nc
        pb.barrier()
        v_copy("dve", Bvb_sb[:, :], Bv_sb[:, :], reads=[("Bv",)], writes=[("Bvb",)])
        for half in range(2):
            bkb = bank(2 + half)
            for ccn in range(NCC):
                t_mm(bkb, PS[2 + half][:1, 0:512], Bvb_sb[:, ccn:ccn + 1],
                     wA_sb[:, ccn * 1024 + half * 512: ccn * 1024 + (half + 1) * 512],
                     reads=[("Bvb",), ("wA", ccn)],
                     start=(ccn == 0), stop=(ccn == 7))
            a_act(dm_sb[0:1, half * 512:(half + 1) * 512], PS[2 + half][:1, 0:512],
                  AF.Copy, reads=[bkb, ("biasr",)], writes=[("biasr",)])
        for ccn in range(NCC):
            v_ts("dve", wA_sb[:, ccn * 1024:(ccn + 1) * 1024],
                 wA_sb[:, ccn * 1024:(ccn + 1) * 1024], Av_sb[:, ccn:ccn + 1],
                 None, ALU.mult, None,
                 reads=[("wA", ccn), ("Av",)], writes=[("wA", ccn)])

        def _stats_h2(blk):
            ln_stats_block(blk, lambda c2, b2: ("x2", c2, b2))
            if phase_limit >= 3:
                h_block(blk, lambda c2, b2: ("x2", c2, b2),
                        lambda c2, b2: h2_sb[:, c2 * TC + b2 * PBLK: c2 * TC + (b2 + 1) * PBLK],
                        lambda c2, b2: ("h2", c2, b2), a2_c, sb_c,
                        [("mt", 4), ("mt", 3)])
                for ccn in range(NCC):
                    col = blk * 8 + ccn
                    v_red("dve", hmax_sb[:, col:col + 1],
                          h2_sb[:, ccn * TC + blk * PBLK: ccn * TC + (blk + 1) * PBLK],
                          ALU.max, reads=[("h2", ccn, blk)],
                          writes=[("hmaxc", col)], absval=True)

        for blk in range(NPB):
            ybuf = yl_sb[blk % 2]
            for con in range(NCC):
                bk = bank(con % 2)
                for ccn in range(NCC):
                    t_mm(bk, PS[con % 2][:, 0:PBLK],
                         wA_sb[:, ccn * 1024 + con * 128: ccn * 1024 + (con + 1) * 128],
                         ybuf[:, ccn * PBLK:(ccn + 1) * PBLK],
                         reads=[("wA", ccn), ("yl", blk % 2)],
                         start=(ccn == 0), stop=False)
                t_mm(bk, PS[con % 2][:, 0:PBLK],
                     dm_sb[0:1, con * 128:(con + 1) * 128], onr2_sb[0:1, :],
                     reads=[("biasr",), ("c", "one_row256")],
                     start=False, stop=True)
                xs = x_sb[:, con * TC + blk * PBLK: con * TC + (blk + 1) * PBLK]
                v_stt("dve", xs, PS[con % 2][:, 0:PBLK], ga_c[:, con:con + 1], xs,
                      ALU.mult, ALU.add,
                      reads=[bk, ("mt", 2), ("x", con)],
                      writes=[("x", con), ("x2", con, blk)])
            if blk + 2 < NPB:
                pb.dma("act", yl_sb[blk % 2][:, :], yd[blk + 2, :, :],
                       reads=[("yd", blk + 2), ("yl", blk % 2)],
                       writes=[("yl", blk % 2)])
            if 1 <= blk <= 4:
                quant_pk(blk - 1)
            if blk > 0:
                _stats_h2(blk - 1)
        _stats_h2(NPB - 1)

        if "x2" in dbg:
            pb.dma("pool", dbg["x2"][:, :], x_sb[:, :],
                   reads=[("x2", ccn, blk) for ccn in range(NCC) for blk in range(NPB)],
                   writes=[("dbgx2",)])
            out_keys.append(("dbgx2",))
        if "y" in dbg:
            def fnd2d(e, o2=dbg["y"].ap(), i=yd.ap()):
                return e.dma_start(out=o2, in_=i)
            pb.op("pool", fnd2d, reads=[("yd", blk) for blk in range(NPB)],
                  writes=[("dbgy",)], amt=16, kind="dma")
            out_keys.append(("dbgy",))

    if phase_limit >= 3:
        pb.barrier()

        def x2key(ccn, blk):
            return ("x2", ccn, blk)

        def h2sl(ccn, blk):
            return h2_sb[:, ccn * TC + blk * PBLK: ccn * TC + (blk + 1) * PBLK]

        if sub4 < 1:
            pb.dma("sp", out_d[0:128, 0:1024], xst_sb[:, 0:1024],
                   reads=[("xst",)], writes=[("outz2",)])
            pb.op("sp", lambda e: e.nop(), reads=[("outz2",)], writes=())
            pb.finalize()
            _emit(nc, pb)
            return nc
        if sub4 < 2:
            pb.dma("sp", out_d[0:128, 0:1024], xst_sb[:, 0:1024],
                   reads=[("xst",)], writes=[("outz2",)])
            pb.op("sp", lambda e: e.nop(), reads=[("outz2",)], writes=())
            pb.finalize()
            _emit(nc, pb)
            return nc
        v_red("dve", hmax1_sb[:, :], hmax_sb[:, :], ALU.max,
              reads=[("hmaxc", c2) for c2 in range(64)], writes=[("hmax1",)])
        if sub4 < 3:
            pb.dma("sp", out_d[0:128, 0:1024], xst_sb[:, 0:1024],
                   reads=[("xst",)], writes=[("outz2",)])
            pb.op("sp", lambda e: e.nop(), reads=[("outz2",)], writes=())
            pb.finalize()
            _emit(nc, pb)
            return nc
        t_mm(bank(4), PS[4][0:1, 0:128], hmax1_sb[:, 0:1], idf_sb[:, :],
             reads=[("hmax1",), ("c", "ident_f32")], start=True, stop=True, is_tr=True)
        v_red("dve", sA_sb[:, :], PS[4][0:1, 0:128], ALU.max,
              reads=[bank(4)], writes=[("sA",)])
        if sub4 < 4:
            pb.dma("sp", out_d[0:128, 0:1024], xst_sb[:, 0:1024],
                   reads=[("xst",)], writes=[("outz2",)])
            pb.op("sp", lambda e: e.nop(), reads=[("outz2",)], writes=())
            pb.finalize()
            _emit(nc, pb)
            return nc
        v_memset("dve", sc16_sb[:, :], 0.0, writes=[("sc16",)])
        v_copy("dve", sc16_sb[0:1, :], sA_sb[:, :], reads=[("sA",), ("sc16",)],
               writes=[("sc16",)])
        pb.dma("pool", mx1_in.ap(), sc16_sb[:, :], reads=[("sc16",)],
               writes=[("mx1in",)])
        pb.cc("ccmax1", "AllReduce", ALU.max, [list(range(8))],
              mx1_in.ap().opt(), mx1_out.ap().opt(),
              reads=[("mx1in",)], writes=[("mx1out",)])
        pb.dma("pool", sc16b_sb[:, :], mx1_out.ap(), reads=[("mx1out",), ("sc16b",)],
               writes=[("sc16b",)])
        a_act(sA_sb[:, :], clipx_sb[:, :], AF.Sigmoid,
              reads=[("c", "clipx"), ("sA",)], writes=[("sA",)])
        v_tt("dve", sA_sb[:, :], sA_sb[:, :], sc16b_sb[0:1, 0:1], ALU.mult,
             reads=[("sA",), ("sc16b",)], writes=[("sA",)])
        v_ts("dve", sA_sb[:, :], sA_sb[:, :], 1.0 / QMAX, None, ALU.mult, None,
             reads=[("sA",)], writes=[("sA",)])
        v_recip("dve", sB_sb[:, :], sA_sb[:, :], reads=[("sA",), ("sB",)],
                writes=[("sB",)])
        def bcast_scalar(src_key, src_ap, dst_ap, dst_key, bcol):
            t_mm(bank(5), PS[5][:, bcol:bcol + 1], onerf_sb[:, :], src_ap,
                 reads=[("c", "one_row_f32"), src_key], start=True, stop=True)
            v_copy("dve", dst_ap, PS[5][:, bcol:bcol + 1], reads=[bank(5), dst_key],
                   writes=[dst_key])

        if sub4 < 5:
            pb.dma("sp", out_d[0:128, 0:1024], xst_sb[:, 0:1024],
                   reads=[("xst",)], writes=[("outz2",)])
            pb.op("sp", lambda e: e.nop(), reads=[("outz2",)], writes=())
            pb.finalize()
            _emit(nc, pb)
            return nc
        bcast_scalar(("sB",), sB_sb[:, :], rB_sb[:, :], ("rB",), 0)
        bcast_scalar(("sA",), sA_sb[:, :], qsxB_sb[:, :], ("qsxB",), 1)
        if sub4 < 6:
            pb.dma("sp", out_d[0:128, 0:1024], xst_sb[:, 0:1024],
                   reads=[("xst",)], writes=[("outz2",)])
            pb.op("sp", lambda e: e.nop(), reads=[("outz2",)], writes=())
            pb.finalize()
            _emit(nc, pb)
            return nc
        for blk in range(NPB):
            for ccn in range(NCC):
                hs = h2sl(ccn, blk)
                if ccn % 2 == 0:
                    bq = 4 + (blk % 2)
                    a_act(PS[bq][:, 0:PBLK], hs, AF.Copy,
                          reads=[("h2", ccn, blk), ("rB",), ("magic",), bank(bq)],
                          writes=[bank(bq)], scale=rB_sb[:, 0:1], bias=RMAGIC)
                    a_act(hs, PS[bq][:, 0:PBLK], AF.Copy,
                          reads=[bank(bq), ("magic",), ("h2", ccn, blk)],
                          writes=[("h2", ccn, blk)], bias=-RMAGIC)
                    v_ts("dve", hs, hs, QMAX, -QMAX - 1.0, ALU.min, ALU.max,
                         reads=[("h2", ccn, blk)], writes=[("h2", ccn, blk)])
                else:
                    bq = 2 + (blk % 2)
                    v_ts("dve", PS[bq][:, 0:PBLK], hs, rB_sb[:, 0:1], RMAGIC,
                         ALU.mult, ALU.add,
                         reads=[("h2", ccn, blk), ("rB",), bank(bq)],
                         writes=[bank(bq)])
                    v_ts("dve", hs, PS[bq][:, 0:PBLK], RMAGIC, QMAX,
                         ALU.subtract, ALU.min,
                         reads=[bank(bq), ("h2", ccn, blk)],
                         writes=[("h2", ccn, blk)])
                    v_ts("dve", hs, hs, -QMAX - 1.0, None, ALU.max, None,
                         reads=[("h2", ccn, blk)], writes=[("h2", ccn, blk)])

        for i in range(4, NPK):
            quant_pk(i, mq=True, rb0=4, dbuf=True)

        if "xq" in dbg:
            pb.dma("pool", dbg["xq"][:, :], h2_sb[:, :],
                   reads=[("h2", ccn, blk) for ccn in range(NCC) for blk in range(NPB)],
                   writes=[("dbgxq",)])
            out_keys.append(("dbgxq",))

    if phase_limit >= 4:

        v_ts("dve", sgv_sb[:, 0:4], sgv_sb[:, 0:4], qsxB_sb[:, 0:1], None,
             ALU.mult, None, reads=[("qsxB",)] + [("sg", i) for i in range(4)],
             writes=[("sg", i) for i in range(4)])
        v_ts("dve", suv_sb[:, 0:4], suv_sb[:, 0:4], qsxB_sb[:, 0:1], None,
             ALU.mult, None, reads=[("qsxB",)] + [("su", i) for i in range(4)],
             writes=[("su", i) for i in range(4)])
        def jit_load(ich):
            quant_load("wg_nat", ich, wdn_sb, ("wnat",), "sp")
            quant_load("wu_nat", ich, wdq1_sb, ("wq1",), "act")

        def jit_chain(ich):
            quant_gu("wg_nat", ich, wig_sb[ich % 2], ("wig", ich % 2),
                     sgv_sb, "sg", True,
                     wdn_sb, ("wnat",), wmx_sb, ("wmx",), wri_sb, ("wri",),
                     wsc_sb, ("wsc",), "sp", 4, "sp", preloaded=True)
            quant_gu("wu_nat", ich, wiu_sb[ich % 2], ("wiu", ich % 2),
                     suv_sb, "su", True,
                     wdq1_sb, ("wq1",), wmxu_sb, ("wmxu",), wriu_sb, ("wriu",),
                     wscu_sb, ("wscu",), "act", 4, "sp", preloaded=True)

        for ich in range(NI):
            if ich < NPK:
                wgt, gkey = pslot(ich, 0), ("pk", ich, 0)
                wut, ukey = pslot(ich, 1), ("pk", ich, 1)
            else:
                wgt, gkey = wig_sb[ich % 2], ("wig", ich % 2)
                wut, ukey = wiu_sb[ich % 2], ("wiu", ich % 2)
            if ich == NPK - 2:
                jit_load(NPK)
            if NPK <= ich + 1 < NI:
                jit_chain(ich + 1)
            if NPK <= ich + 2 < NI:
                jit_load(ich + 2)
            if ich in (1, 3, 5, 7):
                prep_wd((ich - 1) // 2)
            for blk in range(NFB):
                bkg = bank((2 * blk) % 4)
                bku = bank((2 * blk + 1) % 4)
                for ccn in range(NCC):
                    xqs = h2_sb[:, ccn * TC + blk * FBLK: ccn * TC + (blk + 1) * FBLK]
                    t_mm(bkg, PS[(2 * blk) % 4][:, 0:FBLK],
                         wgt[:, ccn * 128:(ccn + 1) * 128], xqs,
                         reads=[gkey + (ccn,), ("h2", ccn, 2 * blk),
                                ("h2", ccn, 2 * blk + 1)],
                         start=(ccn == 0), stop=(ccn == 7))
                for ccn in range(NCC):
                    xqs = h2_sb[:, ccn * TC + blk * FBLK: ccn * TC + (blk + 1) * FBLK]
                    t_mm(bku, PS[(2 * blk + 1) % 4][:, 0:FBLK],
                         wut[:, ccn * 128:(ccn + 1) * 128], xqs,
                         reads=[ukey + (ccn,), ("h2", ccn, 2 * blk),
                                ("h2", ccn, 2 * blk + 1)],
                         start=(ccn == 0), stop=(ccn == 7))
                gsl = gsl_sb[blk % 2]
                a_act(gsl[:, :], PS[(2 * blk) % 4][:, 0:FBLK], AF.Silu,
                      reads=[bkg, ("sg", ich), ("gsl", blk % 2)],
                      writes=[("gsl", blk % 2)], scale=sgv_sb[:, ich:ich + 1])
                ss = s_sb[:, ich * TC + blk * FBLK: ich * TC + (blk + 1) * FBLK]
                v_tt("dve", ss, PS[(2 * blk + 1) % 4][:, 0:FBLK], gsl[:, :], ALU.mult,
                     reads=[bku, ("gsl", blk % 2)], writes=[("s", ich, blk)])
            v_red("dve", hmax1_sb[:, :], s_sb[:, ich * TC:(ich + 1) * TC], ALU.max,
                  reads=[("s", ich, b2) for b2 in range(NFB)] + [("hmax1",)],
                  writes=[("hmax1",)], absval=True)
            v_tt("dve", smax_sb[:, ich:ich + 1], hmax1_sb[:, :],
                 suv_sb[:, ich:ich + 1], ALU.mult,
                 reads=[("hmax1",), ("su", ich)], writes=[("smc", ich)])

        v_red("dve", smax1_sb[:, 0:1], smax_sb[:, 0:NI], ALU.max,
              reads=[("smc", c2) for c2 in range(NI)], writes=[("smax1",)])
        t_mm(bank(4), PS[4][0:1, 0:128], smax1_sb[:, 0:1], idf_sb[:, :],
             reads=[("smax1",), ("c", "ident_f32")], start=True, stop=True, is_tr=True)
        v_red("dve", sA_sb[:, :], PS[4][0:1, 0:128], ALU.max,
              reads=[bank(4), ("sA",)], writes=[("sA",)])
        v_copy("dve", sc16_sb[0:1, :], sA_sb[:, :], reads=[("sA",), ("sc16",)],
               writes=[("sc16",)])
        pb.dma("pool", mx2_in.ap(), sc16_sb[:, :], reads=[("sc16",)],
               writes=[("mx2in",)])
        pb.cc("ccmax2", "AllReduce", ALU.max, [list(range(8))],
              mx2_in.ap().opt(), mx2_out.ap().opt(),
              reads=[("mx2in",)], writes=[("mx2out",)])
        pb.dma("pool", sc16b_sb[:, :], mx2_out.ap(), reads=[("mx2out",), ("sc16b",)],
               writes=[("sc16b",)])
        a_act(sA_sb[:, :], clipd_sb[:, :], AF.Sigmoid,
              reads=[("c", "clipd"), ("sA",)], writes=[("sA",)])
        v_tt("dve", sA_sb[:, :], sA_sb[:, :], sc16b_sb[0:1, 0:1], ALU.mult,
             reads=[("sA",), ("sc16b",)], writes=[("sA",)])
        v_ts("dve", sA_sb[:, :], sA_sb[:, :], 1.0 / QMAX, None, ALU.mult, None,
             reads=[("sA",)], writes=[("sA",)])
        v_recip("dve", sB_sb[:, :], sA_sb[:, :], reads=[("sA",), ("sB",)],
                writes=[("sB",)])
        bcast_scalar(("sB",), sB_sb[:, :], rs2B_sb[:, :], ("rs2B",), 2)
        bcast_scalar(("sA",), sA_sb[:, :], qsdB_sb[:, :], ("qsdB",), 3)

        def quant_s_half(hf2, ichs=None):
            for ich in (range(NI) if ichs is None else ichs):
                v_tt("dve", sqc_sb[:, :], suv_sb[:, ich:ich + 1], rs2B_sb[:, 0:1],
                     ALU.mult, reads=[("su", ich), ("rs2B",), ("sqc",)],
                     writes=[("sqc",)])
                srd = [("s", ich, 2 * hf2), ("s", ich, 2 * hf2 + 1)]
                for c2 in range(2):
                    ss = s_sb[:, ich * TC + hf2 * 1024 + c2 * 512:
                              ich * TC + hf2 * 1024 + (c2 + 1) * 512]
                    bq = 4 + c2
                    a_act(PS[bq][:, 0:512], ss, AF.Copy,
                          reads=srd + [("sqc",), ("magic",), bank(bq)],
                          writes=[bank(bq)], scale=sqc_sb[:, 0:1], bias=RMAGIC)
                    a_act(ss, PS[bq][:, 0:512], AF.Copy,
                          reads=[bank(bq), ("magic",)] + srd,
                          writes=srd, bias=-RMAGIC)
                ssf = s_sb[:, ich * TC + hf2 * 1024: ich * TC + (hf2 + 1) * 1024]
                v_ts("dve", ssf, ssf, QMAX, -QMAX - 1.0, ALU.min, ALU.max,
                     reads=srd, writes=srd)

        quant_s_half(0)

        if "s" in dbg:
            quant_s_half(1)
            pb.dma("pool", dbg["s"][:, :], s_sb[:, :],
                   reads=[("s", ich, b2) for ich in range(NI) for b2 in range(NFB)],
                   writes=[("dbgs",)])
            out_keys.append(("dbgs",))

    if phase_limit >= 5:
        def down_pass(p):
            for tb in range(NFB):
                if p == 0 and tb < 2 and "s" not in dbg:
                    quant_s_half(1, range(11 * tb, 11 * (tb + 1)))
                if p in (1, 2) and tb == 1:
                    prep_wd(2 * (p + 1))
                if p in (1, 2) and tb == 3:
                    prep_wd(2 * (p + 1) + 1)
                for k in range(2):
                    ccn = 2 * p + k
                    g = 2 * tb + k
                    bk = bank(g % 4)
                    psd = PS[g % 4]
                    for ich in range(NI):
                        t_mm(bk, psd[:, 0:FBLK],
                             wdQ_sb[p % 2][:, ich * 256 + k * 128:
                                           ich * 256 + k * 128 + 128],
                             s_sb[:, ich * TC + tb * FBLK: ich * TC + (tb + 1) * FBLK],
                             reads=[("s", ich, tb), ("wdQ", p % 2, ich, k)],
                             start=(ich == 0), stop=(ich == NI - 1))
                    ob = outb_sb[:, (g % 2) * FBLK:(g % 2) * FBLK + FBLK]
                    v_stt("dve", ob, psd[:, 0:FBLK], qsdB_sb[:, 0:1],
                          x_sb[:, ccn * TC + tb * FBLK: ccn * TC + (tb + 1) * FBLK],
                          ALU.mult, ALU.add,
                          reads=[bk, ("qsdB",), ("x2", ccn, 2 * tb),
                                 ("x2", ccn, 2 * tb + 1), ("outb", g % 2)],
                          writes=[("outb", g % 2)])
                    pb.dma("sp",
                           out_d[ccn * 128:(ccn + 1) * 128, tb * FBLK:(tb + 1) * FBLK],
                           ob, reads=[("outb", g % 2)], writes=[("outd", p, g)])
                    out_keys.append(("outd", p, g))

        for p in range(4):
            down_pass(p)
    else:
        pb.dma("sp", out_d[0:128, 0:1024], xst_sb[:, 0:1024],
               reads=[("xst",)], writes=[("outz",)])
        out_keys.append(("outz",))

    pb.op("sp", lambda e: e.nop(), reads=list(out_keys), writes=())

    pb.finalize()
    _emit(nc, pb)
    return nc


def _emit(nc, pb):
    import contextlib
    stack = contextlib.ExitStack()
    sems = {}
    for s in sorted(pb.sem_names):
        sems[s] = stack.enter_context(nc.semaphore(s))
    by_eng = {e: [] for e in ENGS}
    for o2 in pb.ops:
        by_eng[o2.eng].append(o2)

    def emit_stream(eng_obj, ops):
        for o2 in ops:
            for s, v in o2.waits:
                eng_obj.wait_ge(sems[s], v)
            inst = o2.fn(eng_obj)
            if o2.token is not None and inst is not None:
                inst.then_inc(sems[o2.token[0]], o2.amt)

    with nc.Block() as blk:
        @blk.tensor
        def _(e):
            emit_stream(e, by_eng["pe"])

        @blk.scalar
        def _(e):
            emit_stream(e, by_eng["act"])

        @blk.vector
        def _(e):
            emit_stream(e, by_eng["dve"])

        @blk.gpsimd
        def _(e):
            emit_stream(e, by_eng["pool"])

        @blk.sync
        def _(e):
            emit_stream(e, by_eng["sp"])
    stack.close()


_CACHED = {}

def kernel(**inputs):
    in_maps = _host_prepare(inputs)
    key = (tuple(DEBUG_TAPS), PHASE_LIMIT, SUB2, SUB3, SUB4)
    if key not in _CACHED:
        _CACHED[key] = build_program(DEBUG_TAPS, PHASE_LIMIT, SUB2, SUB3, SUB4)
    nc = _CACHED[key]
    res = run_bass_kernel_spmd(nc, in_maps, core_ids=list(range(8)))
    out = np.zeros((B, T, C), dtype=np.float32)
    for core in range(8):
        b, half = core // 2, core % 2
        out[b, half * TC:(half + 1) * TC, :] = res.results[core]["out"].T
    kernel.last_results = res
    return out


# version 14 (final)
# speedup vs baseline: 1.0490x; 1.0199x over previous
import math
import numpy as np
import ml_dtypes

import concourse.bass as bass
import concourse.mybir as mybir
from concourse.bass_utils import run_bass_kernel_spmd

F32 = mybir.dt.float32
F32R = mybir.dt.float32r
BF16 = mybir.dt.bfloat16
AF = mybir.ActivationFunctionType
ALU = mybir.AluOpType
AXX = mybir.AxisListType.X

B, T, C, H, DH = 4, 4096, 1024, 16, 64
TC = 2048
NCC = 8
PBLK = 256
NPB = TC // PBLK
RCH = 128
NRC = TC // RCH
FBLK = 512
NFB = TC // FBLK
IF = 2816
NI = IF // 128
QMAX = 127.0
EPS = 1e-5
RMAGIC = 1.5 * 2.0 ** 23

_glo = math.log(1.0 - 0.999)
_ghi = math.log(1.0 - 0.9)
GAMMAS = [1.0 - math.exp(_glo + (_ghi - _glo) * h / (H - 1)) for h in range(H)]

DEBUG_TAPS = ()
PHASE_LIMIT = 99
SUB2 = 9
SUB3 = 3
SUB4 = 9

ENGS = ("pe", "act", "dve", "pool", "sp")


class _Op:
    __slots__ = ("eng", "fn", "sem", "amt", "needs_inc", "token", "waits", "gi",
                 "kind", "deps")

    def __init__(self, eng, fn, sem, amt, gi, kind):
        self.eng = eng
        self.fn = fn
        self.sem = sem
        self.amt = amt
        self.needs_inc = False
        self.token = None
        self.waits = ()
        self.gi = gi
        self.kind = kind
        self.deps = set()


class PB:
    NDMA = 4

    def __init__(self, nc):
        self.nc = nc
        self.ops = []
        self.kw = {}
        self.kr = {}
        self.pending = {eng: set() for eng in ENGS}
        self.sem_names = set()
        self.dma_idx = {}
        self.dma_last = {}

    def _dep_add(self, deps, op):
        if op is None:
            return
        lane = (op.eng, op.sem)
        cur = deps.get(lane)
        if cur is None or op.gi > cur.gi:
            deps[lane] = op

    def op(self, eng, fn, reads=(), writes=(), sem=None, amt=1, kind="c"):
        sem = sem or (("q" + eng) if kind == "dma" else ("e" + eng))
        o = _Op(eng, fn, sem, amt, len(self.ops), kind)
        deps = {}
        for k in reads:
            self._dep_add(deps, self.kw.get(k))
            if isinstance(k, tuple) and k and k[0] == "PS":
                for r in self.kr.get(k, {}).values():
                    if r.eng != eng:
                        self._dep_add(deps, r)
        for k in writes:
            self._dep_add(deps, self.kw.get(k))
            for r in self.kr.get(k, {}).values():
                self._dep_add(deps, r)
        for p in self.pending[eng]:
            self._dep_add(deps, p)
        self.pending[eng] = set()
        o.deps = set(deps.values())
        for k in reads:
            self.kr.setdefault(k, {})[(o.eng, o.sem)] = o
        for k in writes:
            self.kw[k] = o
            self.kr[k] = {}
        self.ops.append(o)
        return o

    def dma(self, queue, out_ap, in_ap, reads=(), writes=(), slow=False):
        def fn(e, o=out_ap, i=in_ap, s=slow):
            if s:
                return e.dma_start(out=o, in_=i, allow_slow_non_contiguous=True)
            return e.dma_start(out=o, in_=i)
        idx = self.dma_idx.get(queue, 0)
        self.dma_idx[queue] = idx + 1
        rot = idx % self.NDMA
        sem = f"q{queue}{rot}"
        o = self.op(queue, fn, reads=reads, writes=writes, sem=sem, amt=16,
                    kind="dma")
        prev = self.dma_last.get((queue, rot))
        if prev is not None:
            o.deps.add(prev)
        self.dma_last[(queue, rot)] = o
        return o

    def dma_tr(self, queue, out_ap, in_ap, reads=(), writes=()):
        def fn(e, o=out_ap, i=in_ap):
            return e.dma_start_transpose(out=o, in_=i)
        idx = self.dma_idx.get(queue, 0)
        self.dma_idx[queue] = idx + 1
        rot = idx % self.NDMA
        sem = f"q{queue}{rot}"
        o = self.op(queue, fn, reads=reads, writes=writes, sem=sem, amt=16,
                    kind="dma")
        prev = self.dma_last.get((queue, rot))
        if prev is not None:
            o.deps.add(prev)
        self.dma_last[(queue, rot)] = o
        return o

    def cc(self, ccname, kindstr, aluop, groups, in_ap, out_ap, reads, writes):
        def fn(e, ks=kindstr, op=aluop, g=groups, i=in_ap, o=out_ap):
            return e.collective_compute(ks, op, replica_groups=g, ins=[i], outs=[o])
        self.sem_names.add(ccname)
        return self.op("pool", fn, reads=reads, writes=writes, sem=ccname, amt=1,
                       kind="cc")

    def mm(self, bank, fn, reads, start, stop):
        o = _Op("pe", fn, "epe", 1, len(self.ops), "c")
        deps = {}
        for k in reads:
            self._dep_add(deps, self.kw.get(k))
        if start:
            for r in self.kr.get(bank, {}).values():
                if r.eng != "pe":
                    self._dep_add(deps, r)
            w = self.kw.get(bank)
            if w is not None and w.eng != "pe":
                self._dep_add(deps, w)
        for p in self.pending["pe"]:
            self._dep_add(deps, p)
        self.pending["pe"] = set()
        o.deps = set(deps.values())
        for k in reads:
            self.kr.setdefault(k, {})[(o.eng, o.sem)] = o
        if start:
            self.kr[bank] = {}
        if stop:
            self.kw[bank] = o
            self.kr[bank] = {}
        self.ops.append(o)
        return o

    def barrier(self):
        dmas = list(self.dma_last.values())
        bar_ops = []
        for eng in ENGS:
            o = self.op(eng, lambda e: e.drain())
            for p in dmas:
                o.deps.add(p)
            bar_ops.append(o)
        for eng in ENGS:
            self.pending[eng] = set(bar_ops)

    def finalize(self):
        for o in self.ops:
            for dd in o.deps:
                dd.needs_inc = True
        for o in self.ops:
            if o.kind in ("dma", "cc"):
                o.needs_inc = True
        counts = {}
        for o in self.ops:
            if o.needs_inc:
                counts[o.sem] = counts.get(o.sem, 0) + o.amt
                o.token = (o.sem, counts[o.sem])
        marks = {eng: {} for eng in ENGS}
        for o in self.ops:
            w = {}
            m = marks[o.eng]
            for dd in o.deps:
                s, v = dd.token
                if m.get(s, 0) < v and w.get(s, 0) < v:
                    w[s] = v
            for s, v in w.items():
                m[s] = v
            o.waits = sorted(w.items(), key=str)
        self.sem_names.update(counts.keys())
        return counts


def _consts():
    gam = np.array(GAMMAS, dtype=np.float64)
    u = np.arange(RCH)
    tt = np.arange(RCH)
    diff = tt[None, :] - u[:, None]
    dm = np.zeros((RCH, H * RCH), dtype=np.float64)
    for h in range(H):
        dm[:, h * RCH:(h + 1) * RCH] = np.where(
            diff >= 0, gam[h] ** np.maximum(diff, 0), 0.0)
    dm = np.asarray(dm.astype(ml_dtypes.bfloat16))
    gq2 = np.zeros((DH, H * RCH), dtype=np.float64)
    for h in range(H):
        gq2[:, h * RCH:(h + 1) * RCH] = gam[h] ** (tt[None, :] + 1)
    gk = np.zeros((RCH, C), dtype=np.float64)
    for h in range(H):
        gk[:, h * DH:(h + 1) * DH] = (gam[h] ** (RCH - 1 - u))[:, None]
    gk = np.asarray(gk.astype(ml_dtypes.bfloat16))
    gdec = np.zeros((DH, H * DH), dtype=np.float32)
    for h in range(H):
        gdec[:, h * DH:(h + 1) * DH] = gam[h] ** RCH
    selT = np.zeros((2, 128), dtype=np.float32)
    selT[0, :64] = 1.0
    selT[1, 64:] = 1.0
    return dict(
        dmask=dm,
        gq2=np.asarray(gq2.astype(ml_dtypes.bfloat16)),
        gk=gk, gdec=gdec,
        ident_bf=np.asarray(np.eye(128, dtype=ml_dtypes.bfloat16)),
        ident_f32=np.eye(128, dtype=np.float32),
        ones_col_bf=np.asarray(np.ones((128, 1), dtype=ml_dtypes.bfloat16)),
        one_row_bf=np.asarray(np.ones((1, 128), dtype=ml_dtypes.bfloat16)),
        one_row_f32=np.ones((1, 128), dtype=np.float32),
        one_row256=np.asarray(np.ones((1, 256), dtype=ml_dtypes.bfloat16)),
        selT=selT)


def _wl(wT):
    return np.ascontiguousarray(
        wT.reshape(8, 128, 1024).transpose(1, 0, 2).reshape(128, 8192))


def _host_prepare(inputs):
    x = np.asarray(inputs["L0_attn_x"], dtype=np.float32)
    con = _consts()
    shared = {
        "wq_l": _wl(np.asarray(inputs["L0_attn_wq"], np.float32).T),
        "wk_l": _wl(np.asarray(inputs["L0_attn_wk"], np.float32).T),
        "wv_l": _wl(np.asarray(inputs["L0_attn_wv"], np.float32).T),
        "wo_l": _wl(np.asarray(inputs["L0_attn_wo"], np.float32).T),
        "wmod": np.asarray(inputs["L0_wmod"], np.float32),
        "wg_nat": np.asarray(inputs["L0_ffn_w_gate"], np.float32),
        "wu_nat": np.asarray(inputs["L0_ffn_w_up"], np.float32),
        "wd_nat": np.asarray(inputs["L0_ffn_w_down"], np.float32),
        "gnw_v": np.ascontiguousarray(
            np.asarray(inputs["L0_attn_gn_w"], np.float32).reshape(8, 128).T),
        "gnb_v": np.ascontiguousarray(
            np.asarray(inputs["L0_attn_gn_b"], np.float32).reshape(8, 128).T),
        "clipx": np.asarray(inputs["L0_ffn_clip_x"], np.float32).reshape(1, 1),
        "clipd": np.asarray(inputs["L0_ffn_clip_down"], np.float32).reshape(1, 1),
    }
    shared.update(con)
    cmat = np.asarray(inputs["L0_c"], np.float32)
    in_maps = []
    for core in range(8):
        b, half = core // 2, core % 2
        m = dict(shared)
        m["xT"] = np.ascontiguousarray(x[b, half * TC:(half + 1) * TC, :].T)
        m["cvec"] = np.ascontiguousarray(cmat[b].reshape(8, 128).T)
        m["smask_send"] = np.full((64, 1), 1.0 if half == 0 else 0.0, np.float32)
        m["smask_recv"] = np.full((64, 1), 1.0 if half == 1 else 0.0, np.float32)
        in_maps.append(m)
    return in_maps


def build_program(debug_taps=(), phase_limit=99, sub2=9, sub3=3, sub4=9):
    nc = bass.Bass("TRN2", target_bir_lowering=False)

    d = {}
    def din(name, shape, dt=F32):
        d[name] = nc.dram_tensor(name, list(shape), dt, kind="ExternalInput")
        return d[name]

    din("xT", (C, TC))
    for n in ("wq_l", "wk_l", "wv_l", "wo_l"):
        din(n, (128, 8192))
    din("wmod", (C, 6 * C), F32R)
    din("wg_nat", (IF, C)); din("wu_nat", (IF, C)); din("wd_nat", (C, IF))
    din("gnw_v", (128, 8)); din("gnb_v", (128, 8))
    din("clipx", (1, 1)); din("clipd", (1, 1))
    din("cvec", (128, 8))
    din("smask_send", (64, 1)); din("smask_recv", (64, 1))
    din("dmask", (128, H * RCH), BF16); din("gq2", (DH, H * RCH), BF16)
    din("gk", (RCH, C), BF16); din("gdec", (DH, H * DH))
    din("ident_bf", (128, 128), BF16); din("ident_f32", (128, 128))
    din("ones_col_bf", (128, 1), BF16); din("one_row_bf", (1, 128), BF16)
    din("one_row_f32", (1, 128)); din("one_row256", (1, 256), BF16)
    din("selT", (2, 128))

    out_d = nc.dram_tensor("out", [C, TC], F32, kind="ExternalOutput")
    dbg = {}
    for t in debug_taps:
        shp, dt = {"h": ([128, NCC * TC], BF16), "x2": ([128, NCC * TC], BF16),
                   "xq": ([128, NCC * TC], BF16), "s": ([128, NI * TC], BF16),
                   "mod": ([128, 48], F32), "state": ([DH, H * DH], F32),
                   "y": ([NPB, 128, NCC * PBLK], BF16)}[t]
        dbg[t] = nc.dram_tensor("dbg_" + t, shp, dt, kind="ExternalOutput")

    hd = nc.dram_tensor("hd", [NPB, 128, NCC * PBLK], BF16)
    vtmd = nc.dram_tensor("vtmd", [NRC, 128, C], BF16)
    ktmd = nc.dram_tensor("ktmd", [NRC, 128, C], BF16)
    yd = nc.dram_tensor("yd", [NPB, 128, NCC * PBLK], BF16)
    ccs_in = nc.dram_tensor("ccs_in", [DH, H * DH], F32)
    ccs_out = nc.dram_tensor("ccs_out", [DH, H * DH], F32)
    gn_in = nc.dram_tensor("gn_in", [32, 1], F32)
    gn_out = nc.dram_tensor("gn_out", [32, 1], F32)
    mx1_in = nc.dram_tensor("mx1_in", [16, 1], F32); mx1_out = nc.dram_tensor("mx1_out", [16, 1], F32)
    mx2_in = nc.dram_tensor("mx2_in", [16, 1], F32); mx2_out = nc.dram_tensor("mx2_out", [16, 1], F32)

    SB = {}
    def sb(name, shape, dt, off):
        SB[name] = nc.alloc_sbuf_tensor_at(name, list(shape), dt, offset=off)
        return SB[name]

    SBASE = 18 * 1024
    LIM = SBASE + 190 * 1024

    o = SBASE
    x_sb = sb("x", (128, NCC * TC), BF16, o); o += 32768
    idb_sb = sb("idb", (128, 128), BF16, o); o += 256
    idf_sb = sb("idf", (128, 128), F32, o); o += 512
    onec_sb = sb("onec", (128, 1), BF16, o); o += 32
    oner_sb = sb("oner", (1, 128), BF16, o); o += 256
    onerf_sb = sb("onerf", (1, 128), F32, o); o += 512
    selT_sb = sb("selTc", (2, 128), F32, o); o += 512
    cvec_sb = sb("cvecc", (128, 8), F32, o); o += 32
    silc_sb = sb("silc", (128, 8), F32R, o); o += 32
    gnw_sb = sb("gnwc", (128, 8), F32, o); o += 32
    gnb_sb = sb("gnbc", (128, 8), F32, o); o += 32
    mod_sb = sb("modv", (128, 48), F32, o); o += 192
    modt_sb = sb("modt", (128, 48), F32, o); o += 192
    gscr_sb = sb("gscr", (128, 24), F32, o); o += 96
    clipx_sb = sb("clipxs", (1, 1), F32, o); o += 32
    clipd_sb = sb("clipds", (1, 1), F32, o); o += 32
    smask_s_sb = sb("smask_s", (64, 1), F32, o); o += 32
    smask_r_sb = sb("smask_r", (64, 1), F32, o); o += 32
    gsum_sb = sb("gsum", (8, 2), F32, o); o += 32
    gsq_sb = sb("gsq", (8, 2), F32, o); o += 32
    gsum2_sb = sb("gsum2", (2, 8), F32, o); o += 32
    gsq2_sb = sb("gsq2", (2, 8), F32, o); o += 32
    ysum_sb = sb("ysum", (128, 64), F32, o); o += 256
    ysq_sb = sb("ysq", (128, 64), F32, o); o += 256
    ysum8_sb = sb("ysum8", (128, 8), F32, o); o += 32
    ysq8_sb = sb("ysq8", (128, 8), F32, o); o += 32
    Av_sb = sb("Av", (128, 8), F32, o); o += 32
    Bv_sb = sb("Bv", (128, 8), F32, o); o += 32
    mu8_sb = sb("mu8", (128, 8), F32, o); o += 32
    sq8_sb = sb("sq8", (128, 8), F32, o); o += 32
    hmax_sb = sb("hmax64", (128, 64), F32, o); o += 256
    hmax1_sb = sb("hmax1", (128, 1), F32, o); o += 32
    smax_sb = sb("smaxT", (128, NI * NFB), F32, o); o += 512
    smax1_sb = sb("smax1", (128, 1), F32, o); o += 32
    sc16_sb = sb("sc16", (16, 1), F32, o); o += 32
    sc16b_sb = sb("sc16b", (16, 1), F32, o); o += 32
    sA_sb = sb("sA", (1, 1), F32, o); o += 32
    sB_sb = sb("sB", (1, 1), F32, o); o += 32
    rbf_sb = sb("rbf", (1, 1), BF16, o); o += 32
    rB_sb = sb("rB", (128, 1), F32, o); o += 32
    qsxB_sb = sb("qsxB", (128, 1), F32, o); o += 32
    rs2B_sb = sb("rs2B", (128, 1), F32, o); o += 32
    qsdB_sb = sb("qsdB", (128, 1), F32, o); o += 32
    wmx_sb = sb("wmx", (128, 1), F32, o); o += 32
    wri_sb = sb("wri", (128, 1), F32, o); o += 32
    wsc_sb = sb("wsc", (128, 1), F32, o); o += 32
    sgv_sb = sb("sgv", (128, NI), F32, o); o += 128
    suv_sb = sb("suv", (128, NI), F32, o); o += 128
    sqc_sb = sb("sqc", (128, 1), F32, o); o += 32
    epsv_sb = sb("epsv", (128, 1), F32, o); o += 32
    magic_sb = sb("magicc", (128, 2), F32, o); o += 32
    wmxu_sb = sb("wmxu", (128, 1), F32, o); o += 32
    wriu_sb = sb("wriu", (128, 1), F32, o); o += 32
    wscu_sb = sb("wscu", (128, 1), F32, o); o += 32
    onr2_sb = sb("onr2", (1, 256), BF16, o); o += 512
    Bvb_sb = sb("Bvb", (128, 8), BF16, o); o += 32
    R0_END = o
    assert R0_END <= SBASE + 40 * 1024, R0_END

    o = R0_END
    mrow_sb = sb("mrow", (1, TC), BF16, o); o += 4096
    rrow_sb = sb("rrow", (1, TC), BF16, o); o += 4096
    mrr_sb = sb("mrr", (1, TC), BF16, o); o += 4096
    rows_sb = sb("rowst", (1, PBLK * 5), F32, o); o += 5120
    xsq_sb = [sb(f"xsq{i}", (128, PBLK), BF16, o + i * 512) for i in range(2)]; o += 1024
    bcb_sb = sb("bcb", (128, 2 * PBLK), BF16, o); o += 1024
    hsc_sb = [sb(f"hsc{i}", (128, PBLK), BF16, o + i * 512) for i in range(2)]; o += 1024
    wfr_sb = [sb(f"wfr{i}", (128, 512), F32R, o + i * 2048) for i in range(2)]; o += 4096
    xst_sb = sb("xst", (128, TC), F32, o); o += 8192
    wf_sb = [sb(f"wf{i}", (128, 1024), F32, o + i * 4096) for i in range(2)]; o += 8192
    yl_sb = [sb(f"yl{i}", (128, NCC * PBLK), BF16, o + i * 4096) for i in range(2)]; o += 8192
    yb_sb = [sb(f"yb{i}", (128, NCC * PBLK), BF16, o + i * 4096) for i in range(2)]; o += 8192
    WA_OFF = o
    wA_sb = sb("wA", (128, 8192), BF16, o); o += 16384
    wB_sb = sb("wB", (128, 8192), BF16, o); o += 16384
    dm_sb = sb("dm", (128, H * RCH), BF16, o); o += 4096
    gq2_sb = sb("gq2c", (DH, H * RCH), BF16, o); o += 4096
    gk_sb = sb("gkc", (RCH, C), BF16, o); o += 2048
    gdec_sb = sb("gdecc", (DH, H * DH), F32, o); o += 4096
    S_sb = sb("S", (DH, H * DH), F32, o); o += 4096
    Sr_sb = sb("Srecv", (DH, H * DH), F32, o); o += 4096
    Sbf_sb = sb("Sbf", (DH, H * DH), BF16, o); o += 2048
    h_sb = [sb(f"h{i}", (128, NCC * PBLK), BF16, o + i * 4096) for i in range(2)]; o += 8192
    ktm_sb = [sb(f"ktm{i}", (128, C), BF16, o + i * 2048) for i in range(2)]; o += 4096
    vtm_sb = [sb(f"vtm{i}", (128, C), BF16, o + i * 2048) for i in range(2)]; o += 4096
    stm_sb = [sb(f"stm{i}", (128, H * RCH), BF16, o + i * 4096) for i in range(2)]; o += 8192
    modrow_sb = sb("modrow", (1, 512), F32, o); o += 2048
    qt_sb = sb("qt", (DH, H * PBLK), BF16, o); o += 8192
    kt_sb = sb("kt", (DH, H * PBLK), BF16, o); o += 8192
    assert o <= 229375, ("attn", o)

    o = R0_END
    s_sb = sb("s", (128, NI * TC), BF16, o); o += NI * TC * 2
    OUTB_OFF = o
    wig_sb = [sb(f"wig{i}", (128, 1024), BF16, o + i * 2048) for i in range(2)]; o += 4096
    wiu_sb = [sb(f"wiu{i}", (128, 1024), BF16, o + i * 2048) for i in range(2)]; o += 4096
    gsl_sb = [sb(f"gsl{i}", (128, FBLK), BF16, o + i * 1024) for i in range(2)]; o += 2048
    wdn_sb = sb("wdnat", (128, IF // 2), F32, o); o += 5632
    wdq1_sb = sb("wdq1", (128, IF // 2), F32, o); o += 5632
    wdi_sb = sb("wdint", (128, IF // 2), BF16, o); o += 2816
    wnat_sb = wdn_sb
    wq1_sb = wdq1_sb
    wint_sb = wdi_sb
    wdQ_sb = [sb(f"wdq{i}", (128, NI * 256), BF16, o + i * 11264) for i in range(2)]
    o += 22528
    x2t_sb = [sb(f"x2t{i}", (128, 256), BF16, o + i * 512) for i in range(2)]; o += 1024
    nb1_sb = sb("nb1", (128, 1024), F32, R0_END + 16384)
    nb2_sb = sb("nb2", (128, 1024), F32, R0_END + 20480)
    wdmx_sb = sb("wdmx", (128, 1), F32, o); o += 32
    wdri_sb = sb("wdri", (128, 1), F32, o); o += 32
    wdsc_sb = sb("wdsc", (128, 1), F32, o); o += 32
    assert o <= 196576, ("ffn", o)
    outb_sb = sb("outb", (128, 1024), F32, OUTB_OFF)
    h2_sb = sb("h2", (128, NCC * TC), BF16, 196576)
    assert 196576 >= o, ("h2", o)

    PS = [nc.alloc_psum_tensor(f"pb{i}", [128, 512], F32) for i in range(6)]
    PSB = [nc.alloc_psum_tensor(f"pbb{i}", [128, 1024], BF16) for i in range(2)]
    def bank(i):
        return ("PS", i)
    def bankB(i):
        return ("PS", "B", i)

    pb = PB(nc)

    def t_mm(bk, out_ap, lhsT, rhs, reads, start, stop, is_tr=False):
        def fn(e, o2=out_ap, l=lhsT, r=rhs, s=start, t=stop, it=is_tr):
            return e.matmul(o2, l, r, start=s, stop=t, is_transpose=it)
        return pb.mm(bk, fn, reads, start, stop)

    def a_act(out_ap, in_ap, func, reads, writes, scale=1.0, bias=0.0, accum=None):
        def fn(e, o2=out_ap, i=in_ap, f=func, s=scale, b=bias, a=accum):
            if a is not None:
                return e.activation(o2, i, f, bias=b, scale=s, accum_out=a)
            return e.activation(o2, i, f, bias=b, scale=s)
        return pb.op("act", fn, reads=reads, writes=writes)

    def v_tt(eng, out_ap, a_ap, b_ap, aluop, reads, writes):
        def fn(e, o2=out_ap, a=a_ap, b=b_ap, op2=aluop):
            return e.tensor_tensor(o2, a, b, op2)
        return pb.op(eng, fn, reads=reads, writes=writes)

    def v_ts(eng, out_ap, in_ap, s1, s2, op0, op1, reads, writes):
        def fn(e, o2=out_ap, i=in_ap, a=s1, b=s2, p0=op0, p1=op1):
            if b is None:
                return e.tensor_single_scalar(o2, i, a, p0)
            return e.tensor_scalar(o2, i, a, b, p0, p1)
        return pb.op(eng, fn, reads=reads, writes=writes)

    def v_stt(eng, out_ap, in0, scal, in1, op0, op1, reads, writes):
        def fn(e, o2=out_ap, i0=in0, s=scal, i1=in1, p0=op0, p1=op1):
            return e.scalar_tensor_tensor(o2, i0, s, i1, p0, p1)
        return pb.op(eng, fn, reads=reads, writes=writes)

    def v_red(eng, out_ap, in_ap, aluop, reads, writes, absval=False):
        def fn(e, o2=out_ap, i=in_ap, op2=aluop, av=absval):
            return e.tensor_reduce(o2, i, AXX, op2, apply_absolute_value=av)
        return pb.op(eng, fn, reads=reads, writes=writes)

    def v_copy(eng, out_ap, in_ap, reads, writes):
        def fn(e, o2=out_ap, i=in_ap):
            return e.tensor_copy(o2, i)
        return pb.op(eng, fn, reads=reads, writes=writes)

    def v_recip(eng, out_ap, in_ap, reads, writes):
        def fn(e, o2=out_ap, i=in_ap):
            return e.reciprocal(o2, i)
        return pb.op(eng, fn, reads=reads, writes=writes)

    def v_memset(eng, ap, val, writes):
        def fn(e, a=ap, v=val):
            return e.memset(a, v)
        return pb.op(eng, fn, reads=(), writes=writes)

    out_keys = []

    v_memset("dve", epsv_sb[:, :], EPS, writes=[("epsv",)])
    v_memset("dve", magic_sb[:, 0:1], RMAGIC, writes=[("magic",)])
    v_memset("dve", magic_sb[:, 1:2], -RMAGIC, writes=[("magic",)])
    nc.const_aps.register(F32, RMAGIC, magic_sb[:, 0:1])
    nc.const_aps.register(F32, -RMAGIC, magic_sb[:, 1:2])
    for nme, dst in (("dmask", dm_sb), ("gq2", gq2_sb), ("gk", gk_sb),
                     ("gdec", gdec_sb), ("ident_bf", idb_sb), ("ident_f32", idf_sb),
                     ("ones_col_bf", onec_sb), ("one_row_bf", oner_sb),
                     ("one_row_f32", onerf_sb), ("one_row256", onr2_sb),
                     ("selT", selT_sb), ("cvec", cvec_sb),
                     ("gnw_v", gnw_sb), ("gnb_v", gnb_sb),
                     ("clipx", clipx_sb), ("clipd", clipd_sb),
                     ("smask_send", smask_s_sb), ("smask_recv", smask_r_sb)):
        pb.dma("pool", dst[:, :], d[nme][:, :], reads=(), writes=[("c", nme)])

    for ccn in range(NCC):
        pb.dma("sp", xst_sb[:, :], d["xT"][ccn * 128:(ccn + 1) * 128, :],
               reads=[("xst",)], writes=[("xst",)])
        v_copy("dve", x_sb[:, ccn * TC:(ccn + 1) * TC], xst_sb[:, :],
               reads=[("xst",)], writes=[("x", ccn)])

    def load_weight(slot, src_name, slotkey, q="sp"):
        for ccn in range(NCC):
            st = wf_sb[ccn % 2]
            pb.dma(q, st[:, :], d[src_name][:, ccn * 1024:(ccn + 1) * 1024],
                   reads=[("wf", ccn % 2)], writes=[("wf", ccn % 2)])
            v_copy("dve", slot[:, ccn * 1024:(ccn + 1) * 1024], st[:, :],
                   reads=[("wf", ccn % 2)], writes=[(slotkey, ccn)])

    a_act(silc_sb[:, :], cvec_sb[:, :], AF.Silu, reads=[("c", "cvec")],
          writes=[("silc",)])
    for i in range(12):
        for j in range(NCC):
            pb.dma("act", wfr_sb[(2 * i + j) % 2][:, :],
                   d["wmod"][j * 128:(j + 1) * 128, i * 512:(i + 1) * 512],
                   reads=[("wfr", (2 * i + j) % 2)],
                   writes=[("wfr", (2 * i + j) % 2)])
            t_mm(bank(5), PS[5][:1, 0:512],
                 silc_sb[:, j:j + 1],
                 wfr_sb[(2 * i + j) % 2][:, :],
                 reads=[("wfr", (2 * i + j) % 2), ("silc",)],
                 start=(j == 0), stop=(j == 7))
        v_copy("dve", modrow_sb[:1, 0:512], PS[5][:1, 0:512],
               reads=[bank(5), ("modrow",)], writes=[("modrow",)])
        for k in range(4):
            t_mm(bank(2), PS[2][:, i * 4 + k: i * 4 + k + 1],
                 modrow_sb[:1, k * 128:(k + 1) * 128], idf_sb[0:1, 0:1],
                 reads=[("modrow",), ("c", "ident_f32")],
                 start=(i == 0 and k == 0), stop=(i == 11 and k == 3), is_tr=True)
    v_copy("dve", mod_sb[:, :], PS[2][:, 0:48], reads=[bank(2)], writes=[("modv",)])
    v_copy("dve", modt_sb[:, 0:8], mod_sb[:, 0:8], reads=[("modv",)], writes=[("mt", 0)])
    v_ts("dve", modt_sb[:, 8:16], mod_sb[:, 8:16], 1.0, None, ALU.add, None,
         reads=[("modv",)], writes=[("mt", 1)])
    v_copy("dve", modt_sb[:, 24:32], mod_sb[:, 24:32], reads=[("modv",)], writes=[("mt", 3)])
    v_ts("dve", modt_sb[:, 32:40], mod_sb[:, 32:40], 1.0, None, ALU.add, None,
         reads=[("modv",)], writes=[("mt", 4)])
    for (c0, mi) in ((16, 2), (40, 5)):
        a_act(gscr_sb[:, 0:8], mod_sb[:, c0:c0 + 8], AF.Sign,
              reads=[("modv",), ("gscr",)], writes=[("gscr",)])
        v_tt("dve", gscr_sb[:, 8:16], gscr_sb[:, 0:8], gscr_sb[:, 0:8], ALU.mult,
             reads=[("gscr",)], writes=[("gscr",)])
        v_ts("dve", gscr_sb[:, 8:16], gscr_sb[:, 8:16], -1.0, 1.0, ALU.mult, ALU.add,
             reads=[("gscr",)], writes=[("gscr",)])
        v_tt("dve", gscr_sb[:, 8:16], gscr_sb[:, 0:8], gscr_sb[:, 8:16], ALU.add,
             reads=[("gscr",)], writes=[("gscr",)])
        a_act(gscr_sb[:, 16:24], mod_sb[:, c0:c0 + 8], AF.Abs,
              reads=[("modv",), ("gscr",)], writes=[("gscr",)])
        v_ts("dve", gscr_sb[:, 16:24], gscr_sb[:, 16:24], 0.25, None, ALU.max, None,
             reads=[("gscr",)], writes=[("gscr",)])
        v_tt("dve", modt_sb[:, c0:c0 + 8], gscr_sb[:, 8:16], gscr_sb[:, 16:24],
             ALU.mult, reads=[("gscr",)], writes=[("mt", mi)])

    load_weight(wA_sb, "wk_l", "wA")
    load_weight(wB_sb, "wv_l", "wB")

    sa_c = modt_sb[:, 0:8]; a1_c = modt_sb[:, 8:16]; ga_c = modt_sb[:, 16:24]
    sb_c = modt_sb[:, 24:32]; a2_c = modt_sb[:, 32:40]; gb_c = modt_sb[:, 40:48]
    MT = [("mt", i) for i in range(6)]

    if "mod" in dbg:
        pb.dma("pool", dbg["mod"][:, :], modt_sb[:, :], reads=MT, writes=[("dbgmod",)])
        out_keys.append(("dbgmod",))

    def ln_stats(src_keyf):
        for blk in range(NPB):
            ln_stats_block(blk, src_keyf)

    def ln_stats_block(blk, src_keyf):
        if True:
            t0 = blk * PBLK
            for ccn in range(NCC):
                xs = x_sb[:, ccn * TC + t0: ccn * TC + t0 + PBLK]
                t_mm(bank(4), PS[4][:1, 0:PBLK], onec_sb[:, :], xs,
                     reads=[("c", "ones_col_bf"), src_keyf(ccn, blk)],
                     start=(ccn == 0), stop=(ccn == 7))
            for ccn in range(NCC):
                xs = x_sb[:, ccn * TC + t0: ccn * TC + t0 + PBLK]
                sqt = xsq_sb[ccn % 2]
                a_act(sqt[:, :], xs, AF.Square,
                      reads=[src_keyf(ccn, blk), ("xsq", ccn % 2)],
                      writes=[("xsq", ccn % 2)])
                t_mm(bank(5), PS[5][:1, 0:PBLK], onec_sb[:, :], sqt[:, :],
                     reads=[("c", "ones_col_bf"), ("xsq", ccn % 2)],
                     start=(ccn == 0), stop=(ccn == 7))
            rw = rows_sb
            P = PBLK
            v_ts("dve", rw[:1, 0:P], PS[4][:1, 0:P], 1.0 / C, None, ALU.mult, None,
                 reads=[bank(4), ("rows",)], writes=[("rows",)])
            v_ts("dve", rw[:1, P:2 * P], PS[5][:1, 0:P], 1.0 / C, None, ALU.mult, None,
                 reads=[bank(5), ("rows",)], writes=[("rows",)])
            v_tt("dve", rw[:1, 2 * P:3 * P], rw[:1, 0:P], rw[:1, 0:P], ALU.mult,
                 reads=[("rows",)], writes=[("rows",)])
            v_tt("dve", rw[:1, 2 * P:3 * P], rw[:1, P:2 * P], rw[:1, 2 * P:3 * P],
                 ALU.subtract, reads=[("rows",)], writes=[("rows",)])
            a_act(rw[:1, 3 * P:4 * P], rw[:1, 2 * P:3 * P], AF.Sqrt,
                  reads=[("rows",), ("epsv",)], writes=[("rows",)],
                  bias=epsv_sb[0:1, 0:1])
            v_recip("dve", rw[:1, 4 * P:5 * P], rw[:1, 3 * P:4 * P],
                    reads=[("rows",)], writes=[("rows",)])
            v_copy("dve", mrow_sb[:1, t0:t0 + P], rw[:1, 0:P],
                   reads=[("rows",)], writes=[("mrow", blk)])
            v_copy("dve", rrow_sb[:1, t0:t0 + P], rw[:1, 4 * P:5 * P],
                   reads=[("rows",)], writes=[("rrow", blk)])
            v_tt("dve", mrr_sb[:1, t0:t0 + P], mrow_sb[:1, t0:t0 + P],
                 rrow_sb[:1, t0:t0 + P], ALU.mult,
                 reads=[("mrow", blk), ("rrow", blk), ("mrr", blk)],
                 writes=[("mrr", blk)])

    def bcast_rows(blk):
        t0 = blk * PBLK
        t_mm(bank(4), PS[4][:, 0:PBLK], oner_sb[:, :], rrow_sb[:1, t0:t0 + PBLK],
             reads=[("c", "one_row_bf"), ("rrow", blk)], start=True, stop=False)
        t_mm(bank(4), PS[4][:, PBLK:2 * PBLK], oner_sb[:, :], mrr_sb[:1, t0:t0 + PBLK],
             reads=[("c", "one_row_bf"), ("mrr", blk)], start=False, stop=True)
        a_act(bcb_sb[:, :], PS[4][:, 0:2 * PBLK], AF.Copy,
              reads=[bank(4), ("bcb",)], writes=[("bcb",)])

    def h_block(blk, srckeyf, dstf, dkeyf, a_vec, s_vec, mkeys):
        t0 = blk * PBLK
        bcast_rows(blk)
        for ccn in range(NCC):
            xs = x_sb[:, ccn * TC + t0: ccn * TC + t0 + PBLK]
            h1 = hsc_sb[ccn % 2]
            v_tt("dve", h1[:, :], xs, bcb_sb[:, 0:PBLK], ALU.mult,
                 reads=[srckeyf(ccn, blk), ("bcb",), ("hsc", ccn % 2)],
                 writes=[("hsc", ccn % 2)])
            v_tt("dve", h1[:, :], h1[:, :], bcb_sb[:, PBLK:2 * PBLK], ALU.subtract,
                 reads=[("bcb",), ("hsc", ccn % 2)],
                 writes=[("hsc", ccn % 2)])
            v_ts("dve", dstf(ccn, blk), h1[:, :], a_vec[:, ccn:ccn + 1],
                 s_vec[:, ccn:ccn + 1], ALU.mult, ALU.add,
                 reads=[("hsc", ccn % 2)] + mkeys + [dkeyf(ccn, blk)],
                 writes=[dkeyf(ccn, blk)])

    NPK = 14
    PSLOT_COL = [18, 19, 20, 21, 15, 16, 17, 8, 9, 10, 11, 12, 13, 14]

    def pslot(i, m):
        base = PSLOT_COL[i] * TC + m * 1024
        return s_sb[:, base:base + 1024]

    def quant_load(nat_name, ich, natb, natk, q):
        pb.dma(q, natb[:, 0:1024], d[nat_name][ich * 128:(ich + 1) * 128, :],
               reads=[natk], writes=[natk])

    def quant_gu(nat_name, ich, dstslot, dkey, svec, sveckey, mulq,
                 natb, natk, mx, mxk, ri, rik, sc, sck, q, rb0, qtr,
                 preloaded=False):
        if not preloaded:
            quant_load(nat_name, ich, natb, natk, q)
        v_red("dve", mx[:, :], natb[:, 0:1024], ALU.max,
              reads=[natk, mxk], writes=[mxk], absval=True)
        v_recip("dve", ri[:, :], mx[:, :], reads=[mxk, rik],
                writes=[rik])
        v_ts("dve", ri[:, :], ri[:, :], QMAX, None, ALU.mult, None,
             reads=[rik], writes=[rik])
        if mulq:
            v_ts("dve", sc[:, :], mx[:, :], 1.0 / QMAX, None, ALU.mult, None,
                 reads=[mxk, sck], writes=[sck])
            v_tt("dve", svec[:, ich:ich + 1], sc[:, :], qsxB_sb[:, 0:1], ALU.mult,
                 reads=[sck, ("qsxB",)], writes=[(sveckey, ich)])
        else:
            v_ts("dve", svec[:, ich:ich + 1], mx[:, :], 1.0 / QMAX, None,
                 ALU.mult, None, reads=[mxk], writes=[(sveckey, ich)])
        for c2 in range(2):
            bq = rb0 + c2
            a_act(PS[bq][:, 0:512], natb[:, c2 * 512:(c2 + 1) * 512], AF.Copy,
                  reads=[natk, rik, ("magic",), bank(bq)], writes=[bank(bq)],
                  scale=ri[:, 0:1], bias=RMAGIC)
            a_act(wint_sb[:, c2 * 512:(c2 + 1) * 512], PS[bq][:, 0:512], AF.Copy,
                  reads=[bank(bq), ("magic",), ("wint",)], writes=[("wint",)],
                  bias=-RMAGIC)
        pb.dma_tr(qtr, dstslot[:, :].rearrange("p (j c) -> p j c", j=NCC),
                  wint_sb[:, 0:1024].rearrange("p (j c) -> p j c", j=NCC),
                  reads=[("wint",)], writes=[dkey + (j,) for j in range(NCC)])

    def quant_pk(i, mq=False, rb0=2, dbuf=False):
        ng, nk = ((nb1_sb, ("nb1",)) if dbuf and i % 2 else (wdn_sb, ("wnat",)))
        nu, uk = ((nb2_sb, ("nb2",)) if dbuf and i % 2 else (wdq1_sb, ("wq1",)))
        quant_gu("wg_nat", i, pslot(i, 0), ("pk", i, 0), sgv_sb, "sg", mq,
                 ng, nk, wmx_sb, ("wmx",), wri_sb, ("wri",),
                 wsc_sb, ("wsc",), "sp", rb0, "sp")
        quant_gu("wu_nat", i, pslot(i, 1), ("pk", i, 1), suv_sb, "su", mq,
                 nu, uk, wmxu_sb, ("wmxu",), wriu_sb, ("wriu",),
                 wscu_sb, ("wscu",), "pool", rb0, "sp")

    def prep_wd(ccn, phase=None):
        q = ccn // 2
        if phase in (None, 0):
            pb.dma("sp", wdn_sb[:, :],
                   d["wd_nat"][ccn * 128:(ccn + 1) * 128, 0:IF // 2],
                   reads=[("wnat",)], writes=[("wnat",)])
            pb.dma("act", wdq1_sb[:, :],
                   d["wd_nat"][ccn * 128:(ccn + 1) * 128, IF // 2:IF],
                   reads=[("wq1",)], writes=[("wq1",)])
            v_red("dve", wdmx_sb[:, :], wdn_sb[:, :], ALU.max,
                  reads=[("wnat",), ("wdmx",)], writes=[("wdmx",)], absval=True)
            v_red("dve", wdsc_sb[:, :], wdq1_sb[:, :], ALU.max,
                  reads=[("wq1",), ("wdsc",)], writes=[("wdsc",)], absval=True)
            v_tt("dve", wdmx_sb[:, :], wdmx_sb[:, :], wdsc_sb[:, :], ALU.max,
                 reads=[("wdmx",), ("wdsc",)], writes=[("wdmx",)])
            v_recip("dve", wdri_sb[:, :], wdmx_sb[:, :], reads=[("wdmx",), ("wdri",)],
                    writes=[("wdri",)])
            v_ts("dve", wdri_sb[:, :], wdri_sb[:, :], QMAX, None, ALU.mult, None,
                 reads=[("wdri",)], writes=[("wdri",)])
            v_ts("dve", wdsc_sb[:, :], wdmx_sb[:, :], 1.0 / QMAX, None, ALU.mult,
                 None, reads=[("wdmx",), ("wdsc",)], writes=[("wdsc",)])
            v_tt("dve", wdsc_sb[:, :], wdsc_sb[:, :], gb_c[:, ccn:ccn + 1], ALU.mult,
                 reads=[("wdsc",), ("mt", 5)], writes=[("wdsc",)])
        for hf in (range(2) if phase is None else (phase,)):
            natb, natk = ((wdn_sb, ("wnat",)) if hf == 0
                          else (wdq1_sb, ("wq1",)))
            for c2 in range(3):
                w = 512 if c2 < 2 else IF // 2 - 1024
                bq = 4 + (c2 % 2)
                a_act(PS[bq][:, 0:w], natb[:, c2 * 512:c2 * 512 + w], AF.Copy,
                      reads=[natk, ("wdri",), ("magic",), bank(bq)],
                      writes=[bank(bq)], scale=wdri_sb[:, 0:1], bias=RMAGIC)
                a_act(wdi_sb[:, c2 * 512:c2 * 512 + w], PS[bq][:, 0:w], AF.Copy,
                      reads=[bank(bq), ("magic",), ("wint",)],
                      writes=[("wint",)], bias=-RMAGIC)
            v_ts("dve", wdi_sb[:, :], wdi_sb[:, :], wdsc_sb[:, 0:1], None,
                 ALU.mult, None, reads=[("wint",), ("wdsc",)],
                 writes=[("wint",)])
            nj = NI // 2
            c0 = (ccn % 2) * 128
            pb.dma_tr("sp",
                      wdQ_sb[q % 2][:, :].rearrange(
                          "p (j k) -> p j k", k=256)[:, hf * nj:(hf + 1) * nj,
                                                     c0:c0 + 128],
                      wdi_sb[:, :].rearrange("p (j c) -> p j c", j=nj),
                      reads=[("wint",)],
                      writes=[("wdQ", q % 2, hf * nj + jj, ccn % 2)
                              for jj in range(nj)])


    def xkey(ccn, blk):
        return ("x", ccn)
    ln_stats(xkey)

    def hsl(ccn, blk):
        return h_sb[blk % 2][:, ccn * PBLK:(ccn + 1) * PBLK]

    def hkey(ccn, blk):
        return ("h", blk % 2, ccn)

    v_memset("dve", S_sb[:, :], 0.0, writes=[("S",)])

    def spill_h(b2):
        pb.dma("sp", hd[b2, :, :], h_sb[b2 % 2][:, :],
               reads=[("h", b2 % 2, ccn) for ccn in range(NCC)],
               writes=[("hd", b2)])

    h_block(0, xkey, hsl, hkey, a1_c, sa_c, [("mt", 1), ("mt", 0)])
    spill_h(0)
    for blk in range(NPB):
        for tci in range(2):
            ch = blk * 2 + tci
            kbuf = ktm_sb[ch % 2]
            vbuf = vtm_sb[ch % 2]
            for half in range(2):
                bk = bank(half)
                for ccn in range(NCC):
                    t_mm(bk, PS[half][:, 0:512],
                         hsl(ccn, blk)[:, tci * 128:(tci + 1) * 128],
                         wA_sb[:, ccn * 1024 + half * 512: ccn * 1024 + (half + 1) * 512],
                         reads=[hkey(ccn, blk), ("wA", ccn)],
                         start=(ccn == 0), stop=(ccn == 7))
                v_tt("dve", kbuf[:, half * 512:(half + 1) * 512], PS[half][:, 0:512],
                     gk_sb[:, half * 512:(half + 1) * 512], ALU.mult,
                     reads=[bk, ("c", "gk"), ("ktm", ch % 2)],
                     writes=[("ktm", ch % 2)])
                bkv = bank(2 + half)
                for ccn in range(NCC):
                    t_mm(bkv, PS[2 + half][:, 0:512],
                         hsl(ccn, blk)[:, tci * 128:(tci + 1) * 128],
                         wB_sb[:, ccn * 1024 + half * 512: ccn * 1024 + (half + 1) * 512],
                         reads=[hkey(ccn, blk), ("wB", ccn)],
                         start=(ccn == 0), stop=(ccn == 7))
                a_act(vbuf[:, half * 512:(half + 1) * 512], PS[2 + half][:, 0:512],
                      AF.Copy, reads=[bkv, ("vtm", ch % 2)], writes=[("vtm", ch % 2)])
            pb.dma("act", ktmd[ch, :, :], kbuf[:, :],
                   reads=[("ktm", ch % 2)], writes=[("ktmd", ch)])
            pb.dma("act", vtmd[ch, :, :], vbuf[:, :],
                   reads=[("vtm", ch % 2)], writes=[("vtmd", ch)])
            for h in range(H):
                hb = 4 if h < 8 else 5
                t_mm(bank(hb), PS[hb][0:DH, (h % 8) * DH:(h % 8 + 1) * DH],
                     kbuf[:, h * DH:(h + 1) * DH], vbuf[:, h * DH:(h + 1) * DH],
                     reads=[("ktm", ch % 2), ("vtm", ch % 2)],
                     start=(h % 8 == 0), stop=(h % 8 == 7))
            v_tt("dve", Sr_sb[:, :], S_sb[:, :], gdec_sb[:, :], ALU.mult,
                 reads=[("S",), ("c", "gdec"), ("Srecv",)], writes=[("Srecv",)])
            v_tt("dve", S_sb[:, 0:512], Sr_sb[:, 0:512], PS[4][0:DH, 0:512], ALU.add,
                 reads=[("Srecv",), bank(4), ("S",)], writes=[("S",)])
            v_tt("dve", S_sb[:, 512:1024], Sr_sb[:, 512:1024], PS[5][0:DH, 0:512],
                 ALU.add, reads=[("Srecv",), bank(5), ("S",)], writes=[("S",)])
            if tci == 0 and blk + 1 < NPB:
                h_block(blk + 1, xkey, hsl, hkey, a1_c, sa_c, [("mt", 1), ("mt", 0)])
                spill_h(blk + 1)

    v_ts("dve", Sr_sb[:, :], S_sb[:, :], smask_s_sb[:, 0:1], None, ALU.mult, None,
         reads=[("S",), ("c", "smask_send"), ("Srecv",)], writes=[("Srecv",)])
    pb.dma("pool", ccs_in.ap(), Sr_sb[:, :], reads=[("Srecv",)], writes=[("ccsin",)])
    pb.cc("ccstate", "AllReduce", ALU.add, [[0, 1], [2, 3], [4, 5], [6, 7]],
          ccs_in.ap().opt(), ccs_out.ap().opt(),
          reads=[("ccsin",)], writes=[("ccsout",)])
    pb.dma("pool", Sr_sb[:, :], ccs_out.ap(), reads=[("ccsout",), ("Srecv",)],
           writes=[("Srecv",)])
    v_ts("dve", S_sb[:, :], Sr_sb[:, :], smask_r_sb[:, 0:1], None, ALU.mult, None,
         reads=[("Srecv",), ("c", "smask_recv"), ("S",)], writes=[("S",)])
    v_copy("dve", Sbf_sb[:, :], S_sb[:, :], reads=[("S",), ("Sbf",)],
           writes=[("Sbf",)])

    if "state" in dbg:
        pb.dma("pool", dbg["state"][:, :], S_sb[:, :], reads=[("S",)],
               writes=[("dbgS",)])
        out_keys.append(("dbgS",))

    if phase_limit >= 2:
        def fetch_kv(ch):
            pb.dma("act", ktm_sb[ch % 2][:, :], ktmd[ch, :, :],
                   reads=[("ktmd", ch), ("ktm", ch % 2)], writes=[("ktm", ch % 2)])
            pb.dma("act", vtm_sb[ch % 2][:, :], vtmd[ch, :, :],
                   reads=[("vtmd", ch), ("vtm", ch % 2)], writes=[("vtm", ch % 2)])

        def fetch_h(b2):
            pb.dma("sp", h_sb[b2 % 2][:, :], hd[b2, :, :],
                   reads=[("hd", b2)] + [("h", b2 % 2, ccn) for ccn in range(NCC)],
                   writes=[("h", b2 % 2, ccn) for ccn in range(NCC)])

        def proj(blk):
            for (slot, skey, dst, dkey) in ((wB_sb, "wB", qt_sb, "qt"),
                                            (wA_sb, "wA", kt_sb, "kt")):
                for ecn in range(NCC):
                    bk = bank(ecn % 2)
                    for ccn in range(NCC):
                        t_mm(bk, PS[ecn % 2][:, 0:PBLK],
                             slot[:, ccn * 1024 + ecn * 128: ccn * 1024 + (ecn + 1) * 128],
                             hsl(ccn, blk),
                             reads=[(skey, ccn), hkey(ccn, blk)],
                             start=(ccn == 0), stop=(ccn == 7))
                    for sub in range(2):
                        hh = 2 * ecn + sub
                        dks = ([("qt", hh, 0), ("qt", hh, 1)] if dkey == "qt"
                               else [("kt", hh)])
                        a_act(dst[:DH, hh * PBLK:(hh + 1) * PBLK],
                              PS[ecn % 2][sub * 64:(sub + 1) * 64, 0:PBLK], AF.Copy,
                              reads=[bk] + dks, writes=dks)

        def scores(ch):
            blk, tci = ch // 2, ch % 2
            smbuf = stm_sb[ch % 2]
            for g in range(4):
                bks = bank(2 + g % 2)
                for hi in range(4):
                    hh = 4 * g + hi
                    sl = slice(hh * PBLK + tci * 128, hh * PBLK + tci * 128 + 128)
                    t_mm(bks, PS[2 + g % 2][:, hi * 128:(hi + 1) * 128],
                         kt_sb[:DH, sl], qt_sb[:DH, sl],
                         reads=[("kt", hh), ("qt", hh, tci)],
                         start=(hi == 0), stop=(hi == 3))
                v_tt("dve", smbuf[:, g * 512:(g + 1) * 512],
                     PS[2 + g % 2][:, 0:512], dm_sb[:, g * 512:(g + 1) * 512],
                     ALU.mult,
                     reads=[bks, ("c", "dmask"), ("stm", ch % 2)],
                     writes=[("stm", ch % 2)])

        def rest(ch):
            blk, tci = ch // 2, ch % 2
            kbuf = ktm_sb[ch % 2]
            vbuf = vtm_sb[ch % 2]
            smbuf = stm_sb[ch % 2]
            if sub2 >= 4:
                for h in range(H):
                    hb = 0 if h < 8 else 1
                    t_mm(bank(hb), PS[hb][0:DH, (h % 8) * DH:(h % 8 + 1) * DH],
                         kbuf[:, h * DH:(h + 1) * DH], vbuf[:, h * DH:(h + 1) * DH],
                         reads=[("ktm", ch % 2), ("vtm", ch % 2)],
                         start=(h % 8 == 0), stop=(h % 8 == 7))
            qv = qt_sb[:DH, :].rearrange("p (h t) -> p h t", h=H)[:, :, tci * 128:(tci + 1) * 128]
            gv = gq2_sb[:DH, :].rearrange("p (h t) -> p h t", h=H)
            v_tt("dve", qv, qv, gv, ALU.mult,
                 reads=[("c", "gq2")] + [("qt", hh, tci) for hh in range(H)],
                 writes=[("qt", hh, tci) for hh in range(H)])
            for g in range(4 if sub2 >= 3 else 0):
                bky = bank(4 + g % 2)
                for hi in range(4):
                    hh = 4 * g + hi
                    t_mm(bky, PS[4 + g % 2][0:DH, hi * 128:(hi + 1) * 128],
                         vbuf[:, hh * DH:(hh + 1) * DH],
                         smbuf[:, (g * 4 + hi) * 128:(g * 4 + hi + 1) * 128],
                         reads=[("vtm", ch % 2), ("stm", ch % 2)],
                         start=(hi == 0), stop=(False if sub3 >= 2 else hi == 3))
                if sub3 >= 2:
                    for hi in range(4):
                        hh = 4 * g + hi
                        sl = slice(hh * PBLK + tci * 128, hh * PBLK + tci * 128 + 128)
                        t_mm(bky, PS[4 + g % 2][0:DH, hi * 128:(hi + 1) * 128],
                             Sbf_sb[:DH, hh * DH:(hh + 1) * DH], qt_sb[:DH, sl],
                             reads=[("Sbf",), ("qt", hh, tci)],
                             start=False, stop=(hi == 3))
                for hi in range(4):
                    hh = 4 * g + hi
                    ccn = hh // 2
                    r0 = (hh % 2) * 64 if sub3 >= 3 else 0
                    a_act(yb_sb[blk % 2][r0:r0 + 64,
                                         ccn * PBLK + tci * 128: ccn * PBLK + tci * 128 + 128],
                          PS[4 + g % 2][0:DH, hi * 128:(hi + 1) * 128], AF.Copy,
                          reads=[bky, ("yb", blk % 2)], writes=[("yb", blk % 2)])
            if sub2 >= 4:
                v_tt("dve", Sr_sb[:, :], S_sb[:, :], gdec_sb[:, :], ALU.mult,
                     reads=[("S",), ("c", "gdec"), ("Srecv",)], writes=[("Srecv",)])
                v_tt("dve", S_sb[:, 0:512], Sr_sb[:, 0:512], PS[0][0:DH, 0:512],
                     ALU.add, reads=[("Srecv",), bank(0), ("S",)], writes=[("S",)])
                v_tt("dve", S_sb[:, 512:1024], Sr_sb[:, 512:1024], PS[1][0:DH, 0:512],
                     ALU.add, reads=[("Srecv",), bank(1), ("S",)], writes=[("S",)])
                v_copy("dve", Sbf_sb[:, :], S_sb[:, :], reads=[("S",), ("Sbf",)],
                       writes=[("Sbf",)])
            if ch + 2 < NRC:
                fetch_kv(ch + 2)

        def gn_stats_y(blk):
            for ccn in range(NCC if sub2 >= 5 else 0):
                ys = yb_sb[blk % 2][:, ccn * PBLK:(ccn + 1) * PBLK]
                col = blk * 8 + ccn
                a_act(hsc_sb[0][:, :], ys, AF.Copy,
                      reads=[("yb", blk % 2), ("hsc", 0)],
                      writes=[("hsc", 0), ("ysum", col)],
                      accum=ysum_sb[:, col:col + 1])
                a_act(hsc_sb[1][:, :], ys, AF.Square,
                      reads=[("yb", blk % 2), ("hsc", 1)],
                      writes=[("hsc", 1), ("ysq", col)],
                      accum=ysq_sb[:, col:col + 1])
            if sub2 >= 3:
                pb.dma("act", yd[blk, :, :], yb_sb[blk % 2][:, :],
                       reads=[("yb", blk % 2)], writes=[("yd", blk)])
                if blk < 2:
                    pb.dma("act", yl_sb[blk][:, :], yd[blk, :, :],
                           reads=[("yd", blk), ("yl", blk)], writes=[("yl", blk)])

        fetch_h(0)
        fetch_h(1)
        load_weight(wB_sb, "wq_l", "wB")
        fetch_kv(0)
        fetch_kv(1)
        for blk in range(NPB):
            proj(blk)
            if blk == NPB - 1:
                load_weight(wA_sb, "wo_l", "wA")
            if blk > 0:
                gn_stats_y(blk - 1)
            if blk + 2 < NPB:
                fetch_h(blk + 2)
            if sub2 >= 2:
                scores(2 * blk)
                scores(2 * blk + 1)
                rest(2 * blk)
                rest(2 * blk + 1)
        gn_stats_y(NPB - 1)

        if sub2 < 6:
            pb.dma("sp", out_d[0:128, 0:1024], xst_sb[:, 0:1024],
                   reads=[("xst",)], writes=[("outz",)])
            pb.op("sp", lambda e: e.nop(), reads=[("outz",)], writes=())
            pb.finalize()
            _emit(nc, pb)
            return nc
        ysr = ysum_sb[:, :].rearrange("p (a b) -> p b a", a=NPB)
        v_red("dve", ysum8_sb[:, :], ysr, ALU.add,
              reads=[("ysum", c2) for c2 in range(64)], writes=[("ysum8",)])
        ysr2 = ysq_sb[:, :].rearrange("p (a b) -> p b a", a=NPB)
        v_red("dve", ysq8_sb[:, :], ysr2, ALU.add,
              reads=[("ysq", c2) for c2 in range(64)], writes=[("ysq8",)])
        t_mm(bank(2), PS[2][0:8, 0:128], ysum8_sb[:, :], idf_sb[:, :],
             reads=[("ysum8",), ("c", "ident_f32")], start=True, stop=True, is_tr=True)
        t_mm(bank(3), PS[3][0:8, 0:128], ysq8_sb[:, :], idf_sb[:, :],
             reads=[("ysq8",), ("c", "ident_f32")], start=True, stop=True, is_tr=True)
        v_red("dve", gsum_sb[:, :],
              PS[2][0:8, 0:128].rearrange("p (h e) -> p h e", h=2), ALU.add,
              reads=[bank(2)], writes=[("gsum",)])
        v_red("dve", gsq_sb[:, :],
              PS[3][0:8, 0:128].rearrange("p (h e) -> p h e", h=2), ALU.add,
              reads=[bank(3)], writes=[("gsq",)])
        pb.dma("pool", gn_in[0:16, :], gsum_sb[:, :], reads=[("gsum",)],
               writes=[("gnin",)], slow=True)
        pb.dma("pool", gn_in[16:32, :], gsq_sb[:, :], reads=[("gsq",), ("gnin",)],
               writes=[("gnin",)], slow=True)
        pb.cc("ccgn", "AllReduce", ALU.add, [[0, 1], [2, 3], [4, 5], [6, 7]],
              gn_in.ap().opt(), gn_out.ap().opt(),
              reads=[("gnin",)], writes=[("gnout",)])
        pb.dma("pool", gsum2_sb[:, :],
               gn_out[0:16, :].rearrange("(a b) c -> b (a c)", b=2),
               reads=[("gnout",)], writes=[("gsum2",)], slow=True)
        pb.dma("pool", gsq2_sb[:, :],
               gn_out[16:32, :].rearrange("(a b) c -> b (a c)", b=2),
               reads=[("gnout",)], writes=[("gsq2",)], slow=True)
        for ccn in range(NCC):
            t_mm(bank(2), PS[2][:, ccn:ccn + 1], selT_sb[:, :], gsum2_sb[:, ccn:ccn + 1],
                 reads=[("c", "selT"), ("gsum2",)], start=(ccn == 0), stop=(ccn == 7))
            t_mm(bank(3), PS[3][:, ccn:ccn + 1], selT_sb[:, :], gsq2_sb[:, ccn:ccn + 1],
                 reads=[("c", "selT"), ("gsq2",)], start=(ccn == 0), stop=(ccn == 7))
        NTOT = float(T * DH)
        v_ts("dve", mu8_sb[:, :], PS[2][:, 0:8], 1.0 / NTOT, None, ALU.mult, None,
             reads=[bank(2)], writes=[("mu8",)])
        v_ts("dve", sq8_sb[:, :], PS[3][:, 0:8], 1.0 / NTOT, None, ALU.mult, None,
             reads=[bank(3)], writes=[("sq8",)])
        v_tt("dve", Av_sb[:, :], mu8_sb[:, :], mu8_sb[:, :], ALU.mult,
             reads=[("mu8",), ("Av",)], writes=[("Av",)])
        v_tt("dve", Av_sb[:, :], sq8_sb[:, :], Av_sb[:, :], ALU.subtract,
             reads=[("sq8",), ("Av",)], writes=[("Av",)])
        a_act(Bv_sb[:, :], Av_sb[:, :], AF.Sqrt, reads=[("Av",), ("Bv",), ("epsv",)],
              writes=[("Bv",)], bias=epsv_sb[:, 0:1])
        v_recip("dve", Av_sb[:, :], Bv_sb[:, :], reads=[("Bv",), ("Av",)],
                writes=[("Av",)])
        v_tt("dve", Av_sb[:, :], Av_sb[:, :], gnw_sb[:, :], ALU.mult,
             reads=[("Av",), ("c", "gnw_v")], writes=[("Av",)])
        v_tt("dve", Bv_sb[:, :], mu8_sb[:, :], Av_sb[:, :], ALU.mult,
             reads=[("mu8",), ("Av",), ("Bv",)], writes=[("Bv",)])
        v_tt("dve", Bv_sb[:, :], gnb_sb[:, :], Bv_sb[:, :], ALU.subtract,
             reads=[("c", "gnb_v"), ("Bv",)], writes=[("Bv",)])

        if sub2 < 7:
            pb.dma("sp", out_d[0:128, 0:1024], xst_sb[:, 0:1024],
                   reads=[("xst",)], writes=[("outz",)])
            pb.op("sp", lambda e: e.nop(), reads=[("outz",)], writes=())
            pb.finalize()
            _emit(nc, pb)
            return nc
        pb.barrier()
        v_copy("dve", Bvb_sb[:, :], Bv_sb[:, :], reads=[("Bv",)], writes=[("Bvb",)])
        for half in range(2):
            bkb = bank(2 + half)
            for ccn in range(NCC):
                t_mm(bkb, PS[2 + half][:1, 0:512], Bvb_sb[:, ccn:ccn + 1],
                     wA_sb[:, ccn * 1024 + half * 512: ccn * 1024 + (half + 1) * 512],
                     reads=[("Bvb",), ("wA", ccn)],
                     start=(ccn == 0), stop=(ccn == 7))
            a_act(dm_sb[0:1, half * 512:(half + 1) * 512], PS[2 + half][:1, 0:512],
                  AF.Copy, reads=[bkb, ("biasr",)], writes=[("biasr",)])
        for ccn in range(NCC):
            v_ts("dve", wA_sb[:, ccn * 1024:(ccn + 1) * 1024],
                 wA_sb[:, ccn * 1024:(ccn + 1) * 1024], Av_sb[:, ccn:ccn + 1],
                 None, ALU.mult, None,
                 reads=[("wA", ccn), ("Av",)], writes=[("wA", ccn)])

        def _stats_h2(blk):
            ln_stats_block(blk, lambda c2, b2: ("x2", c2, b2))
            if phase_limit >= 3:
                h_block(blk, lambda c2, b2: ("x2", c2, b2),
                        lambda c2, b2: h2_sb[:, c2 * TC + b2 * PBLK: c2 * TC + (b2 + 1) * PBLK],
                        lambda c2, b2: ("h2", c2, b2), a2_c, sb_c,
                        [("mt", 4), ("mt", 3)])
                for ccn in range(NCC):
                    col = blk * 8 + ccn
                    v_red("dve", hmax_sb[:, col:col + 1],
                          h2_sb[:, ccn * TC + blk * PBLK: ccn * TC + (blk + 1) * PBLK],
                          ALU.max, reads=[("h2", ccn, blk)],
                          writes=[("hmaxc", col)], absval=True)

        for blk in range(NPB):
            ybuf = yl_sb[blk % 2]
            for con in range(NCC):
                bk = bank(con % 2)
                for ccn in range(NCC):
                    t_mm(bk, PS[con % 2][:, 0:PBLK],
                         wA_sb[:, ccn * 1024 + con * 128: ccn * 1024 + (con + 1) * 128],
                         ybuf[:, ccn * PBLK:(ccn + 1) * PBLK],
                         reads=[("wA", ccn), ("yl", blk % 2)],
                         start=(ccn == 0), stop=False)
                t_mm(bk, PS[con % 2][:, 0:PBLK],
                     dm_sb[0:1, con * 128:(con + 1) * 128], onr2_sb[0:1, :],
                     reads=[("biasr",), ("c", "one_row256")],
                     start=False, stop=True)
                xs = x_sb[:, con * TC + blk * PBLK: con * TC + (blk + 1) * PBLK]
                v_stt("dve", xs, PS[con % 2][:, 0:PBLK], ga_c[:, con:con + 1], xs,
                      ALU.mult, ALU.add,
                      reads=[bk, ("mt", 2), ("x", con)],
                      writes=[("x", con), ("x2", con, blk)])
            if blk + 2 < NPB:
                pb.dma("act", yl_sb[blk % 2][:, :], yd[blk + 2, :, :],
                       reads=[("yd", blk + 2), ("yl", blk % 2)],
                       writes=[("yl", blk % 2)])
            if 1 <= blk <= 4:
                quant_pk(blk - 1)
            if blk > 0:
                _stats_h2(blk - 1)
        _stats_h2(NPB - 1)

        if "x2" in dbg:
            pb.dma("pool", dbg["x2"][:, :], x_sb[:, :],
                   reads=[("x2", ccn, blk) for ccn in range(NCC) for blk in range(NPB)],
                   writes=[("dbgx2",)])
            out_keys.append(("dbgx2",))
        if "y" in dbg:
            def fnd2d(e, o2=dbg["y"].ap(), i=yd.ap()):
                return e.dma_start(out=o2, in_=i)
            pb.op("pool", fnd2d, reads=[("yd", blk) for blk in range(NPB)],
                  writes=[("dbgy",)], amt=16, kind="dma")
            out_keys.append(("dbgy",))

    if phase_limit >= 3:
        pb.barrier()

        def x2key(ccn, blk):
            return ("x2", ccn, blk)

        def h2sl(ccn, blk):
            return h2_sb[:, ccn * TC + blk * PBLK: ccn * TC + (blk + 1) * PBLK]

        if sub4 < 1:
            pb.dma("sp", out_d[0:128, 0:1024], xst_sb[:, 0:1024],
                   reads=[("xst",)], writes=[("outz2",)])
            pb.op("sp", lambda e: e.nop(), reads=[("outz2",)], writes=())
            pb.finalize()
            _emit(nc, pb)
            return nc
        if sub4 < 2:
            pb.dma("sp", out_d[0:128, 0:1024], xst_sb[:, 0:1024],
                   reads=[("xst",)], writes=[("outz2",)])
            pb.op("sp", lambda e: e.nop(), reads=[("outz2",)], writes=())
            pb.finalize()
            _emit(nc, pb)
            return nc
        v_red("dve", hmax1_sb[:, :], hmax_sb[:, :], ALU.max,
              reads=[("hmaxc", c2) for c2 in range(64)], writes=[("hmax1",)])
        if sub4 < 3:
            pb.dma("sp", out_d[0:128, 0:1024], xst_sb[:, 0:1024],
                   reads=[("xst",)], writes=[("outz2",)])
            pb.op("sp", lambda e: e.nop(), reads=[("outz2",)], writes=())
            pb.finalize()
            _emit(nc, pb)
            return nc
        t_mm(bank(4), PS[4][0:1, 0:128], hmax1_sb[:, 0:1], idf_sb[:, :],
             reads=[("hmax1",), ("c", "ident_f32")], start=True, stop=True, is_tr=True)
        v_red("dve", sA_sb[:, :], PS[4][0:1, 0:128], ALU.max,
              reads=[bank(4)], writes=[("sA",)])
        if sub4 < 4:
            pb.dma("sp", out_d[0:128, 0:1024], xst_sb[:, 0:1024],
                   reads=[("xst",)], writes=[("outz2",)])
            pb.op("sp", lambda e: e.nop(), reads=[("outz2",)], writes=())
            pb.finalize()
            _emit(nc, pb)
            return nc
        v_memset("dve", sc16_sb[:, :], 0.0, writes=[("sc16",)])
        v_copy("dve", sc16_sb[0:1, :], sA_sb[:, :], reads=[("sA",), ("sc16",)],
               writes=[("sc16",)])
        pb.dma("pool", mx1_in.ap(), sc16_sb[:, :], reads=[("sc16",)],
               writes=[("mx1in",)])
        pb.cc("ccmax1", "AllReduce", ALU.max, [list(range(8))],
              mx1_in.ap().opt(), mx1_out.ap().opt(),
              reads=[("mx1in",)], writes=[("mx1out",)])
        pb.dma("pool", sc16b_sb[:, :], mx1_out.ap(), reads=[("mx1out",), ("sc16b",)],
               writes=[("sc16b",)])
        a_act(sA_sb[:, :], clipx_sb[:, :], AF.Sigmoid,
              reads=[("c", "clipx"), ("sA",)], writes=[("sA",)])
        v_tt("dve", sA_sb[:, :], sA_sb[:, :], sc16b_sb[0:1, 0:1], ALU.mult,
             reads=[("sA",), ("sc16b",)], writes=[("sA",)])
        v_ts("dve", sA_sb[:, :], sA_sb[:, :], 1.0 / QMAX, None, ALU.mult, None,
             reads=[("sA",)], writes=[("sA",)])
        v_recip("dve", sB_sb[:, :], sA_sb[:, :], reads=[("sA",), ("sB",)],
                writes=[("sB",)])
        def bcast_scalar(src_key, src_ap, dst_ap, dst_key, bcol):
            t_mm(bank(5), PS[5][:, bcol:bcol + 1], onerf_sb[:, :], src_ap,
                 reads=[("c", "one_row_f32"), src_key], start=True, stop=True)
            v_copy("dve", dst_ap, PS[5][:, bcol:bcol + 1], reads=[bank(5), dst_key],
                   writes=[dst_key])

        if sub4 < 5:
            pb.dma("sp", out_d[0:128, 0:1024], xst_sb[:, 0:1024],
                   reads=[("xst",)], writes=[("outz2",)])
            pb.op("sp", lambda e: e.nop(), reads=[("outz2",)], writes=())
            pb.finalize()
            _emit(nc, pb)
            return nc
        bcast_scalar(("sB",), sB_sb[:, :], rB_sb[:, :], ("rB",), 0)
        bcast_scalar(("sA",), sA_sb[:, :], qsxB_sb[:, :], ("qsxB",), 1)
        if sub4 < 6:
            pb.dma("sp", out_d[0:128, 0:1024], xst_sb[:, 0:1024],
                   reads=[("xst",)], writes=[("outz2",)])
            pb.op("sp", lambda e: e.nop(), reads=[("outz2",)], writes=())
            pb.finalize()
            _emit(nc, pb)
            return nc
        for blk in range(NPB):
            for ccn in range(NCC):
                hs = h2sl(ccn, blk)
                if ccn % 2 == 0:
                    bq = 4 + (blk % 2)
                    a_act(PS[bq][:, 0:PBLK], hs, AF.Copy,
                          reads=[("h2", ccn, blk), ("rB",), ("magic",), bank(bq)],
                          writes=[bank(bq)], scale=rB_sb[:, 0:1], bias=RMAGIC)
                    a_act(hs, PS[bq][:, 0:PBLK], AF.Copy,
                          reads=[bank(bq), ("magic",), ("h2", ccn, blk)],
                          writes=[("h2", ccn, blk)], bias=-RMAGIC)
                    v_ts("dve", hs, hs, QMAX, -QMAX - 1.0, ALU.min, ALU.max,
                         reads=[("h2", ccn, blk)], writes=[("h2", ccn, blk)])
                else:
                    bq = 2 + (blk % 2)
                    v_ts("dve", PS[bq][:, 0:PBLK], hs, rB_sb[:, 0:1], RMAGIC,
                         ALU.mult, ALU.add,
                         reads=[("h2", ccn, blk), ("rB",), bank(bq)],
                         writes=[bank(bq)])
                    v_ts("dve", hs, PS[bq][:, 0:PBLK], RMAGIC, QMAX,
                         ALU.subtract, ALU.min,
                         reads=[bank(bq), ("h2", ccn, blk)],
                         writes=[("h2", ccn, blk)])
                    v_ts("dve", hs, hs, -QMAX - 1.0, None, ALU.max, None,
                         reads=[("h2", ccn, blk)], writes=[("h2", ccn, blk)])

        for i in range(4, NPK):
            quant_pk(i, mq=True, rb0=4, dbuf=True)

        if "xq" in dbg:
            pb.dma("pool", dbg["xq"][:, :], h2_sb[:, :],
                   reads=[("h2", ccn, blk) for ccn in range(NCC) for blk in range(NPB)],
                   writes=[("dbgxq",)])
            out_keys.append(("dbgxq",))

    if phase_limit >= 4:

        v_ts("dve", sgv_sb[:, 0:4], sgv_sb[:, 0:4], qsxB_sb[:, 0:1], None,
             ALU.mult, None, reads=[("qsxB",)] + [("sg", i) for i in range(4)],
             writes=[("sg", i) for i in range(4)])
        v_ts("dve", suv_sb[:, 0:4], suv_sb[:, 0:4], qsxB_sb[:, 0:1], None,
             ALU.mult, None, reads=[("qsxB",)] + [("su", i) for i in range(4)],
             writes=[("su", i) for i in range(4)])
        def jit_load(ich):
            quant_load("wg_nat", ich, wdn_sb, ("wnat",), "sp")
            quant_load("wu_nat", ich, wdq1_sb, ("wq1",), "act")

        def jit_chain(ich):
            quant_gu("wg_nat", ich, wig_sb[ich % 2], ("wig", ich % 2),
                     sgv_sb, "sg", True,
                     wdn_sb, ("wnat",), wmx_sb, ("wmx",), wri_sb, ("wri",),
                     wsc_sb, ("wsc",), "sp", 4, "sp", preloaded=True)
            quant_gu("wu_nat", ich, wiu_sb[ich % 2], ("wiu", ich % 2),
                     suv_sb, "su", True,
                     wdq1_sb, ("wq1",), wmxu_sb, ("wmxu",), wriu_sb, ("wriu",),
                     wscu_sb, ("wscu",), "act", 4, "sp", preloaded=True)

        for ich in range(NI):
            if ich < NPK:
                wgt, gkey = pslot(ich, 0), ("pk", ich, 0)
                wut, ukey = pslot(ich, 1), ("pk", ich, 1)
            else:
                wgt, gkey = wig_sb[ich % 2], ("wig", ich % 2)
                wut, ukey = wiu_sb[ich % 2], ("wiu", ich % 2)
            if ich == NPK - 2:
                jit_load(NPK)
            if NPK <= ich + 1 < NI:
                jit_chain(ich + 1)
            if NPK <= ich + 2 < NI:
                jit_load(ich + 2)
            if ich in (1, 3, 5, 7):
                prep_wd((ich - 1) // 2, phase=0)
            if ich in (2, 4, 6, 8):
                prep_wd((ich - 2) // 2, phase=1)
            for blk in range(NFB):
                bkg = bank((2 * blk) % 4)
                bku = bank((2 * blk + 1) % 4)
                for ccn in range(NCC):
                    xqs = h2_sb[:, ccn * TC + blk * FBLK: ccn * TC + (blk + 1) * FBLK]
                    t_mm(bkg, PS[(2 * blk) % 4][:, 0:FBLK],
                         wgt[:, ccn * 128:(ccn + 1) * 128], xqs,
                         reads=[gkey + (ccn,), ("h2", ccn, 2 * blk),
                                ("h2", ccn, 2 * blk + 1)],
                         start=(ccn == 0), stop=(ccn == 7))
                for ccn in range(NCC):
                    xqs = h2_sb[:, ccn * TC + blk * FBLK: ccn * TC + (blk + 1) * FBLK]
                    t_mm(bku, PS[(2 * blk + 1) % 4][:, 0:FBLK],
                         wut[:, ccn * 128:(ccn + 1) * 128], xqs,
                         reads=[ukey + (ccn,), ("h2", ccn, 2 * blk),
                                ("h2", ccn, 2 * blk + 1)],
                         start=(ccn == 0), stop=(ccn == 7))
                gsl = gsl_sb[blk % 2]
                a_act(gsl[:, :], PS[(2 * blk) % 4][:, 0:FBLK], AF.Silu,
                      reads=[bkg, ("sg", ich), ("gsl", blk % 2)],
                      writes=[("gsl", blk % 2)], scale=sgv_sb[:, ich:ich + 1])
                ss = s_sb[:, ich * TC + blk * FBLK: ich * TC + (blk + 1) * FBLK]
                v_tt("dve", ss, PS[(2 * blk + 1) % 4][:, 0:FBLK], gsl[:, :], ALU.mult,
                     reads=[bku, ("gsl", blk % 2)], writes=[("s", ich, blk)])
            v_red("dve", hmax1_sb[:, :], s_sb[:, ich * TC:(ich + 1) * TC], ALU.max,
                  reads=[("s", ich, b2) for b2 in range(NFB)] + [("hmax1",)],
                  writes=[("hmax1",)], absval=True)
            v_tt("dve", smax_sb[:, ich:ich + 1], hmax1_sb[:, :],
                 suv_sb[:, ich:ich + 1], ALU.mult,
                 reads=[("hmax1",), ("su", ich)], writes=[("smc", ich)])

        v_red("dve", smax1_sb[:, 0:1], smax_sb[:, 0:NI], ALU.max,
              reads=[("smc", c2) for c2 in range(NI)], writes=[("smax1",)])
        t_mm(bank(4), PS[4][0:1, 0:128], smax1_sb[:, 0:1], idf_sb[:, :],
             reads=[("smax1",), ("c", "ident_f32")], start=True, stop=True, is_tr=True)
        v_red("dve", sA_sb[:, :], PS[4][0:1, 0:128], ALU.max,
              reads=[bank(4), ("sA",)], writes=[("sA",)])
        v_copy("dve", sc16_sb[0:1, :], sA_sb[:, :], reads=[("sA",), ("sc16",)],
               writes=[("sc16",)])
        pb.dma("pool", mx2_in.ap(), sc16_sb[:, :], reads=[("sc16",)],
               writes=[("mx2in",)])
        pb.cc("ccmax2", "AllReduce", ALU.max, [list(range(8))],
              mx2_in.ap().opt(), mx2_out.ap().opt(),
              reads=[("mx2in",)], writes=[("mx2out",)])
        pb.dma("pool", sc16b_sb[:, :], mx2_out.ap(), reads=[("mx2out",), ("sc16b",)],
               writes=[("sc16b",)])
        a_act(sA_sb[:, :], clipd_sb[:, :], AF.Sigmoid,
              reads=[("c", "clipd"), ("sA",)], writes=[("sA",)])
        v_tt("dve", sA_sb[:, :], sA_sb[:, :], sc16b_sb[0:1, 0:1], ALU.mult,
             reads=[("sA",), ("sc16b",)], writes=[("sA",)])
        v_ts("dve", sA_sb[:, :], sA_sb[:, :], 1.0 / QMAX, None, ALU.mult, None,
             reads=[("sA",)], writes=[("sA",)])
        v_recip("dve", sB_sb[:, :], sA_sb[:, :], reads=[("sA",), ("sB",)],
                writes=[("sB",)])
        bcast_scalar(("sB",), sB_sb[:, :], rs2B_sb[:, :], ("rs2B",), 2)
        bcast_scalar(("sA",), sA_sb[:, :], qsdB_sb[:, :], ("qsdB",), 3)

        def quant_s_half(hf2, ichs=None):
            for ich in (range(NI) if ichs is None else ichs):
                v_tt("dve", sqc_sb[:, :], suv_sb[:, ich:ich + 1], rs2B_sb[:, 0:1],
                     ALU.mult, reads=[("su", ich), ("rs2B",), ("sqc",)],
                     writes=[("sqc",)])
                srd = [("s", ich, 2 * hf2), ("s", ich, 2 * hf2 + 1)]
                for c2 in range(2):
                    ss = s_sb[:, ich * TC + hf2 * 1024 + c2 * 512:
                              ich * TC + hf2 * 1024 + (c2 + 1) * 512]
                    bq = 4 + c2
                    a_act(PS[bq][:, 0:512], ss, AF.Copy,
                          reads=srd + [("sqc",), ("magic",), bank(bq)],
                          writes=[bank(bq)], scale=sqc_sb[:, 0:1], bias=RMAGIC)
                    a_act(ss, PS[bq][:, 0:512], AF.Copy,
                          reads=[bank(bq), ("magic",)] + srd,
                          writes=srd, bias=-RMAGIC)
                ssf = s_sb[:, ich * TC + hf2 * 1024: ich * TC + (hf2 + 1) * 1024]
                v_ts("dve", ssf, ssf, QMAX, -QMAX - 1.0, ALU.min, ALU.max,
                     reads=srd, writes=srd)

        quant_s_half(0)

        if "s" in dbg:
            quant_s_half(1)
            pb.dma("pool", dbg["s"][:, :], s_sb[:, :],
                   reads=[("s", ich, b2) for ich in range(NI) for b2 in range(NFB)],
                   writes=[("dbgs",)])
            out_keys.append(("dbgs",))

    if phase_limit >= 5:
        def down_pass(p):
            for tb in range(NFB):
                if p == 0 and tb < 2 and "s" not in dbg:
                    quant_s_half(1, range(11 * tb, 11 * (tb + 1)))
                if p in (1, 2) and tb == 1:
                    prep_wd(2 * (p + 1))
                if p in (1, 2) and tb == 3:
                    prep_wd(2 * (p + 1) + 1)
                for k in range(2):
                    ccn = 2 * p + k
                    g = 2 * tb + k
                    bk = bank(g % 4)
                    psd = PS[g % 4]
                    for ich in range(NI):
                        t_mm(bk, psd[:, 0:FBLK],
                             wdQ_sb[p % 2][:, ich * 256 + k * 128:
                                           ich * 256 + k * 128 + 128],
                             s_sb[:, ich * TC + tb * FBLK: ich * TC + (tb + 1) * FBLK],
                             reads=[("s", ich, tb), ("wdQ", p % 2, ich, k)],
                             start=(ich == 0), stop=(ich == NI - 1))
                    ob = outb_sb[:, (g % 2) * FBLK:(g % 2) * FBLK + FBLK]
                    v_stt("dve", ob, psd[:, 0:FBLK], qsdB_sb[:, 0:1],
                          x_sb[:, ccn * TC + tb * FBLK: ccn * TC + (tb + 1) * FBLK],
                          ALU.mult, ALU.add,
                          reads=[bk, ("qsdB",), ("x2", ccn, 2 * tb),
                                 ("x2", ccn, 2 * tb + 1), ("outb", g % 2)],
                          writes=[("outb", g % 2)])
                    pb.dma("sp",
                           out_d[ccn * 128:(ccn + 1) * 128, tb * FBLK:(tb + 1) * FBLK],
                           ob, reads=[("outb", g % 2)], writes=[("outd", p, g)])
                    out_keys.append(("outd", p, g))

        for p in range(4):
            down_pass(p)
    else:
        pb.dma("sp", out_d[0:128, 0:1024], xst_sb[:, 0:1024],
               reads=[("xst",)], writes=[("outz",)])
        out_keys.append(("outz",))

    pb.op("sp", lambda e: e.nop(), reads=list(out_keys), writes=())

    pb.finalize()
    _emit(nc, pb)
    return nc


def _emit(nc, pb):
    import contextlib
    stack = contextlib.ExitStack()
    sems = {}
    for s in sorted(pb.sem_names):
        sems[s] = stack.enter_context(nc.semaphore(s))
    by_eng = {e: [] for e in ENGS}
    for o2 in pb.ops:
        by_eng[o2.eng].append(o2)

    def emit_stream(eng_obj, ops):
        for o2 in ops:
            for s, v in o2.waits:
                eng_obj.wait_ge(sems[s], v)
            inst = o2.fn(eng_obj)
            if o2.token is not None and inst is not None:
                inst.then_inc(sems[o2.token[0]], o2.amt)

    with nc.Block() as blk:
        @blk.tensor
        def _(e):
            emit_stream(e, by_eng["pe"])

        @blk.scalar
        def _(e):
            emit_stream(e, by_eng["act"])

        @blk.vector
        def _(e):
            emit_stream(e, by_eng["dve"])

        @blk.gpsimd
        def _(e):
            emit_stream(e, by_eng["pool"])

        @blk.sync
        def _(e):
            emit_stream(e, by_eng["sp"])
    stack.close()


_CACHED = {}

def kernel(**inputs):
    in_maps = _host_prepare(inputs)
    key = (tuple(DEBUG_TAPS), PHASE_LIMIT, SUB2, SUB3, SUB4)
    if key not in _CACHED:
        _CACHED[key] = build_program(DEBUG_TAPS, PHASE_LIMIT, SUB2, SUB3, SUB4)
    nc = _CACHED[key]
    res = run_bass_kernel_spmd(nc, in_maps, core_ids=list(range(8)))
    out = np.zeros((B, T, C), dtype=np.float32)
    for core in range(8):
        b, half = core // 2, core % 2
        out[b, half * TC:(half + 1) * TC, :] = res.results[core]["out"].T
    kernel.last_results = res
    return out
